# version 04
# baseline (speedup 1.0000x reference)
import math
import contextlib
import numpy as np
import ml_dtypes
import concourse.bass as bass
import concourse.mybir as mybir
from concourse.bass_utils import run_bass_kernel_spmd
from concourse.alu_op_type import AluOpType as OP
from bass_rust import ReduceOp

F32 = mybir.dt.float32
BF16 = mybir.dt.bfloat16
I32 = mybir.dt.int32
AF = mybir.ActivationFunctionType
AX = mybir.AxisListType
BF = ml_dtypes.bfloat16

B, T, D = 4, 8192, 1024
TL = 4096
CH = 512
NCH = TL // CH
NG = 2 * NCH
DI = 2048
DH = DI // 2
DS = 16
DCV = 4
DTR = 64
FF = 4096
NH, HL = 16, 64
NE, NSH = 16, 2
EPS = 1e-5
VELT_C = float(2 ** 20 + 1)
FP8_MAX = 448.0

DBG_STAGE = [None]



def _t(a):
    return np.ascontiguousarray(np.asarray(a).T)


def _cols(vec, p=128):
    v = np.asarray(vec, np.float32).reshape(-1, p)
    return np.ascontiguousarray(v.T)


def stage_inputs(inputs):
    x = np.asarray(inputs["L0_attn_x"])
    shards = []
    shared = {}
    for li in range(2):
        shared[f"m{li}_inprojT"] = _t(inputs["L0_attn_in_proj_w"][li])
        shared[f"m{li}_convw"] = _cols(np.asarray(inputs["L0_attn_conv_w"][li]).T.reshape(-1))
        shared[f"m{li}_convb"] = _cols(inputs["L0_attn_conv_b"][li])
        shared[f"m{li}_xpT"] = _t(inputs["L0_attn_x_proj_w"][li])
        shared[f"m{li}_dtT"] = _t(inputs["L0_attn_dt_proj_w"][li])
        shared[f"m{li}_dtb"] = _cols(inputs["L0_attn_dt_proj_b"][li])
        shared[f"m{li}_outT"] = _t(inputs["L0_attn_out_proj_w"][li])
        shared[f"m{li}_D"] = _cols(inputs["L0_attn_D"][li])
        shared[f"m{li}_rmsw"] = _cols(inputs["L0_attn_rms_w"][li])
        shared[f"m{li}_lnw"] = _cols(inputs["L0_attn_ln_w"][li])
        shared[f"m{li}_lnb"] = _cols(inputs["L0_attn_ln_b"][li])
    for li in range(2):
        cw = np.asarray(inputs["L0_attn_conv_w"][li], np.float32)
        shared[f"m{li}_convw"] = np.ascontiguousarray(
            cw.reshape(16, 128, 4).transpose(1, 0, 2).reshape(128, 64))
    shared["ff1T"] = _t(inputs["L0_attn_ff_w1"])
    shared["ffb1"] = _cols(inputs["L0_attn_ff_b1"])
    shared["ff2T"] = _t(inputs["L0_attn_ff_w2"])
    shared["ffb2"] = _cols(inputs["L0_attn_ff_b2"])
    shared["centT"] = _t(inputs["L0_ffn_centroids"])
    shared["eb_row"] = np.ascontiguousarray(
        np.broadcast_to(np.asarray(inputs["L0_ffn_expert_biases"], np.float32)[None, :], (128, NE)))
    shared["WrT"] = np.ascontiguousarray(
        np.asarray(inputs["L0_ffn_Wr"], np.float32).transpose(0, 2, 1))
    shared["br"] = np.asarray(inputs["L0_ffn_br"], np.float32)
    shared["WsT"] = np.ascontiguousarray(
        np.asarray(inputs["L0_ffn_Ws"], np.float32).transpose(0, 2, 1))
    shared["bs"] = _cols(np.asarray(inputs["L0_ffn_bs"], np.float32).reshape(-1))
    shared["qkvT"] = _t(inputs["L1_attn_w"])
    shared["woutT"] = np.ascontiguousarray(np.asarray(inputs["L1_attn_w_out"], np.float32))
    shared["fqT"] = _t(inputs["L1_ffn_weight"])
    shared["fqb"] = _cols(inputs["L1_ffn_bias"])

    ident = np.eye(128, dtype=np.float32)
    consts = {
        "identity": ident,
        "ones1": np.ones((1, 128), np.float32),
        "cmask": np.triu(np.ones((128, 128), np.float32)),
        "selB": np.stack([np.eye(32, dtype=np.float32)[n].repeat(1) for n in range(16)], 0),
    }
    selB = np.zeros((16, 32, 128), np.float32)
    selC = np.zeros((16, 32, 128), np.float32)
    for n in range(16):
        selB[n, n, :] = 1.0
        selC[n, 16 + n, :] = 1.0
    consts["selB"] = np.ascontiguousarray(selB.transpose(1, 0, 2).reshape(32, 16 * 128))
    consts["selC"] = np.ascontiguousarray(selC.transpose(1, 0, 2).reshape(32, 16 * 128))
    selH = np.zeros((8, 128, 16), np.float32)
    for c in range(8):
        for j in range(2):
            selH[c, j * 64:(j + 1) * 64, 2 * c + j] = 1.0
    consts["selH"] = np.ascontiguousarray(selH.reshape(8 * 128, 16))
    selHB = np.zeros((8, 16, 128), np.float32)
    for c in range(8):
        for j in range(2):
            selHB[c, 2 * c + j, j * 64:(j + 1) * 64] = 1.0
    consts["selHB"] = np.ascontiguousarray(selHB.transpose(1, 0, 2).reshape(16, 8 * 128))
    sel16 = np.zeros((16, 16, 128), np.float32)
    for e in range(16):
        sel16[e, e, :] = 1.0
    consts["sel16"] = np.ascontiguousarray(sel16.transpose(1, 0, 2).reshape(16, 16 * 128))
    misc = np.zeros((128, 12), np.float32)
    misc[:, 0] = 1.0
    misc[:, 1] = EPS
    misc[:, 2] = 1.0 / 1024.0
    misc[:, 3] = 0.5
    misc[:, 4] = 1e-6
    misc[:, 5] = VELT_C
    misc[:, 6] = FP8_MAX
    misc[:, 7] = 1e-12
    misc[:, 8] = 1.0 / (FP8_MAX * FP8_MAX)
    consts["misc"] = misc

    for c in range(8):
        b, half = c // 2, c % 2
        sh = {}
        sh.update(shared)
        sh.update(consts)
        xs = x[b, half * TL:(half + 1) * TL, :]
        sh["x"] = _t(xs)
        f = float(half)
        sh["flag"] = np.full((128, 1), f, np.float32)
        sh["sel_lo"] = ident * (1.0 - f)
        sh["sel_hi"] = ident * f
        for li in range(2):
            al = np.asarray(inputs["L0_attn_A_log"][li], np.float32)[half * DH:(half + 1) * DH]
            sh[f"m{li}_Alog"] = np.ascontiguousarray(
                al.reshape(8, 128, 16).transpose(1, 0, 2).reshape(128, 128))
        shards.append(sh)
    return shards



class Ctx:
    def __init__(self):
        self.nc = bass.Bass("TRN2", target_bir_lowering=False)
        self.stack = contextlib.ExitStack()
        self.sb_free = {}
        self.dram = {}
        self.n = dict(te=0, v=0, s=0, g=0, gd=0, sd=0)
        self.scopes = []
        self.block = None
        self.sem = {}

    def dram_in(self, name, shape, dtype=F32):
        ap = self.nc.dram_tensor(name, list(shape), dtype, kind="ExternalInput").ap()
        self.dram[name] = ap
        return ap

    def dram_out(self, name, shape, dtype=F32):
        ap = self.nc.dram_tensor(name, list(shape), dtype, kind="ExternalOutput").ap()
        self.dram[name] = ap
        return ap

    def dram_tmp(self, name, shape, dtype=BF16):
        ap = self.nc.dram_tensor(name, list(shape), dtype, kind="Internal").ap()
        self.dram[name] = ap
        return ap

    def sbuf(self, name, shape, dtype=BF16):
        st = self.scopes[-1] if self.scopes else self.stack
        h = st.enter_context(self.nc.sbuf_tensor(name, list(shape), dtype))
        self.sb_free[name] = h
        return h

    def push_scope(self):
        self.scopes.append(contextlib.ExitStack())

    def pop_scope(self):
        self.scopes.pop().close()

    def psum(self, name, shape, dtype=F32):
        h = self.stack.enter_context(self.nc.psum_tensor(name, list(shape), dtype))
        return h

    def semaphore(self, name):
        s = self.stack.enter_context(self.nc.semaphore(name))
        self.sem[name] = s
        return s

    def te(self, f):
        self.block.tensor(f)

    def v(self, f):
        self.block.vector(f)

    def s(self, f):
        self.block.scalar(f)

    def g(self, f):
        self.block.gpsimd(f)

    def sy(self, f):
        self.block.sync(f)

    def barrier(self):
        n = dict(self.n)
        sems = self.sem

        def mk(eng_names_sem):
            ncc = self.n.get("cc", 0)
            def f(e):
                for key, scale in [("te", 1), ("v", 1), ("s", 1), ("g", 1), ("gd", 16), ("sd", 16)]:
                    if n[key] > 0:
                        e.wait_ge(sems[key], n[key] * scale)
                if ncc > 0:
                    e.wait_ge(sems["cc"], ncc)
            return f
        self.te(mk(None))
        self.v(mk(None))
        self.s(mk(None))
        self.g(mk(None))
        self.sy(mk(None))

    def inc(self, key, k=1):
        self.n[key] += k
        return self.n[key] * (16 if key in ("gd", "sd") else 1)





def mark(ctx, inst, key):
    inst.then_inc(ctx.sem[key], 16 if key in ("gd", "sd") else 1)
    return ctx.inc(key)


def gload(ctx, dst_ap, src_ap):
    out = {}
    def f(g):
        out["w"] = mark(ctx, g.dma_start(out=dst_ap, in_=src_ap), "gd")
    ctx.g(f)
    return out["w"]


def sload(ctx, dst_ap, src_ap):
    out = {}
    def f(sy):
        out["w"] = mark(ctx, sy.dma_start(out=dst_ap, in_=src_ap), "sd")
    ctx.sy(f)
    return out["w"]


def collective(ctx, kind, op, groups, in_ap, out_ap):
    sems = ctx.sem
    prev = ctx.n.get("cc", 0)
    def f(g):
        if prev:
            g.wait_ge(sems["cc"], prev)
        g.collective_compute(kind, op, replica_groups=groups, ins=[in_ap], outs=[out_ap]
                             ).then_inc(sems["cc"], 1)
    ctx.g(f)
    ctx.n["cc"] = prev + 1
    return ctx.n["cc"]


PAIRS = [[0, 1], [2, 3], [4, 5], [6, 7]]
ALL8 = [list(range(8))]


class LNParams:
    def __init__(self, ctx, tname):
        self.sq = ctx.sbuf(tname + "_sq", [128, 2 * CH], BF16)
        self.stat = ctx.sbuf(tname + "_st", [1, 4 * CH], F32)
        self.statb = ctx.sbuf(tname + "_stb", [1, 2 * CH], BF16)
        self.mu_b = ctx.sbuf(tname + "_mub", [128, CH], BF16)
        self.istd_b = ctx.sbuf(tname + "_isb", [128, CH], BF16)


def emit_norm_stats(ctx, lnp, src_tiles, t, ps_a, ps_b, consts, rms=False, vwait=0):
    sems = ctx.sem
    n = len(src_tiles)
    for i, ap in enumerate(src_tiles):
        tprev = ctx.n["te"]
        def s1(s, i=i, ap=ap, tprev=tprev):
            if i == 0 and vwait:
                s.wait_ge(sems["v"], vwait)
            if i >= 2:
                s.wait_ge(sems["te"], tprev)
            mark(ctx, s.activation(lnp.sq[:, (i % 2) * CH:(i % 2) * CH + t], ap, AF.Square), "s")
        ctx.s(s1)
        sw = ctx.n["s"]
        def t1(te, i=i, ap=ap, sw=sw):
            te.wait_ge(sems["s"], sw)
            if i == 0 and vwait:
                te.wait_ge(sems["v"], vwait)
            if not rms:
                te.matmul(ps_a[0:1, 0:t], consts["ones_col"][:, :], ap,
                          start=(i == 0), stop=(i == n - 1))
            mark(ctx, te.matmul(ps_b[0:1, 0:t], consts["ones_col"][:, :],
                                lnp.sq[:, (i % 2) * CH:(i % 2) * CH + t],
                                start=(i == 0), stop=(i == n - 1)), "te")
        ctx.te(t1)
    tw = ctx.n["te"]
    misc = consts["misc"]
    def v1(v):
        v.wait_ge(sems["te"], tw)
        v.tensor_scalar(lnp.stat[0:1, 0:t], ps_b[0:1, 0:t], misc[0:1, 2:3], None, OP.mult)
        if not rms:
            v.tensor_scalar(lnp.stat[0:1, CH:CH + t], ps_a[0:1, 0:t], misc[0:1, 2:3], None, OP.mult)
            v.tensor_tensor(lnp.stat[0:1, 2 * CH:2 * CH + t], lnp.stat[0:1, CH:CH + t],
                            lnp.stat[0:1, CH:CH + t], OP.mult)
            v.tensor_tensor(lnp.stat[0:1, 0:t], lnp.stat[0:1, 0:t],
                            lnp.stat[0:1, 2 * CH:2 * CH + t], OP.subtract)
            inst = v.tensor_copy(lnp.statb[0:1, 0:t], lnp.stat[0:1, CH:CH + t])
        else:
            inst = v.engine_nop()
        mark(ctx, inst, "v")
    ctx.v(v1)
    vw = ctx.n["v"]
    def s2(s):
        s.wait_ge(sems["v"], vw)
        s.activation(lnp.stat[0:1, 2 * CH:2 * CH + t], lnp.stat[0:1, 0:t], AF.Ln,
                     bias=misc[0:1, 1:2])
        inst = s.activation(lnp.stat[0:1, 3 * CH:3 * CH + t],
                            lnp.stat[0:1, 2 * CH:2 * CH + t], AF.Exp, scale=-0.5)
        mark(ctx, inst, "s")
    ctx.s(s2)
    sw2 = ctx.n["s"]
    def v2(v):
        v.wait_ge(sems["s"], sw2)
        inst = v.tensor_copy(lnp.statb[0:1, CH:CH + t], lnp.stat[0:1, 3 * CH:3 * CH + t])
        mark(ctx, inst, "v")
    ctx.v(v2)
    vw2 = ctx.n["v"]
    def t2(te):
        te.wait_ge(sems["v"], vw2)
        if not rms:
            mark(ctx, te.matmul(ps_a[:, 0:t], consts["ones_bf"][:, :], lnp.statb[0:1, 0:t],
                                start=True, stop=True), "te")
        mark(ctx, te.matmul(ps_b[:, 0:t], consts["ones_bf"][:, :], lnp.statb[0:1, CH:CH + t],
                            start=True, stop=True), "te")
    ctx.te(t2)
    tw2 = ctx.n["te"]
    def s3(s):
        s.wait_ge(sems["te"], tw2)
        if not rms:
            s.copy(lnp.mu_b[:, 0:t], ps_a[:, 0:t])
        inst = s.copy(lnp.istd_b[:, 0:t], ps_b[:, 0:t])
        mark(ctx, inst, "s")
    ctx.s(s3)
    return ctx.n["s"]



def build_program(dbg_stage=None):
    ctx = Ctx()
    nc = ctx.nc
    sems = ctx.sem

    din = {}
    din["x"] = ctx.dram_in("x", [D, TL])
    din["flag"] = ctx.dram_in("flag", [128, 1])
    din["sel_lo"] = ctx.dram_in("sel_lo", [128, 128])
    din["sel_hi"] = ctx.dram_in("sel_hi", [128, 128])
    din["identity"] = ctx.dram_in("identity", [128, 128])
    din["ones1"] = ctx.dram_in("ones1", [1, 128])
    din["cmask"] = ctx.dram_in("cmask", [128, 128])
    din["selB"] = ctx.dram_in("selB", [32, 16 * 128])
    din["selC"] = ctx.dram_in("selC", [32, 16 * 128])
    din["selH"] = ctx.dram_in("selH", [8 * 128, 16])
    din["selHB"] = ctx.dram_in("selHB", [16, 8 * 128])
    din["misc"] = ctx.dram_in("misc", [128, 12])
    din["sel16"] = ctx.dram_in("sel16", [16, 16 * 128])
    for li in range(2):
        din[f"m{li}_inprojT"] = ctx.dram_in(f"m{li}_inprojT", [D, 2 * DI])
        din[f"m{li}_convw"] = ctx.dram_in(f"m{li}_convw", [128, 64])
        din[f"m{li}_convb"] = ctx.dram_in(f"m{li}_convb", [128, 16])
        din[f"m{li}_xpT"] = ctx.dram_in(f"m{li}_xpT", [DI, 96])
        din[f"m{li}_dtT"] = ctx.dram_in(f"m{li}_dtT", [DTR, DI])
        din[f"m{li}_dtb"] = ctx.dram_in(f"m{li}_dtb", [128, 16])
        din[f"m{li}_outT"] = ctx.dram_in(f"m{li}_outT", [DI, D])
        din[f"m{li}_D"] = ctx.dram_in(f"m{li}_D", [128, 16])
        din[f"m{li}_rmsw"] = ctx.dram_in(f"m{li}_rmsw", [128, 8])
        din[f"m{li}_lnw"] = ctx.dram_in(f"m{li}_lnw", [128, 8])
        din[f"m{li}_lnb"] = ctx.dram_in(f"m{li}_lnb", [128, 8])
        din[f"m{li}_Alog"] = ctx.dram_in(f"m{li}_Alog", [128, 128])
    din["ff1T"] = ctx.dram_in("ff1T", [D, 2 * FF])
    din["ffb1"] = ctx.dram_in("ffb1", [128, 64])
    din["ff2T"] = ctx.dram_in("ff2T", [FF, D])
    din["ffb2"] = ctx.dram_in("ffb2", [128, 8])
    din["centT"] = ctx.dram_in("centT", [D, NE])
    din["eb_row"] = ctx.dram_in("eb_row", [128, NE])
    din["WrT"] = ctx.dram_in("WrT", [NE, D, D])
    din["br"] = ctx.dram_in("br", [NE, D])
    din["WsT"] = ctx.dram_in("WsT", [NSH, D, D])
    din["bs"] = ctx.dram_in("bs", [128, 16])
    din["qkvT"] = ctx.dram_in("qkvT", [D, 3 * D])
    din["woutT"] = ctx.dram_in("woutT", [D, D])
    din["fqT"] = ctx.dram_in("fqT", [D, D])
    din["fqb"] = ctx.dram_in("fqb", [128, 8])
    out_dram = ctx.dram_out("out", [D, TL], F32)

    bdl_in = [ctx.dram_tmp(f"bdl_in{i}", [2 * DI + 32, CH]) for i in range(2)]
    bdl_out = [ctx.dram_tmp(f"bdl_out{k}", [2 * (2 * DI + 32), CH]) for k in range(NCH)]
    yag_in = [ctx.dram_tmp(f"yag_in{i}", [DH, CH]) for i in range(2)]
    yag_out = [ctx.dram_tmp(f"yag_out{g}", [DI, CH]) for g in range(NG)]
    tail_in = ctx.dram_tmp("tail_in", [D, 4])
    tail_out = ctx.dram_tmp("tail_out", [2 * D, 4])
    z_dram = ctx.dram_tmp("z_dram", [DI, TL])
    res_dram = ctx.dram_tmp("res_dram", [D, TL])
    stream_dram = ctx.dram_tmp("stream_dram", [D, TL])
    routed_dram = ctx.dram_tmp("routed_dram", [D, TL], F32)
    xc_dram = ctx.dram_tmp("xc_dram", [DI, TL])
    v_dram = ctx.dram_tmp("v_dram", [FF, TL])
    kexp_dram = ctx.dram_tmp("kexp_dram", [D, TL])
    p_dram = ctx.dram_tmp("p_dram", [D, TL])
    vtm_dram = ctx.dram_tmp("vtm_dram", [TL, D])
    sag_in = ctx.dram_tmp("sag_in", [128, 520], F32)
    sag_out = ctx.dram_tmp("sag_out", [256, 520], F32)
    amax_in = ctx.dram_tmp("amax_in", [1, 8], F32)
    amax_out = ctx.dram_tmp("amax_out", [1, 8], F32)

    C = {}
    C["ident_bf"] = ctx.sbuf("ident_bf", [128, 128], BF16)
    C["ident_f32"] = ctx.sbuf("ident_f32", [128, 128], F32)
    C["ones_f32"] = ctx.sbuf("ones_f32", [1, 128], F32)
    C["ones_bf"] = ctx.sbuf("ones_bf", [1, 128], BF16)
    C["ones_col"] = ctx.sbuf("ones_col", [128, 1], BF16)
    C["cmask"] = ctx.sbuf("cmask_sb", [128, 128], BF16)
    C["selB"] = ctx.sbuf("selB_sb", [32, 16 * 128], BF16)
    C["selC"] = ctx.sbuf("selC_sb", [32, 16 * 128], BF16)
    C["selH"] = ctx.sbuf("selH_sb", [128, 8 * 16], BF16)
    C["selHB"] = ctx.sbuf("selHB_sb", [16, 8 * 128], BF16)
    C["sel_lo"] = ctx.sbuf("sel_lo_sb", [128, 128], BF16)
    C["sel_hi"] = ctx.sbuf("sel_hi_sb", [128, 128], BF16)
    C["sel16"] = ctx.sbuf("sel16_sb", [16, 16 * 128], BF16)
    C["misc"] = ctx.sbuf("misc_sb", [128, 12], F32)
    C["flag"] = ctx.sbuf("flag_sb", [128, 1], F32)
    C["flag_bf"] = ctx.sbuf("flag_bf", [128, 128], BF16)
    vecs = {}
    for li in range(2):
        for nm, w in [("convw", 64), ("convb", 16), ("dtb", 16), ("D", 16),
                      ("rmsw", 8), ("lnw", 8), ("lnb", 8), ("Alog", 128)]:
            vecs[f"m{li}_{nm}"] = ctx.sbuf(f"m{li}_{nm}_sb", [128, w], F32)
        vecs[f"m{li}_A"] = ctx.sbuf(f"m{li}_A_sb", [128, 128], F32)
    for nm, w in [("ffb1", 64), ("ffb2", 8), ("bs", 16), ("fqb", 8)]:
        vecs[nm] = ctx.sbuf(nm + "_sb", [128, w], F32)
    vecs["eb_row"] = ctx.sbuf("eb_row_sb", [128, NE], BF16)
    vecs["br"] = ctx.sbuf("br_sb", [NE, D], BF16)

    wbig = ctx.sbuf("wbig", [128, 16384], BF16)
    state = ctx.sbuf("state", [128, 128], F32)
    wk = {}
    wk["normed"] = ctx.sbuf("normed", [128, 8 * CH], BF16)
    wk["sp1"] = ctx.sbuf("sp1", [128, 3 * CH], BF16)
    wk["gat"] = ctx.sbuf("gat", [128, 16 * CH], BF16)
    wk["mix"] = ctx.sbuf("mix", [128, 8 * CH], BF16)
    wk["srcx"] = ctx.sbuf("srcx", [128, 8 * CH], BF16)
    wk["sz"] = ctx.sbuf("szt", [128, 2 * CH], BF16)
    lnp = LNParams(ctx, "ln")

    def alloc_mamba():
        wk["x_in"] = ctx.sbuf("x_in", [128, 3 * (CH + 4)], BF16)
        wk["halo"] = ctx.sbuf("halo", [128, 16 * 4], BF16)
        wk["x_c"] = ctx.sbuf("x_c", [128, 2 * CH], BF16)
        wk["delta"] = ctx.sbuf("delta", [128, 2 * CH], BF16)
        wk["xdbl"] = ctx.sbuf("xdbl", [96, CH], BF16)
        wk["bc"] = ctx.sbuf("bc", [32, CH], BF16)
        wk["Bb"] = ctx.sbuf("Bb", [128, 16 * CH], BF16)
        wk["Cb"] = ctx.sbuf("Cb", [128, 16 * CH], BF16)
        wk["dfeed"] = ctx.sbuf("dfeed", [128, 2 * CH], BF16)
        wk["xfeed"] = ctx.sbuf("xfeed", [128, 2 * CH], BF16)
        wk["pair"] = ctx.sbuf("pair", [128, 8 * CH], BF16)
        wk["dA"] = ctx.sbuf("dA", [128, 4 * CH], BF16)
        wk["dAm"] = ctx.sbuf("dAm", [128, 2 * CH], BF16)
        wk["in1"] = ctx.sbuf("in1", [128, 2 * CH], BF16)
        wk["hbf"] = ctx.sbuf("hbf", [128, 3 * CH], BF16)
        wk["wv"] = ctx.sbuf("wv", [128, 2 * CH], BF16)
        wk["du"] = ctx.sbuf("du", [128, CH], BF16)
        wk["ymine"] = ctx.sbuf("ymine", [128, 2 * CH], BF16)
        wk["diag"] = ctx.sbuf("diag", [128, 4 * 128], BF16)
        wk["tailh"] = ctx.sbuf("tailh", [128, 8 * 4 * 4], BF16)
        wk["yfa"] = ctx.sbuf("yfa", [128, 4 * CH], BF16)
        wk["yfb"] = ctx.sbuf("yfb", [128, 4 * CH], BF16)
        wk["wxp"] = ctx.sbuf("wxp", [128, 16 * 96], BF16)
        wk["wdt"] = ctx.sbuf("wdt", [64, DI], BF16)
        wk["hbig"] = ctx.sbuf("hbig", [128, 4 * CH], F32)

    ps_mm = [ctx.psum(f"ps_mm{i}", [128, CH], F32) for i in range(2)]
    ps_y = [ctx.psum(f"ps_y{i}", [128, CH], F32) for i in range(2)]
    ps_bl = [ctx.psum(f"ps_bl{i}", [128, CH], F32) for i in range(2)]
    ps_x = ctx.psum("ps_x", [128, CH], F32)
    ps_tp = ctx.psum("ps_tp", [128, 128], BF16)

    for s in ["te", "v", "s", "g", "gd", "sd", "cc"]:
        ctx.semaphore(s)

    blk = ctx.stack.enter_context(nc.Block())
    ctx.block = blk

    gload(ctx, C["ident_bf"][:, :], din["identity"])
    sload(ctx, C["ident_f32"][:, :], din["identity"])
    sload(ctx, C["ones_f32"][:, :], din["ones1"])
    gload(ctx, C["ones_bf"][:, :], din["ones1"])
    gload(ctx, C["cmask"][:, :], din["cmask"])
    gload(ctx, C["selB"][:, :], din["selB"])
    gload(ctx, C["selC"][:, :], din["selC"])
    gload(ctx, C["selHB"][:, :], din["selHB"])
    gload(ctx, C["sel_lo"][:, :], din["sel_lo"])
    gload(ctx, C["sel_hi"][:, :], din["sel_hi"])
    gload(ctx, vecs["eb_row"][:, :], din["eb_row"])
    gload(ctx, C["misc"][:, :], din["misc"])
    gload(ctx, C["sel16"][:, :], din["sel16"])
    gload(ctx, C["flag"][:, :], din["flag"])
    gload(ctx, C["selH"][:, :].rearrange("p (c h) -> p c h", c=8), din["selH"].rearrange("(c p) h -> p c h", p=128))
    gload(ctx, vecs["br"][:, :], din["br"])
    for li in range(2):
        for nm in ["convw", "convb", "dtb", "D", "rmsw", "lnw", "lnb", "Alog"]:
            sload(ctx, vecs[f"m{li}_{nm}"][:, :], din[f"m{li}_{nm}"])
    for nm in ["ffb1", "ffb2", "bs", "fqb"]:
        sload(ctx, vecs[nm][:, :], din[nm])
    gw0, sw0 = ctx.n["gd"] * 16, ctx.n["sd"] * 16
    def v0(v):
        v.wait_ge(sems["gd"], gw0)
        v.wait_ge(sems["sd"], sw0)
        inst = v.tensor_copy(C["ones_col"][:, :], C["misc"][:, 0:1])
        inst = v.tensor_scalar(C["flag_bf"][:, :], C["ident_bf"][:, :], C["flag"][:, 0:1], None, OP.mult)
        mark(ctx, inst, "v")
    ctx.v(v0)
    def s0(s):
        s.wait_ge(sems["sd"], sw0)
        for li in range(2):
            inst = s.activation(vecs[f"m{li}_A"][:, :], vecs[f"m{li}_Alog"][:, :], AF.Exp)
        mark(ctx, inst, "s")
    ctx.s(s0)
    def v0b(v):
        v.wait_ge(sems["s"], ctx.n["s"])
        for li in range(2):
            inst = v.tensor_scalar(vecs[f"m{li}_A"][:, :], vecs[f"m{li}_A"][:, :], -1.0, None, OP.mult)
        mark(ctx, inst, "v")
    ctx.v(v0b)
    ctx.barrier()

    def wait_all(e):
        for key in ("te", "v", "s", "g"):
            if ctx.n[key] > 0:
                e.wait_ge(sems[key], ctx.n[key])
        for key in ("gd", "sd"):
            if ctx.n[key] > 0:
                e.wait_ge(sems[key], ctx.n[key] * 16)



    def colmax_bcast(src_col, dst_col):
        vprev = ctx.n["v"]
        def t1(te, vprev=vprev):
            if vprev:
                te.wait_ge(sems["v"], vprev)
            mark(ctx, te.transpose(ps_x[0:1, 0:128], src_col, C["ident_f32"][:, :]), "te")
        ctx.te(t1)
        tw = ctx.n["te"]
        def v1(v, tw=tw):
            v.wait_ge(sems["te"], tw)
            mark(ctx, v.reduce_max(lnp.stat[0:1, 0:1], ps_x[0:1, 0:128], axis=AX.X), "v")
        ctx.v(v1)
        vw = ctx.n["v"]
        def t2(te, vw=vw):
            te.wait_ge(sems["v"], vw)
            mark(ctx, te.matmul(ps_x[:, 0:1], C["ones_f32"][:, :], lnp.stat[0:1, 0:1],
                                start=True, stop=True), "te")
        ctx.te(t2)
        tw2 = ctx.n["te"]
        def v2(v, tw2=tw2):
            v.wait_ge(sems["te"], tw2)
            mark(ctx, v.tensor_copy(dst_col, ps_x[:, 0:1]), "v")
        ctx.v(v2)

    def gw_consumers(te=True, v=False, s=False):
        nte, nv, ns = ctx.n["te"], ctx.n["v"], ctx.n["s"]
        def f(g):
            if te and nte:
                g.wait_ge(sems["te"], nte)
            if v and nv:
                g.wait_ge(sems["v"], nv)
            if s and ns:
                g.wait_ge(sems["s"], ns)
        ctx.g(f)

    XW = CH + 4

    agof = {}
    yagof = {}
    trk = {"vrms": 0, "tz": 0, "tbc": 0, "vfin": 0}

    def emit_mamba_layer(li):
        Acol = vecs[f"m{li}_A"]
        convw = vecs[f"m{li}_convw"]
        convb = vecs[f"m{li}_convb"]
        dtb = vecs[f"m{li}_dtb"]
        Dcol = vecs[f"m{li}_D"]
        rmsw = vecs[f"m{li}_rmsw"]
        lnw = vecs[f"m{li}_lnw"]
        lnb = vecs[f"m{li}_lnb"]
        misc = C["misc"]
        src_dram = din["x"] if li == 0 else res_dram

        gload(ctx, wk["wxp"][:, :].rearrange("p (c r) -> p c r", c=16), din[f"m{li}_xpT"].rearrange("(c p) r -> p c r", p=128))
        gload(ctx, wk["wdt"][:, :], din[f"m{li}_dtT"])
        def gz(g):
            mark(ctx, g.memset(state[:, :], 0.0), "g")
        ctx.g(gz)
        ctx.barrier()

        gload(ctx, wk["tailh"][:, 0:32].rearrange("p (c t) -> p c t", c=8), src_dram[:, TL - 4:TL].rearrange("(c p) t -> p c t", p=128))
        ctx.barrier()
        sload(ctx, tail_in.rearrange("(c p) t -> p c t", p=128), wk["tailh"][:, 0:32].rearrange("p (c t) -> p c t", c=8))
        def gw1(g):
            g.wait_ge(sems["sd"], ctx.n["sd"] * 16)
        ctx.g(gw1)
        collective(ctx, "AllGather", OP.bypass, PAIRS, tail_in, tail_out)
        ccw = ctx.n["cc"]
        def gt2(g, ccw=ccw):
            g.wait_ge(sems["cc"], ccw)
            mark(ctx, g.dma_start(out=wk["tailh"][:, 32:64].rearrange("p (c t) -> p c t", c=8),
                                  in_=tail_out[0:D, :].rearrange("(c p) t -> p c t", p=128)), "gd")
        ctx.g(gt2)
        ctx.barrier()
        tsl = [wk["tailh"][:, 32 + fc * 4:32 + (fc + 1) * 4] for fc in range(8)]
        emit_norm_stats(ctx, lnp, tsl, 4, ps_mm[0], ps_mm[1], C, rms=True)
        swt = ctx.n["s"]
        def vt2(v, swt=swt):
            v.wait_ge(sems["s"], swt)
            for fc in range(8):
                inst = v.scalar_tensor_tensor(wk["tailh"][:, 64 + fc * 4:64 + (fc + 1) * 4],
                                              tsl[fc], rmsw[:, fc:fc + 1],
                                              lnp.istd_b[:, 0:4], OP.mult, OP.mult)
            mark(ctx, inst, "v")
        ctx.v(vt2)
        vwt = ctx.n["v"]
        ctx.barrier()
        for kc in range(8):
            gload(ctx, wbig[:, kc * 2048:(kc + 1) * 2048],
                  din[f"m{li}_inprojT"][kc * 128:(kc + 1) * 128, 0:2048])
        ctx.barrier()
        for oc in range(16):
            sprev = ctx.n["s"]
            def th(te, oc=oc, sprev=sprev):
                if sprev:
                    te.wait_ge(sems["s"], sprev)
                for kc in range(8):
                    inst = te.matmul(ps_mm[oc % 2][:, 0:4],
                                     wbig[:, kc * 2048 + oc * 128:kc * 2048 + (oc + 1) * 128],
                                     wk["tailh"][:, 64 + kc * 4:64 + (kc + 1) * 4],
                                     start=(kc == 0), stop=(kc == 7))
                mark(ctx, inst, "te")
            ctx.te(th)
            tw = ctx.n["te"]
            def sh(s, oc=oc, tw=tw):
                s.wait_ge(sems["te"], tw)
                mark(ctx, s.copy(wk["halo"][:, oc * 4:(oc + 1) * 4], ps_mm[oc % 2][:, 0:4]), "s")
            ctx.s(sh)
        swh = ctx.n["s"]
        def vfl(v, swh=swh):
            v.wait_ge(sems["s"], swh)
            inst = v.tensor_scalar(wk["halo"][:, :], wk["halo"][:, :], C["flag"][:, 0:1], None, OP.mult)
            mark(ctx, inst, "v")
        ctx.v(vfl)
        ctx.barrier()

        def passA(k):
            t0 = k * CH
            sd_entry = ctx.n["sd"] * 16
            vg, tg = trk["vrms"], trk["tz"]
            def gpre(g, vg=vg, tg=tg):
                if vg:
                    g.wait_ge(sems["v"], vg)
                if tg:
                    g.wait_ge(sems["te"], tg)
            ctx.g(gpre)
            gload(ctx, wk["srcx"][:, :].rearrange("p (c t) -> p c t", c=8), src_dram[:, t0:t0 + CH].rearrange("(c p) t -> p c t", p=128))
            if k > 0:
                for kc in range(8):
                    gload(ctx, wbig[:, kc * 2048:(kc + 1) * 2048],
                          din[f"m{li}_inprojT"][kc * 128:(kc + 1) * 128, 0:2048])
            ccb = agof.get(k - 2, 0)
            def sypre(sy, ccb=ccb):
                if ccb:
                    sy.wait_ge(sems["cc"], ccb)
            ctx.sy(sypre)
            gdw_pa = ctx.n["gd"] * 16
            def wait_gd_pa(e, gdw_pa=gdw_pa):
                e.wait_ge(sems["gd"], gdw_pa)
            src_tiles = [wk["srcx"][:, fc * CH:(fc + 1) * CH] for fc in range(8)]
            ctx.s(wait_gd_pa)
            ctx.te(wait_gd_pa)
            emit_norm_stats(ctx, lnp, src_tiles, CH, ps_mm[0], ps_mm[1], C, rms=True)
            sw = ctx.n["s"]
            def vrms(v, sw=sw):
                v.wait_ge(sems["s"], sw)
                for fc in range(8):
                    inst = v.scalar_tensor_tensor(wk["normed"][:, fc * CH:(fc + 1) * CH],
                                                  src_tiles[fc], rmsw[:, fc:fc + 1],
                                                  lnp.istd_b[:, 0:CH], OP.mult, OP.mult)
                mark(ctx, inst, "v")
            ctx.v(vrms)
            vw = ctx.n["v"]
            trk["vrms"] = vw
            for dc in range(16):
                sprev = ctx.n["s"]
                def tip(te, dc=dc, vw=vw, sprev=sprev, gdw_pa=gdw_pa):
                    te.wait_ge(sems["v"], vw)
                    if dc == 0:
                        te.wait_ge(sems["gd"], gdw_pa)
                    if sprev:
                        te.wait_ge(sems["s"], sprev)
                    for kc in range(8):
                        inst = te.matmul(ps_mm[dc % 2][:, :],
                                         wbig[:, kc * 2048 + dc * 128:kc * 2048 + (dc + 1) * 128],
                                         wk["normed"][:, kc * CH:(kc + 1) * CH],
                                         start=(kc == 0), stop=(kc == 7))
                    mark(ctx, inst, "te")
                ctx.te(tip)
                twi = ctx.n["te"]
                slot = dc % 3
                def sxi(s, dc=dc, twi=twi, slot=slot):
                    s.wait_ge(sems["te"], twi)
                    mark(ctx, s.copy(wk["x_in"][:, slot * XW + 4:(slot + 1) * XW],
                                     ps_mm[dc % 2][:, :]), "s")
                ctx.s(sxi)
                swx = ctx.n["s"]
                def vcv(v, dc=dc, slot=slot, swx=swx):
                    v.wait_ge(sems["s"], swx)
                    v.tensor_copy(wk["x_in"][:, slot * XW:slot * XW + 4], wk["halo"][:, dc * 4:(dc + 1) * 4])
                    for j in range(4):
                        inst = v.tensor_scalar(wk["diag"][:, j * 128:(j + 1) * 128], C["ident_bf"][:, :],
                                               convw[:, dc * 4 + j:dc * 4 + j + 1], None, OP.mult)
                    mark(ctx, inst, "v")
                ctx.v(vcv)
                vwc = ctx.n["v"]
                sprev2 = ctx.n["s"]
                sdprev = ctx.n["sd"] * 16
                def tcv(te, dc=dc, slot=slot, vwc=vwc, sprev2=sprev2):
                    te.wait_ge(sems["v"], vwc)
                    for j in range(4):
                        inst = te.matmul(ps_x[:, :], wk["diag"][:, j * 128:(j + 1) * 128],
                                         wk["x_in"][:, slot * XW + j + 1:slot * XW + j + 1 + CH],
                                         start=(j == 0), stop=(j == 3))
                    mark(ctx, inst, "te")
                ctx.te(tcv)
                twc = ctx.n["te"]
                def scv(s, dc=dc, twc=twc, sdprev=sdprev):
                    s.wait_ge(sems["te"], twc)
                    if dc >= 2:
                        s.wait_ge(sems["sd"], sdprev)
                    mark(ctx, s.activation(wk["x_c"][:, (dc % 2) * CH:(dc % 2 + 1) * CH],
                                           ps_x[:, :], AF.Silu, bias=convb[:, dc:dc + 1]), "s")
                ctx.s(scv)
                swc = ctx.n["s"]
                def vht(v, dc=dc, slot=slot, swc=swc):
                    v.wait_ge(sems["s"], swc)
                    mark(ctx, v.tensor_copy(wk["halo"][:, dc * 4:(dc + 1) * 4],
                                            wk["x_in"][:, slot * XW + CH:slot * XW + CH + 4]), "v")
                ctx.v(vht)
                def txp(te, dc=dc, swc=swc):
                    te.wait_ge(sems["s"], swc)
                    mark(ctx, te.matmul(ps_bl[0][0:96, :], wk["wxp"][:, dc * 96:(dc + 1) * 96],
                                        wk["x_c"][:, (dc % 2) * CH:(dc % 2 + 1) * CH],
                                        start=(dc == 0), stop=(dc == 15)), "te")
                ctx.te(txp)
                def syxc(sy, swc=swc):
                    sy.wait_ge(sems["s"], swc)
                ctx.sy(syxc)
                sload(ctx, xc_dram[dc * 128:(dc + 1) * 128, t0:t0 + CH],
                      wk["x_c"][:, (dc % 2) * CH:(dc % 2 + 1) * CH])
                sload(ctx, bdl_in[k % 2][DI + dc * 128:DI + (dc + 1) * 128, :],
                      wk["x_c"][:, (dc % 2) * CH:(dc % 2 + 1) * CH])
            twx = ctx.n["te"]
            def sxd(s, twx=twx, sd_entry=sd_entry):
                s.wait_ge(sems["te"], twx)
                s.wait_ge(sems["sd"], sd_entry)
                mark(ctx, s.copy(wk["xdbl"][:, :], ps_bl[0][0:96, :]), "s")
            ctx.s(sxd)
            swz0 = ctx.n["s"]
            twz0 = ctx.n["te"]
            def gwz0(g, twz0=twz0):
                g.wait_ge(sems["te"], twz0)
            ctx.g(gwz0)
            for kc in range(8):
                gload(ctx, wbig[:, kc * 2048:(kc + 1) * 2048],
                      din[f"m{li}_inprojT"][kc * 128:(kc + 1) * 128, 2048:4096])
            gwz = ctx.n["gd"] * 16
            for oc in range(16):
                sprev = ctx.n["s"]
                sdprev = ctx.n["sd"] * 16
                def tz(te, oc=oc, gwz=gwz, sprev=sprev):
                    te.wait_ge(sems["gd"], gwz)
                    if sprev:
                        te.wait_ge(sems["s"], sprev)
                    for kc in range(8):
                        inst = te.matmul(ps_mm[oc % 2][:, :],
                                         wbig[:, kc * 2048 + oc * 128:kc * 2048 + (oc + 1) * 128],
                                         wk["normed"][:, kc * CH:(kc + 1) * CH],
                                         start=(kc == 0), stop=(kc == 7))
                    mark(ctx, inst, "te")
                ctx.te(tz)
                twz = ctx.n["te"]
                def sz_(s, oc=oc, twz=twz, sdprev=sdprev):
                    s.wait_ge(sems["te"], twz)
                    if oc >= 2:
                        s.wait_ge(sems["sd"], sdprev)
                    mark(ctx, s.activation(wk["sz"][:, (oc % 2) * CH:(oc % 2 + 1) * CH],
                                           ps_mm[oc % 2][:, :], AF.Silu), "s")
                ctx.s(sz_)
                swz = ctx.n["s"]
                def syz(sy, swz=swz):
                    sy.wait_ge(sems["s"], swz)
                ctx.sy(syz)
                sload(ctx, z_dram[oc * 128:(oc + 1) * 128, t0:t0 + CH],
                      wk["sz"][:, (oc % 2) * CH:(oc % 2 + 1) * CH])
            trk["tz"] = ctx.n["te"]
            swxd = ctx.n["s"]
            for oc in range(16):
                sprev = ctx.n["s"]
                def tdt(te, oc=oc, swxd=swxd, sprev=sprev):
                    te.wait_ge(sems["s"], max(swxd, sprev))
                    mark(ctx, te.matmul(ps_bl[1][:, :], wk["wdt"][:, oc * 128:(oc + 1) * 128],
                                        wk["xdbl"][0:64, :], start=True, stop=True), "te")
                ctx.te(tdt)
                twd = ctx.n["te"]
                sdprev = ctx.n["sd"] * 16
                def sdt(s, oc=oc, twd=twd, sdprev=sdprev):
                    s.wait_ge(sems["te"], twd)
                    if oc >= 2:
                        s.wait_ge(sems["sd"], sdprev)
                    s.activation(wk["sp1"][:, 0:CH], ps_bl[1][:, :], AF.Abs, bias=dtb[:, oc:oc + 1])
                    s.activation(wk["sp1"][:, CH:2 * CH], ps_bl[1][:, :], AF.Relu, bias=dtb[:, oc:oc + 1])
                    s.activation(wk["sp1"][:, 2 * CH:3 * CH], wk["sp1"][:, 0:CH], AF.Exp, scale=-1.0)
                    inst = s.activation(wk["sp1"][:, 0:CH], wk["sp1"][:, 2 * CH:3 * CH],
                                        AF.Ln, bias=misc[:, 0:1])
                    mark(ctx, inst, "s")
                ctx.s(sdt)
                sw2 = ctx.n["s"]
                sdp2 = ctx.n["sd"] * 16
                def vdt(v, oc=oc, sw2=sw2, sdp2=sdp2):
                    v.wait_ge(sems["s"], sw2)
                    if oc >= 2:
                        v.wait_ge(sems["sd"], sdp2)
                    mark(ctx, v.tensor_tensor(wk["delta"][:, (oc % 2) * CH:(oc % 2 + 1) * CH],
                                              wk["sp1"][:, 0:CH], wk["sp1"][:, CH:2 * CH],
                                              OP.add), "v")
                ctx.v(vdt)
                vwd = ctx.n["v"]
                def syd(sy, vwd=vwd):
                    sy.wait_ge(sems["v"], vwd)
                ctx.sy(syd)
                sload(ctx, bdl_in[k % 2][oc * 128:(oc + 1) * 128, :],
                      wk["delta"][:, (oc % 2) * CH:(oc % 2 + 1) * CH])
            swbc = ctx.n["s"]
            def sybc(sy, swbc=swbc):
                sy.wait_ge(sems["s"], swbc)
            ctx.sy(sybc)
            sload(ctx, bdl_in[k % 2][2 * DI:2 * DI + 32, :], wk["xdbl"][64:96, :])
            sdw = ctx.n["sd"] * 16
            def gag(g, sdw=sdw):
                g.wait_ge(sems["sd"], sdw)
            ctx.g(gag)
            agof[k] = collective(ctx, "AllGather", OP.bypass, PAIRS, bdl_in[k % 2], bdl_out[k])

        def scan_chunk(g):
            slot = g // 8
            kk = g % 8
            base = slot * (2 * DI + 32)
            ccw = agof[kk]
            ccy = yagof.get(g - 2, 0)
            teb = trk["tbc"]
            def syw(sy, ccw=ccw, ccy=ccy, teb=teb):
                sy.wait_ge(sems["cc"], ccw)
                if ccy:
                    sy.wait_ge(sems["cc"], ccy)
                if teb:
                    sy.wait_ge(sems["te"], teb)
            ctx.sy(syw)
            sload(ctx, wk["bc"][:, :], bdl_out[kk][base + 2 * DI:base + 2 * DI + 32, :])
            sdw0 = ctx.n["sd"] * 16
            for n in range(16):
                sprev = ctx.n["s"]
                def tbc(te, n=n, sprev=sprev, sdw0=sdw0):
                    te.wait_ge(sems["sd"], sdw0)
                    if sprev:
                        te.wait_ge(sems["s"], sprev)
                    te.matmul(ps_bl[0][:, :], C["selB"][:, n * 128:(n + 1) * 128],
                              wk["bc"][:, :], start=True, stop=True)
                    mark(ctx, te.matmul(ps_bl[1][:, :], C["selC"][:, n * 128:(n + 1) * 128],
                                        wk["bc"][:, :], start=True, stop=True), "te")
                ctx.te(tbc)
                if n == 15:
                    trk["tbc"] = ctx.n["te"]
                tw = ctx.n["te"]
                def sbc(s, n=n, tw=tw):
                    s.wait_ge(sems["te"], tw)
                    s.copy(wk["Bb"][:, n * CH:(n + 1) * CH], ps_bl[0][:, :])
                    mark(ctx, s.copy(wk["Cb"][:, n * CH:(n + 1) * CH], ps_bl[1][:, :]), "s")
                ctx.s(sbc)
            tfbm_prev = trk.get("tfbm", [0] * 8)
            tfbm = [0] * 8
            sdwm = {}

            def issue_pair(dcx):
                sb = (dcx % 2) * 4 * CH
                guard = tfbm[dcx - 2] if dcx >= 2 else tfbm_prev[6 + dcx]
                def syp(sy, guard=guard):
                    if guard:
                        sy.wait_ge(sems["te"], guard)
                ctx.sy(syp)
                sload(ctx, wk["pair"][:, sb:sb + CH],
                      bdl_out[kk][base + dcx * 128:base + (dcx + 1) * 128, :])
                sload(ctx, wk["pair"][:, sb + CH:sb + 2 * CH],
                      bdl_out[kk][base + (dcx + 8) * 128:base + (dcx + 9) * 128, :])
                sload(ctx, wk["pair"][:, sb + 2 * CH:sb + 3 * CH],
                      bdl_out[kk][base + DI + dcx * 128:base + DI + (dcx + 1) * 128, :])
                sload(ctx, wk["pair"][:, sb + 3 * CH:sb + 4 * CH],
                      bdl_out[kk][base + DI + (dcx + 8) * 128:base + DI + (dcx + 9) * 128, :])
                sdwm[dcx] = ctx.n["sd"] * 16

            issue_pair(0)
            for dc in range(8):
                if dc + 1 < 8:
                    issue_pair(dc + 1)
                sb = (dc % 2) * 4 * CH
                sdw = sdwm[dc]
                sprev = ctx.n["s"]
                def tfb(te, sb=sb, sdw=sdw, sprev=sprev):
                    te.wait_ge(sems["sd"], sdw)
                    if sprev:
                        te.wait_ge(sems["s"], sprev)
                    te.matmul(ps_mm[0][:, :], C["sel_lo"][:, :], wk["pair"][:, sb:sb + CH],
                              start=True, stop=False)
                    te.matmul(ps_mm[0][:, :], C["sel_hi"][:, :], wk["pair"][:, sb + CH:sb + 2 * CH],
                              start=False, stop=True)
                    te.matmul(ps_mm[1][:, :], C["sel_lo"][:, :], wk["pair"][:, sb + 2 * CH:sb + 3 * CH],
                              start=True, stop=False)
                    mark(ctx, te.matmul(ps_mm[1][:, :], C["sel_hi"][:, :], wk["pair"][:, sb + 3 * CH:sb + 4 * CH],
                                        start=False, stop=True), "te")
                ctx.te(tfb)
                tfbm[dc] = ctx.n["te"]
                tw = ctx.n["te"]
                dslot = dc % 2
                def sfb(s, tw=tw, dslot=dslot):
                    s.wait_ge(sems["te"], tw)
                    s.copy(wk["dfeed"][:, dslot * CH:(dslot + 1) * CH], ps_mm[0][:, :])
                    mark(ctx, s.copy(wk["xfeed"][:, dslot * CH:(dslot + 1) * CH], ps_mm[1][:, :]), "s")
                ctx.s(sfb)
                swf = ctx.n["s"]
                def vdu(v, dslot=dslot, swf=swf):
                    v.wait_ge(sems["s"], swf)
                    mark(ctx, v.tensor_tensor(wk["du"][:, :], wk["dfeed"][:, dslot * CH:(dslot + 1) * CH],
                                              wk["xfeed"][:, dslot * CH:(dslot + 1) * CH], OP.mult), "v")
                ctx.v(vdu)
                cast_marks = {}

                def emit_w(nn):
                    s3 = nn % 3
                    ceng, cval = cast_marks[nn]
                    tprev2 = ctx.n["te"]
                    def vw_(v, nn=nn, s3=s3, ceng=ceng, cval=cval, tprev2=tprev2):
                        v.wait_ge(sems[ceng], cval)
                        if nn >= 2 and tprev2 > 1:
                            v.wait_ge(sems["te"], tprev2 - 1)
                        mark(ctx, v.tensor_tensor(wk["wv"][:, (nn % 2) * CH:(nn % 2 + 1) * CH],
                                                  wk["hbf"][:, s3 * CH:(s3 + 1) * CH],
                                                  wk["Cb"][:, nn * CH:(nn + 1) * CH], OP.mult), "v")
                    ctx.v(vw_)
                    vw_w = ctx.n["v"]
                    sprev3 = ctx.n["s"]
                    def tym(te, dc=dc, nn=nn, vw_w=vw_w, sprev3=sprev3):
                        te.wait_ge(sems["v"], vw_w)
                        if nn == 0 and sprev3:
                            te.wait_ge(sems["s"], sprev3)
                        mark(ctx, te.matmul(ps_y[dc % 2][:, :], C["ident_bf"][:, :],
                                            wk["wv"][:, (nn % 2) * CH:(nn % 2 + 1) * CH],
                                            start=(nn == 0), stop=(nn == 15)), "te")
                    ctx.te(tym)

                exp_marks = {}
                for n in range(16):
                    s4 = n % 4
                    s2 = n % 2
                    em2 = exp_marks.get(n - 2, 0)
                    def vts(v, dc=dc, n=n, s2=s2, dslot=dslot, swf=swf, em2=em2):
                        if n == 0:
                            v.wait_ge(sems["s"], swf)
                        if em2:
                            v.wait_ge(sems["s"], em2)
                        mark(ctx, v.tensor_scalar(wk["dAm"][:, s2 * CH:(s2 + 1) * CH],
                                                  wk["dfeed"][:, dslot * CH:(dslot + 1) * CH],
                                                  Acol[:, dc * 16 + n:dc * 16 + n + 1], None,
                                                  OP.mult), "v")
                    ctx.v(vts)
                    vts_w = ctx.n["v"]
                    def sdA2(s, n=n, s4=s4, s2=s2, vts_w=vts_w):
                        s.wait_ge(sems["v"], vts_w)
                        mark(ctx, s.activation(wk["dA"][:, s4 * CH:(s4 + 1) * CH],
                                               wk["dAm"][:, s2 * CH:(s2 + 1) * CH], AF.Exp), "s")
                    ctx.s(sdA2)
                    exp_marks[n] = ctx.n["s"]
                    sw_dA = ctx.n["s"]
                    cm4 = cast_marks.get(n - 4)
                    def vscan(v, dc=dc, n=n, s4=s4, sw_dA=sw_dA, cm4=cm4):
                        v.wait_ge(sems["s"], sw_dA)
                        v.tensor_tensor(wk["in1"][:, (n % 2) * CH:(n % 2 + 1) * CH], wk["du"][:, :],
                                        wk["Bb"][:, n * CH:(n + 1) * CH], OP.mult)
                        if cm4 is not None:
                            v.wait_ge(sems[cm4[0]], cm4[1])
                        mark(ctx, v.tensor_tensor_scan(wk["hbig"][:, s4 * CH:(s4 + 1) * CH],
                                                       wk["dA"][:, s4 * CH:(s4 + 1) * CH],
                                                       wk["in1"][:, (n % 2) * CH:(n % 2 + 1) * CH],
                                                       state[:, dc * 16 + n:dc * 16 + n + 1],
                                                       OP.mult, OP.add), "v")
                    ctx.v(vscan)
                    vw_scan = ctx.n["v"]
                    s3n = n % 3
                    if n % 2 == 0:
                        def gcast(g_, s3n=s3n, s4=s4, vw_scan=vw_scan):
                            g_.wait_ge(sems["v"], vw_scan)
                            mark(ctx, g_.tensor_copy(wk["hbf"][:, s3n * CH:(s3n + 1) * CH],
                                                     wk["hbig"][:, s4 * CH:(s4 + 1) * CH]), "g")
                        ctx.g(gcast)
                        cast_marks[n] = ("g", ctx.n["g"])
                    else:
                        def scast(s, s3n=s3n, s4=s4, vw_scan=vw_scan):
                            s.wait_ge(sems["v"], vw_scan)
                            mark(ctx, s.activation(wk["hbf"][:, s3n * CH:(s3n + 1) * CH],
                                                   wk["hbig"][:, s4 * CH:(s4 + 1) * CH],
                                                   AF.Identity), "s")
                        ctx.s(scast)
                        cast_marks[n] = ("s", ctx.n["s"])
                    if n >= 2:
                        emit_w(n - 2)
                    if n % 4 == 3:
                        ng0 = (n // 4) * 4
                        _vws = ctx.n["v"]
                        def sst_(s, dc=dc, ng0=ng0, _vws=_vws):
                            s.wait_ge(sems["v"], _vws)
                            mark(ctx, s.copy(state[:, dc * 16 + ng0:dc * 16 + ng0 + 4],
                                             wk["hbig"][:, CH - 1::CH]), "s")
                        ctx.s(sst_)
                emit_w(14)
                emit_w(15)
                tw = ctx.n["te"]
                sdpy = ctx.n["sd"] * 16
                def sy_(s, dc=dc, tw=tw, sdpy=sdpy):
                    s.wait_ge(sems["te"], tw)
                    if dc >= 2:
                        s.wait_ge(sems["sd"], sdpy)
                    mark(ctx, s.copy(wk["ymine"][:, (dc % 2) * CH:(dc % 2 + 1) * CH], ps_y[dc % 2][:, :]), "s")
                ctx.s(sy_)
                _sn = ctx.n["s"]
                def syy(sy, _sn=_sn):
                    sy.wait_ge(sems["s"], _sn)
                ctx.sy(syy)
                sload(ctx, yag_in[g % 2][dc * 128:(dc + 1) * 128, :],
                      wk["ymine"][:, (dc % 2) * CH:(dc % 2 + 1) * CH])
            trk["tfbm"] = tfbm
            sdw2 = ctx.n["sd"] * 16
            def gwy(g_, sdw2=sdw2):
                g_.wait_ge(sems["sd"], sdw2)
            ctx.g(gwy)
            yagof[g] = collective(ctx, "AllGather", OP.bypass, PAIRS, yag_in[g % 2], yag_out[g])

        def gating(k, dst_add_x):
            t0 = k * CH
            ccw = yagof[k + 8]
            def syw(sy, ccw=ccw):
                sy.wait_ge(sems["cc"], ccw)
            ctx.sy(syw)
            if dst_add_x:
                vg = trk["vfin"]
                def gpre2(g, vg=vg):
                    if vg:
                        g.wait_ge(sems["v"], vg)
                ctx.g(gpre2)
                gload(ctx, wk["srcx"][:, :].rearrange("p (c t) -> p c t", c=8), din["x"][:, t0:t0 + CH].rearrange("(c p) t -> p c t", p=128))
            gdw = ctx.n["gd"] * 16
            for grp in range(4):
                tprev = ctx.n["te"]
                vprev = ctx.n["v"]
                def syg(sy, grp=grp, tprev=tprev, vprev=vprev):
                    if tprev:
                        sy.wait_ge(sems["te"], tprev)
                    if vprev:
                        sy.wait_ge(sems["v"], vprev)
                ctx.sy(syg)
                for j in range(4):
                    c16 = grp * 4 + j
                    sload(ctx, wk["yfa"][:, j * CH:(j + 1) * CH],
                          yag_out[k][c16 * 128:(c16 + 1) * 128, :])
                    sload(ctx, wk["yfb"][:, j * CH:(j + 1) * CH],
                          yag_out[k + 8][c16 * 128:(c16 + 1) * 128, :])
                sdw = ctx.n["sd"] * 16
                for j in range(4):
                    c16 = grp * 4 + j
                    vprev2 = ctx.n["v"]
                    def syl(sy, c16=c16, j=j, vprev2=vprev2):
                        if vprev2:
                            sy.wait_ge(sems["v"], vprev2)
                    ctx.sy(syl)
                    sload(ctx, wk["x_c"][:, (j % 2) * CH:(j % 2 + 1) * CH],
                          xc_dram[c16 * 128:(c16 + 1) * 128, t0:t0 + CH])
                    sload(ctx, wk["sz"][:, (j % 2) * CH:(j % 2 + 1) * CH],
                          z_dram[c16 * 128:(c16 + 1) * 128, t0:t0 + CH])
                    sdw2 = ctx.n["sd"] * 16
                    vprev3 = ctx.n["v"]
                    def tys(te, c16=c16, j=j, sdw=sdw, vprev3=vprev3):
                        te.wait_ge(sems["sd"], sdw)
                        if vprev3:
                            te.wait_ge(sems["v"], vprev3)
                        te.matmul(ps_bl[c16 % 2][:, :], C["sel_lo"][:, :],
                                  wk["yfa"][:, j * CH:(j + 1) * CH], start=True, stop=False)
                        mark(ctx, te.matmul(ps_bl[c16 % 2][:, :], C["sel_hi"][:, :],
                                            wk["yfb"][:, j * CH:(j + 1) * CH], start=False, stop=True), "te")
                    ctx.te(tys)
                    tw = ctx.n["te"]
                    def vga(v, c16=c16, j=j, tw=tw, sdw2=sdw2):
                        v.wait_ge(sems["te"], tw)
                        v.wait_ge(sems["sd"], sdw2)
                        v.scalar_tensor_tensor(wk["gat"][:, c16 * CH:(c16 + 1) * CH],
                                               wk["x_c"][:, (j % 2) * CH:(j % 2 + 1) * CH],
                                               Dcol[:, c16:c16 + 1], ps_bl[c16 % 2][:, :],
                                               OP.mult, OP.add)
                        mark(ctx, v.tensor_tensor(wk["gat"][:, c16 * CH:(c16 + 1) * CH],
                                                  wk["gat"][:, c16 * CH:(c16 + 1) * CH],
                                                  wk["sz"][:, (j % 2) * CH:(j % 2 + 1) * CH], OP.mult), "v")
                    ctx.v(vga)
            vwg = ctx.n["v"]
            for oc in range(8):
                sprev = ctx.n["s"]
                gdw_o = trk.get("gdw_o", 0)
                def tmo(te, oc=oc, vwg=vwg, sprev=sprev, gdw_o=gdw_o):
                    te.wait_ge(sems["v"], vwg)
                    if oc == 0 and gdw_o:
                        te.wait_ge(sems["gd"], gdw_o)
                    if sprev:
                        te.wait_ge(sems["s"], sprev)
                    for kc in range(16):
                        inst = te.matmul(ps_mm[oc % 2][:, :],
                                         wbig[:, kc * 1024 + oc * 128:kc * 1024 + (oc + 1) * 128],
                                         wk["gat"][:, kc * CH:(kc + 1) * CH],
                                         start=(kc == 0), stop=(kc == 15))
                    mark(ctx, inst, "te")
                ctx.te(tmo)
                tw = ctx.n["te"]
                def smo(s, oc=oc, tw=tw):
                    s.wait_ge(sems["te"], tw)
                    mark(ctx, s.copy(wk["mix"][:, oc * CH:(oc + 1) * CH], ps_mm[oc % 2][:, :]), "s")
                ctx.s(smo)
            mix_tiles = [wk["mix"][:, oc * CH:(oc + 1) * CH] for oc in range(8)]
            emit_norm_stats(ctx, lnp, mix_tiles, CH, ps_mm[0], ps_mm[1], C, rms=False)
            swn = ctx.n["s"]
            for oc in range(8):
                def vap(v, oc=oc, swn=swn):
                    v.wait_ge(sems["s"], swn)
                    v.tensor_tensor(wk["sp1"][:, 0:CH], wk["mix"][:, oc * CH:(oc + 1) * CH],
                                    lnp.mu_b[:, 0:CH], OP.subtract)
                    mark(ctx, v.scalar_tensor_tensor(wk["sp1"][:, CH:2 * CH], wk["sp1"][:, 0:CH],
                                                     lnw[:, oc:oc + 1], lnp.istd_b[:, 0:CH],
                                                     OP.mult, OP.mult), "v")
                ctx.v(vap)
                vw2 = ctx.n["v"]
                def sap(s, oc=oc, vw2=vw2):
                    s.wait_ge(sems["v"], vw2)
                    mark(ctx, s.activation(wk["sp1"][:, 2 * CH:3 * CH], wk["sp1"][:, CH:2 * CH],
                                           AF.Identity, bias=lnb[:, oc:oc + 1]), "s")
                ctx.s(sap)
                sw3 = ctx.n["s"]
                def vfin(v, oc=oc, sw3=sw3, dst_add_x=dst_add_x, gdw=gdw):
                    v.wait_ge(sems["s"], sw3)
                    if dst_add_x:
                        v.wait_ge(sems["gd"], gdw)
                        inst = v.tensor_tensor(wk["mix"][:, oc * CH:(oc + 1) * CH],
                                               wk["sp1"][:, 2 * CH:3 * CH],
                                               wk["srcx"][:, oc * CH:(oc + 1) * CH], OP.add)
                    else:
                        inst = v.tensor_copy(wk["mix"][:, oc * CH:(oc + 1) * CH],
                                             wk["sp1"][:, 2 * CH:3 * CH])
                    mark(ctx, inst, "v")
                ctx.v(vfin)
            vwf = ctx.n["v"]
            trk["vfin"] = vwf
            def sys(sy, vwf=vwf):
                sy.wait_ge(sems["v"], vwf)
            ctx.sy(sys)
            sload(ctx, res_dram[:, t0:t0 + CH].rearrange("(c p) t -> p c t", p=128), wk["mix"][:, :].rearrange("p (c t) -> p c t", c=8))

        with nc.named_scope(f"m{li}_P1"):
            for k in range(NCH):
                with nc.named_scope(f"m{li}_pA{k}"):
                    passA(k)
                with nc.named_scope(f"m{li}_sc{k}"):
                    scan_chunk(k)
        tg = trk["tz"]
        def gwout(g, tg=tg):
            g.wait_ge(sems["te"], tg)
        ctx.g(gwout)
        for kc in range(16):
            gload(ctx, wbig[:, kc * 1024:(kc + 1) * 1024],
                  din[f"m{li}_outT"][kc * 128:(kc + 1) * 128, :])
        trk["gdw_o"] = ctx.n["gd"] * 16
        with nc.named_scope(f"m{li}_P2"):
            for k in range(NCH):
                with nc.named_scope(f"m{li}_sc{k + 8}"):
                    scan_chunk(k + 8)
                if k > 0:
                    with nc.named_scope(f"m{li}_gt{k - 1}"):
                        gating(k - 1, dst_add_x=(li == 0))
            with nc.named_scope(f"m{li}_gt7"):
                gating(NCH - 1, dst_add_x=(li == 0))

    ctx.push_scope()
    alloc_mamba()
    with nc.named_scope("mamba0"):
        emit_mamba_layer(0)
    if dbg_stage != "m0":
        with nc.named_scope("mamba1"):
            emit_mamba_layer(1)
    ctx.pop_scope()
    if dbg_stage == "m0":
        ctx.barrier()
        def gdump0(g):
            mark(ctx, g.dma_start(out=out_dram, in_=res_dram), "gd")
        ctx.g(gdump0)
        ctx.barrier()
        return ctx

    def emit_ffn():
        for blkj in range(8):
            tprev = ctx.n["te"]
            def gww(g, tprev=tprev):
                if tprev:
                    g.wait_ge(sems["te"], tprev)
            ctx.g(gww)
            for kc in range(8):
                gload(ctx, wbig[:, kc * 512:(kc + 1) * 512],
                      din["ff1T"][kc * 128:(kc + 1) * 128, blkj * 512:(blkj + 1) * 512])
                gload(ctx, wbig[:, 4096 + kc * 512:4096 + (kc + 1) * 512],
                      din["ff1T"][kc * 128:(kc + 1) * 128, FF + blkj * 512:FF + (blkj + 1) * 512])
            gw = ctx.n["gd"] * 16
            for tb in range(8):
                t0 = tb * CH
                gw_consumers(te=True)
                gload(ctx, wk["normed"][:, :].rearrange("p (c t) -> p c t", c=8), res_dram[:, t0:t0 + CH].rearrange("(c p) t -> p c t", p=128))
                gwh = ctx.n["gd"] * 16
                for j in range(4):
                    oc = blkj * 4 + j
                    sprev = ctx.n["s"]
                    def tff(te, j=j, gw=gw, gwh=gwh, sprev=sprev):
                        te.wait_ge(sems["gd"], max(gw, gwh))
                        if sprev:
                            te.wait_ge(sems["s"], sprev)
                        for kc in range(8):
                            te.matmul(ps_mm[0][:, :],
                                      wbig[:, kc * 512 + j * 128:kc * 512 + (j + 1) * 128],
                                      wk["normed"][:, kc * CH:(kc + 1) * CH],
                                      start=(kc == 0), stop=(kc == 7))
                        for kc in range(8):
                            inst = te.matmul(ps_mm[1][:, :],
                                             wbig[:, 4096 + kc * 512 + j * 128:4096 + kc * 512 + (j + 1) * 128],
                                             wk["normed"][:, kc * CH:(kc + 1) * CH],
                                             start=(kc == 0), stop=(kc == 7))
                        mark(ctx, inst, "te")
                    ctx.te(tff)
                    tw = ctx.n["te"]
                    sdp = ctx.n["sd"] * 16
                    def sff(s, oc=oc, tw=tw):
                        s.wait_ge(sems["te"], tw)
                        s.activation(wk["sp1"][:, 0:CH], ps_mm[0][:, :], AF.Identity,
                                     bias=vecs["ffb1"][:, oc:oc + 1])
                        mark(ctx, s.activation(wk["sp1"][:, CH:2 * CH], ps_mm[1][:, :], AF.Gelu,
                                               bias=vecs["ffb1"][:, 32 + oc:33 + oc]), "s")
                    ctx.s(sff)
                    sw = ctx.n["s"]
                    def vff(v, sw=sw, sdp=sdp, j=j):
                        v.wait_ge(sems["s"], sw)
                        if sdp:
                            v.wait_ge(sems["sd"], sdp)
                        mark(ctx, v.tensor_tensor(wk["sz"][:, (j % 2) * CH:(j % 2 + 1) * CH],
                                                  wk["sp1"][:, 0:CH], wk["sp1"][:, CH:2 * CH],
                                                  OP.mult), "v")
                    ctx.v(vff)
                    vw = ctx.n["v"]
                    def syv(sy, vw=vw):
                        sy.wait_ge(sems["v"], vw)
                    ctx.sy(syv)
                    sload(ctx, v_dram[oc * 128:(oc + 1) * 128, t0:t0 + CH],
                          wk["sz"][:, (j % 2) * CH:(j % 2 + 1) * CH])
        ctx.barrier()
        for ochalf in range(2):
            tprev = ctx.n["te"]
            def gww(g, tprev=tprev):
                if tprev:
                    g.wait_ge(sems["te"], tprev)
            ctx.g(gww)
            for kc in range(32):
                gload(ctx, wbig[:, kc * 512:(kc + 1) * 512],
                      din["ff2T"][kc * 128:(kc + 1) * 128, ochalf * 512:(ochalf + 1) * 512])
            gw = ctx.n["gd"] * 16
            for tb in range(8):
                t0 = tb * CH
                psl = [ps_mm[0], ps_mm[1], ps_bl[0], ps_bl[1]]
                for khalf in range(2):
                    gw_consumers(te=True)
                    gload(ctx, wk["gat"][:, :].rearrange("p (c t) -> p c t", c=16),
                          v_dram[khalf * 2048:(khalf + 1) * 2048, t0:t0 + CH].rearrange(
                              "(c p) t -> p c t", p=128))
                    gwv = ctx.n["gd"] * 16
                    sprev = ctx.n["s"]
                    def tf2(te, khalf=khalf, gw=gw, gwv=gwv, sprev=sprev):
                        te.wait_ge(sems["gd"], max(gw, gwv))
                        if khalf == 0 and sprev:
                            te.wait_ge(sems["s"], sprev)
                        for oc4 in range(4):
                            for kc in range(16):
                                kcg = khalf * 16 + kc
                                inst = te.matmul(psl[oc4][:, :],
                                                 wbig[:, kcg * 512 + oc4 * 128:kcg * 512 + (oc4 + 1) * 128],
                                                 wk["gat"][:, kc * CH:(kc + 1) * CH],
                                                 start=(kcg == 0), stop=(kcg == 31))
                        mark(ctx, inst, "te")
                    ctx.te(tf2)
                tw = ctx.n["te"]
                def sf2(s, ochalf=ochalf, tw=tw):
                    s.wait_ge(sems["te"], tw)
                    for oc4 in range(4):
                        oc = ochalf * 4 + oc4
                        inst = s.activation(wk["mix"][:, oc4 * CH:(oc4 + 1) * CH], psl[oc4][:, :],
                                            AF.Identity, bias=vecs["ffb2"][:, oc:oc + 1])
                    mark(ctx, inst, "s")
                ctx.s(sf2)
                sw = ctx.n["s"]
                def syf(sy, sw=sw):
                    sy.wait_ge(sems["s"], sw)
                ctx.sy(syf)
                for oc4 in range(4):
                    oc = ochalf * 4 + oc4
                    sload(ctx, stream_dram[oc * 128:(oc + 1) * 128, t0:t0 + CH],
                          wk["mix"][:, oc4 * CH:(oc4 + 1) * CH])
        ctx.barrier()
        if dbg_stage == "ff2":
            return
        for tb in range(8):
            t0 = tb * CH
            gw_consumers(te=True, v=True, s=True)
            _sdn = ctx.n["sd"] * 16
            def gsd(g, _sdn=_sdn):
                if _sdn:
                    g.wait_ge(sems["sd"], _sdn)
            ctx.g(gsd)
            gload(ctx, wk["mix"][:, :].rearrange("p (c t) -> p c t", c=8), stream_dram[:, t0:t0 + CH].rearrange("(c p) t -> p c t", p=128))
            gload(ctx, wk["srcx"][:, :].rearrange("p (c t) -> p c t", c=8), din["x"][:, t0:t0 + CH].rearrange("(c p) t -> p c t", p=128))
            gw = ctx.n["gd"] * 16
            def vadd(v, gw=gw):
                v.wait_ge(sems["gd"], gw)
                for oc in range(8):
                    inst = v.tensor_tensor(wk["mix"][:, oc * CH:(oc + 1) * CH],
                                           wk["mix"][:, oc * CH:(oc + 1) * CH],
                                           wk["srcx"][:, oc * CH:(oc + 1) * CH], OP.add)
                mark(ctx, inst, "v")
            ctx.v(vadd)
            mix_tiles = [wk["mix"][:, oc * CH:(oc + 1) * CH] for oc in range(8)]
            emit_norm_stats(ctx, lnp, mix_tiles, CH, ps_mm[0], ps_mm[1], C, rms=False, vwait=ctx.n["v"])
            swn = ctx.n["s"]
            for oc in range(8):
                def vap(v, oc=oc, swn=swn):
                    v.wait_ge(sems["s"], swn)
                    v.tensor_tensor(wk["sp1"][:, 0:CH], wk["mix"][:, oc * CH:(oc + 1) * CH],
                                    lnp.mu_b[:, 0:CH], OP.subtract)
                    mark(ctx, v.tensor_tensor(wk["srcx"][:, oc * CH:(oc + 1) * CH],
                                              wk["sp1"][:, 0:CH], lnp.istd_b[:, 0:CH], OP.mult), "v")
                ctx.v(vap)
            vw = ctx.n["v"]
            def syu(sy, vw=vw):
                sy.wait_ge(sems["v"], vw)
            ctx.sy(syu)
            sload(ctx, stream_dram[:, t0:t0 + CH].rearrange("(c p) t -> p c t", p=128), wk["srcx"][:, :].rearrange("p (c t) -> p c t", c=8))
        ctx.barrier()

    def emit_moe():
        centT = ctx.sbuf("centT_sb", [128, 8 * NE], BF16)
        sfm = ctx.sbuf("sfm", [16, TL], BF16)
        gfm = ctx.sbuf("gfm", [16, TL], BF16)
        stok = ctx.sbuf("stok", [128, 32 * NE], BF16)
        colsb = ctx.sbuf("colsb", [128, 10 * 32], F32)
        accf = ctx.sbuf("accf", [128, 4 * CH], F32)
        bsum2 = ctx.sbuf("bsum2", [128, 8], F32)
        gload(ctx, centT[:, :].rearrange("p (c e) -> p c e", c=8), din["centT"].rearrange("(c p) e -> p c e", p=128))
        def vbs(v):
            mark(ctx, v.tensor_tensor(colsb[:, 0:8], vecs["bs"][:, 0:8], vecs["bs"][:, 8:16], OP.add), "v")
        ctx.v(vbs)
        _vwb = ctx.n["v"]
        def sbs(s, _vwb=_vwb):
            s.wait_ge(sems["v"], _vwb)
            mark(ctx, s.activation(bsum2[:, :], colsb[:, 0:8], AF.Identity,
                                   scale=C["misc"][:, 3:4]), "s")
        ctx.s(sbs)
        ctx.barrier()
        for tb in range(8):
            t0 = tb * CH
            gw_consumers(te=True)
            gload(ctx, wk["normed"][:, :].rearrange("p (c t) -> p c t", c=8), stream_dram[:, t0:t0 + CH].rearrange("(c p) t -> p c t", p=128))
            gw = ctx.n["gd"] * 16
            sprev = ctx.n["s"]
            def tsc(te, gw=gw, sprev=sprev):
                te.wait_ge(sems["gd"], gw)
                if sprev:
                    te.wait_ge(sems["s"], sprev)
                for kc in range(8):
                    inst = te.matmul(ps_mm[0][0:16, :], centT[:, kc * NE:(kc + 1) * NE],
                                     wk["normed"][:, kc * CH:(kc + 1) * CH],
                                     start=(kc == 0), stop=(kc == 7))
                mark(ctx, inst, "te")
            ctx.te(tsc)
            tw = ctx.n["te"]
            def ssc(s, tb=tb, tw=tw):
                s.wait_ge(sems["te"], tw)
                mark(ctx, s.copy(sfm[:, tb * CH:(tb + 1) * CH], ps_mm[0][0:16, :]), "s")
            ctx.s(ssc)
        ctx.barrier()
        for sb in range(32):
            vprev = ctx.n["v"]
            def ttp(te, sb=sb, vprev=vprev):
                if vprev:
                    te.wait_ge(sems["v"], vprev)
                mark(ctx, te.transpose(ps_tp[:, 0:16],
                                       sfm[:, sb * 128:(sb + 1) * 128], C["ident_bf"][0:16, 0:16]), "te")
            ctx.te(ttp)
            tw = ctx.n["te"]
            def vtp(v, sb=sb, tw=tw):
                v.wait_ge(sems["te"], tw)
                mark(ctx, v.tensor_copy(stok[:, sb * NE:(sb + 1) * NE], ps_tp[:, 0:16]), "v")
            ctx.v(vtp)
        ctx.barrier()
        def vg1(v):
            for sb in range(32):
                v.tensor_tensor(stok[:, sb * NE:(sb + 1) * NE], stok[:, sb * NE:(sb + 1) * NE],
                                vecs["eb_row"][:, :], OP.add)
            for sb in range(32):
                v.reduce_max(colsb[:, sb:sb + 1], stok[:, sb * NE:(sb + 1) * NE], axis=AX.X)
            for sb in range(32):
                inst = v.tensor_scalar(colsb[:, 32 + sb:33 + sb], colsb[:, sb:sb + 1], -1.0, None, OP.mult)
            mark(ctx, inst, "v")
        ctx.v(vg1)
        vw = ctx.n["v"]
        def sg1(s, vw=vw):
            s.wait_ge(sems["v"], vw)
            for sb in range(32):
                inst = s.activation(stok[:, sb * NE:(sb + 1) * NE], stok[:, sb * NE:(sb + 1) * NE],
                                    AF.Exp, bias=colsb[:, 32 + sb:33 + sb],
                                    accum_out=colsb[:, 64 + sb:65 + sb])
            mark(ctx, inst, "s")
        ctx.s(sg1)
        sw = ctx.n["s"]
        def vg2a(v, sw=sw):
            v.wait_ge(sems["s"], sw)
            for sb in range(32):
                inst = v.reduce_max(colsb[:, 224 + sb:225 + sb], stok[:, sb * NE:(sb + 1) * NE], axis=AX.X)
            mark(ctx, inst, "v")
        ctx.v(vg2a)
        _vw2 = ctx.n["v"]
        def sg2a(s, _vw2=_vw2):
            s.wait_ge(sems["v"], _vw2)
            for sb in range(32):
                inst = s.copy(colsb[:, 96 + sb:97 + sb], colsb[:, 224 + sb:225 + sb])
            mark(ctx, inst, "s")
        ctx.s(sg2a)
        _sw2 = ctx.n["s"]
        def vg2(v, sw=sw, _sw2=_sw2):
            v.wait_ge(sems["s"], _sw2)
            for sb in range(32):
                v.tensor_scalar(wk["sp1"][:, sb * NE:(sb + 1) * NE] if False else lnp.sq[:, sb * NE:(sb + 1) * NE],
                                stok[:, sb * NE:(sb + 1) * NE], colsb[:, 96 + sb:97 + sb], None, OP.is_lt)
            for sb in range(32):
                v.tensor_tensor(lnp.sq[:, sb * NE:(sb + 1) * NE], stok[:, sb * NE:(sb + 1) * NE],
                                lnp.sq[:, sb * NE:(sb + 1) * NE], OP.mult)
            for sb in range(32):
                inst = v.reduce_max(colsb[:, 256 + sb:257 + sb], lnp.sq[:, sb * NE:(sb + 1) * NE], axis=AX.X)
            mark(ctx, inst, "v")
        ctx.v(vg2)
        _vw3 = ctx.n["v"]
        def sg2b(s, _vw3=_vw3):
            s.wait_ge(sems["v"], _vw3)
            for sb in range(32):
                inst = s.copy(colsb[:, 128 + sb:129 + sb], colsb[:, 256 + sb:257 + sb])
            mark(ctx, inst, "s")
        ctx.s(sg2b)
        _sw3 = ctx.n["s"]
        def vg2c(v, _sw3=_sw3):
            v.wait_ge(sems["s"], _sw3)
            for sb in range(32):
                v.tensor_scalar(lnp.sq[:, sb * NE:(sb + 1) * NE], stok[:, sb * NE:(sb + 1) * NE],
                                colsb[:, 128 + sb:129 + sb], None, OP.is_ge)
            for sb in range(32):
                v.tensor_tensor(stok[:, sb * NE:(sb + 1) * NE], stok[:, sb * NE:(sb + 1) * NE],
                                lnp.sq[:, sb * NE:(sb + 1) * NE], OP.mult)
            for sb in range(32):
                inst = v.reciprocal(colsb[:, 288 + sb:289 + sb], colsb[:, 64 + sb:65 + sb])
            mark(ctx, inst, "v")
        ctx.v(vg2c)
        _vw4 = ctx.n["v"]
        def sg2c(s, _vw4=_vw4):
            s.wait_ge(sems["v"], _vw4)
            for sb in range(32):
                inst = s.copy(colsb[:, 160 + sb:161 + sb], colsb[:, 288 + sb:289 + sb])
            mark(ctx, inst, "s")
        ctx.s(sg2c)
        _sw4 = ctx.n["s"]
        def vg2d(v, _sw4=_sw4):
            v.wait_ge(sems["s"], _sw4)
            for sb in range(32):
                inst = v.tensor_scalar(stok[:, sb * NE:(sb + 1) * NE], stok[:, sb * NE:(sb + 1) * NE],
                                       colsb[:, 160 + sb:161 + sb], None, OP.mult)
            mark(ctx, inst, "v")
        ctx.v(vg2d)
        ctx.barrier()
        for sb in range(32):
            sprev = ctx.n["s"]
            def ttb(te, sb=sb, sprev=sprev):
                if sprev:
                    te.wait_ge(sems["s"], sprev)
                mark(ctx, te.transpose(ps_tp[0:16, :], stok[:, sb * NE:(sb + 1) * NE],
                                       C["ident_bf"][:, :]), "te")
            ctx.te(ttb)
            tw = ctx.n["te"]
            def stb(s, sb=sb, tw=tw):
                s.wait_ge(sems["te"], tw)
                mark(ctx, s.copy(gfm[:, sb * 128:(sb + 1) * 128], ps_tp[0:16, :]), "s")
            ctx.s(stb)
        ctx.barrier()
        if dbg_stage == "moeg":
            _s = ctx.n["s"]
            def syg2(sy, _s=_s):
                sy.wait_ge(sems["s"], _s)
            ctx.sy(syg2)
            sload(ctx, v_dram[0:16, 0:TL], gfm[:, :])
            ctx.barrier()
            dump_stream(v_dram[0:D, :])
            return
        for eg in range(9):
            two = 2 if eg < 8 else 2
            tprev = ctx.n["te"]
            def gwe(g, tprev=tprev):
                if tprev:
                    g.wait_ge(sems["te"], tprev)
            ctx.g(gwe)
            for e2 in range(2):
                src_w = din["WrT"][eg * 2 + e2] if eg < 8 else din["WsT"][e2]
                for kc in range(8):
                    gload(ctx, wbig[:, e2 * 8192 + kc * 1024:e2 * 8192 + (kc + 1) * 1024],
                          src_w[kc * 128:(kc + 1) * 128, :])
            gw = ctx.n["gd"] * 16
            for tb in range(8):
                t0 = tb * CH
                gw_consumers(te=True, v=True)
                gload(ctx, wk["normed"][:, :].rearrange("p (c t) -> p c t", c=8), stream_dram[:, t0:t0 + CH].rearrange("(c p) t -> p c t", p=128))
                gwu = ctx.n["gd"] * 16
                for e2 in range(2):
                    sprev = ctx.n["s"]
                    vprev = ctx.n["v"]
                    if eg < 8:
                        def tgb(te, eg=eg, e2=e2, tb=tb, sprev=sprev, vprev=vprev):
                            if sprev:
                                te.wait_ge(sems["s"], sprev)
                            if vprev:
                                te.wait_ge(sems["v"], vprev)
                            e = eg * 2 + e2
                            mark(ctx, te.matmul(ps_x[:, :], C["sel16"][:, e * 128:(e + 1) * 128],
                                                gfm[:, tb * CH:(tb + 1) * CH], start=True, stop=True), "te")
                        ctx.te(tgb)
                        tw = ctx.n["te"]
                        def sgb(s, tw=tw, e2=e2):
                            s.wait_ge(sems["te"], tw)
                            mark(ctx, s.copy(wk["sz"][:, e2 * CH:(e2 + 1) * CH], ps_x[:, :]), "s")
                        ctx.s(sgb)
                        swg = ctx.n["s"]
                        def vue(v, e2=e2, swg=swg, gwu=gwu):
                            v.wait_ge(sems["s"], swg)
                            v.wait_ge(sems["gd"], gwu)
                            for kc in range(8):
                                inst = v.tensor_tensor(wk["gat"][:, (e2 * 8 + kc) * CH:(e2 * 8 + kc + 1) * CH],
                                                       wk["normed"][:, kc * CH:(kc + 1) * CH],
                                                       wk["sz"][:, e2 * CH:(e2 + 1) * CH], OP.mult)
                            mark(ctx, inst, "v")
                        ctx.v(vue)
                    else:
                        def vue(v, e2=e2, gwu=gwu, vprev=vprev):
                            v.wait_ge(sems["gd"], gwu)
                            if vprev:
                                v.wait_ge(sems["v"], vprev)
                            for kc in range(8):
                                inst = v.tensor_scalar(wk["gat"][:, (e2 * 8 + kc) * CH:(e2 * 8 + kc + 1) * CH],
                                                       wk["normed"][:, kc * CH:(kc + 1) * CH],
                                                       C["misc"][:, 3:4], None, OP.mult)
                            mark(ctx, inst, "v")
                        ctx.v(vue)
                vwu = ctx.n["v"]
                for oc in range(8):
                    sprev = ctx.n["s"]
                    def tme(te, oc=oc, gw=gw, vwu=vwu, sprev=sprev):
                        te.wait_ge(sems["gd"], gw)
                        te.wait_ge(sems["v"], vwu)
                        if sprev:
                            te.wait_ge(sems["s"], sprev)
                        for e2 in range(2):
                            for kc in range(8):
                                idx = e2 * 8 + kc
                                inst = te.matmul(ps_mm[oc % 2][:, :],
                                                 wbig[:, e2 * 8192 + kc * 1024 + oc * 128:
                                                       e2 * 8192 + kc * 1024 + (oc + 1) * 128],
                                                 wk["gat"][:, idx * CH:(idx + 1) * CH],
                                                 start=(idx == 0), stop=(idx == 15))
                        mark(ctx, inst, "te")
                    ctx.te(tme)
                    tw = ctx.n["te"]
                    def sme(s, oc=oc, tw=tw):
                        s.wait_ge(sems["te"], tw)
                        mark(ctx, s.copy(wk["mix"][:, oc * CH:(oc + 1) * CH], ps_mm[oc % 2][:, :]), "s")
                    ctx.s(sme)
                sw = ctx.n["s"]
                def gacc(g, eg=eg, t0=t0, sw=sw):
                    g.wait_ge(sems["s"], sw)
                    if eg == 0:
                        inst = g.dma_start(out=routed_dram[:, t0:t0 + CH].rearrange("(c p) t -> p c t", p=128),
                                           in_=wk["mix"][:, :].rearrange("p (c t) -> p c t", c=8))
                    else:
                        inst = g.dma_start(out=routed_dram[:, t0:t0 + CH].rearrange("(c p) t -> p c t", p=128),
                                           in_=wk["mix"][:, :].rearrange("p (c t) -> p c t", c=8),
                                           accum_op=OP.add)
                    mark(ctx, inst, "gd")
                ctx.g(gacc)
            ctx.barrier()
        if dbg_stage == "moer":
            def gdr(g):
                mark(ctx, g.dma_start(out=out_dram, in_=routed_dram), "gd")
            ctx.g(gdr)
            ctx.barrier()
            return
        for tb in range(8):
            t0 = tb * CH
            gw_consumers(te=True, v=True, s=True)
            _sdn = ctx.n["sd"] * 16
            def gsd2(g, _sdn=_sdn):
                if _sdn:
                    g.wait_ge(sems["sd"], _sdn)
            ctx.g(gsd2)
            gload(ctx, wk["normed"][:, :].rearrange("p (c t) -> p c t", c=8), stream_dram[:, t0:t0 + CH].rearrange("(c p) t -> p c t", p=128))
            for half in range(2):
                vprev = ctx.n["v"]
                def gacl(g, vprev=vprev):
                    if vprev:
                        g.wait_ge(sems["v"], vprev)
                ctx.g(gacl)
                gload(ctx, accf[:, :].rearrange("p (c t) -> p c t", c=4),
                      routed_dram[half * 512:half * 512 + 512, t0:t0 + CH].rearrange(
                          "(c p) t -> p c t", p=128))
                gw2 = ctx.n["gd"] * 16
                for j in range(4):
                    oc = half * 4 + j
                    sprev = ctx.n["s"]
                    vprev2 = ctx.n["v"]
                    def tbr(te, oc=oc, tb=tb, sprev=sprev, vprev2=vprev2):
                        if sprev:
                            te.wait_ge(sems["s"], sprev)
                        if vprev2:
                            te.wait_ge(sems["v"], vprev2)
                        mark(ctx, te.matmul(ps_x[:, :], vecs["br"][:, oc * 128:(oc + 1) * 128],
                                            gfm[:, tb * CH:(tb + 1) * CH], start=True, stop=True), "te")
                    ctx.te(tbr)
                    tw = ctx.n["te"]
                    def vas(v, oc=oc, j=j, tw=tw, gw2=gw2):
                        v.wait_ge(sems["te"], tw)
                        v.wait_ge(sems["gd"], gw2)
                        v.tensor_tensor(wk["sp1"][:, 0:CH], accf[:, j * CH:(j + 1) * CH], ps_x[:, :], OP.add)
                        v.tensor_tensor(wk["sp1"][:, 0:CH], wk["sp1"][:, 0:CH],
                                        wk["normed"][:, oc * CH:(oc + 1) * CH], OP.add)
                        mark(ctx, v.scalar_tensor_tensor(wk["mix"][:, oc * CH:(oc + 1) * CH],
                                                         wk["sp1"][:, 0:CH], bsum2[:, oc:oc + 1],
                                                         wk["normed"][:, oc * CH:(oc + 1) * CH],
                                                         OP.add, OP.add), "v")
                    ctx.v(vas)
            mix_tiles = [wk["mix"][:, oc * CH:(oc + 1) * CH] for oc in range(8)]
            emit_norm_stats(ctx, lnp, mix_tiles, CH, ps_mm[0], ps_mm[1], C, rms=False, vwait=ctx.n["v"])
            swn = ctx.n["s"]
            for oc in range(8):
                def vap(v, oc=oc, swn=swn):
                    v.wait_ge(sems["s"], swn)
                    v.tensor_tensor(wk["sp1"][:, 0:CH], wk["mix"][:, oc * CH:(oc + 1) * CH],
                                    lnp.mu_b[:, 0:CH], OP.subtract)
                    mark(ctx, v.tensor_tensor(wk["srcx"][:, oc * CH:(oc + 1) * CH],
                                              wk["sp1"][:, 0:CH], lnp.istd_b[:, 0:CH], OP.mult), "v")
                ctx.v(vap)
            vw = ctx.n["v"]
            def syu(sy, vw=vw):
                sy.wait_ge(sems["v"], vw)
            ctx.sy(syu)
            sload(ctx, stream_dram[:, t0:t0 + CH].rearrange("(c p) t -> p c t", p=128), wk["srcx"][:, :].rearrange("p (c t) -> p c t", c=8))
        ctx.barrier()

    ktm_dram = ctx.dram_tmp("ktm_dram", [TL, D])

    def emit_l1a():
        qs_sb = ctx.sbuf("qs_sb", [128, 8 * CH], BF16)
        kexp_sb = ctx.sbuf("kexp_sb", [128, 8 * CH], BF16)
        rkn_sb = ctx.sbuf("rkn_sb", [128, 8 * CH], BF16)
        Ssb = ctx.sbuf("Ssb", [128, 8 * 64], F32)
        Ssb_bf = ctx.sbuf("Ssb_bf", [128, 8 * 64], BF16)
        ksum_sb = ctx.sbuf("ksum_sb", [128, 10], F32)
        kcar = ctx.sbuf("kcar", [128, 9], F32)
        ktmp = ctx.sbuf("ktmp", [128, 64], F32)
        zt = ctx.sbuf("zt", [128, CH], BF16)
        rden_sb = ctx.sbuf("rden_sb", [16, CH], BF16)
        vtm_sb = ctx.sbuf("vtm_sb", [128, 1024], BF16)
        ktm_sb = ctx.sbuf("ktm_sb", [128, 1024], BF16)
        msb = ctx.sbuf("msb", [128, 2 * 128], BF16)
        def vz(v):
            v.memset(zt[:, :], 0.0)
            v.memset(ksum_sb[:, :], 0.0)
            mark(ctx, v.memset(Ssb[:, :], 0.0), "v")
        ctx.v(vz)
        for kc in range(8):
            gload(ctx, wbig[:, kc * 2048:(kc + 1) * 2048],
                  din["qkvT"][kc * 128:(kc + 1) * 128, 0:2048])
        gw = ctx.n["gd"] * 16
        ctx.barrier()
        for tb in range(8):
            t0 = tb * CH
            gw_consumers(te=True, v=True)
            gload(ctx, wk["normed"][:, :].rearrange("p (c t) -> p c t", c=8), stream_dram[:, t0:t0 + CH].rearrange("(c p) t -> p c t", p=128))
            gwu = ctx.n["gd"] * 16
            for oc in range(16):
                sprev = ctx.n["s"]
                def tqk(te, oc=oc, gw=gw, gwu=gwu, sprev=sprev):
                    te.wait_ge(sems["gd"], max(gw, gwu))
                    if sprev:
                        te.wait_ge(sems["s"], sprev)
                    for kc in range(8):
                        inst = te.matmul(ps_mm[oc % 2][:, :],
                                         wbig[:, kc * 2048 + oc * 128:kc * 2048 + (oc + 1) * 128],
                                         wk["normed"][:, kc * CH:(kc + 1) * CH],
                                         start=(kc == 0), stop=(kc == 7))
                    mark(ctx, inst, "te")
                ctx.te(tqk)
                tw = ctx.n["te"]
                def sqk(s, oc=oc, tw=tw):
                    s.wait_ge(sems["te"], tw)
                    mark(ctx, s.activation(wk["sp1"][:, (oc % 2) * CH:(oc % 2 + 1) * CH],
                                           ps_mm[oc % 2][:, :], AF.Exp, scale=0.125), "s")
                ctx.s(sqk)
                sw = ctx.n["s"]
                sdp = ctx.n["sd"] * 16
                if oc < 8:
                    def syq(sy, sw=sw):
                        sy.wait_ge(sems["s"], sw)
                    ctx.sy(syq)
                    sload(ctx, p_dram[oc * 128:(oc + 1) * 128, t0:t0 + CH],
                          wk["sp1"][:, (oc % 2) * CH:(oc % 2 + 1) * CH])
                else:
                    ko = oc - 8
                    def vke(v, ko=ko, oc=oc, tb=tb, sw=sw, sdp=sdp):
                        v.wait_ge(sems["s"], sw)
                        if sdp:
                            v.wait_ge(sems["sd"], sdp)
                        v.tensor_scalar(kexp_sb[:, ko * CH:(ko + 1) * CH],
                                        wk["sp1"][:, (oc % 2) * CH:(oc % 2 + 1) * CH],
                                        C["misc"][:, 4:5], None, OP.add)
                        mark(ctx, v.reduce_sum(ktmp[:, tb * 8 + ko:tb * 8 + ko + 1],
                                               kexp_sb[:, ko * CH:(ko + 1) * CH], axis=AX.X), "v")
                    ctx.v(vke)
                    vw = ctx.n["v"]
                    def syk(sy, vw=vw):
                        sy.wait_ge(sems["v"], vw)
                    ctx.sy(syk)
                    sload(ctx, kexp_dram[ko * 128:(ko + 1) * 128, t0:t0 + CH],
                          kexp_sb[:, ko * CH:(ko + 1) * CH])
        ctx.barrier()
        def vks(v):
            for ko in range(8):
                inst = v.reduce_sum(ksum_sb[:, ko:ko + 1], ktmp[:, ko::8], axis=AX.X)
            mark(ctx, inst, "v")
        ctx.v(vks)
        ctx.barrier()
        for kc in range(8):
            gload(ctx, wbig[:, kc * 1024:(kc + 1) * 1024],
                  din["qkvT"][kc * 128:(kc + 1) * 128, 2048:3072])
        gw = ctx.n["gd"] * 16
        ctx.barrier()
        nst_tot = [0]
        for tb in range(8):
            t0 = tb * CH
            gw_consumers(te=True, v=True, s=True)
            gload(ctx, wk["normed"][:, :].rearrange("p (c t) -> p c t", c=8), stream_dram[:, t0:t0 + CH].rearrange("(c p) t -> p c t", p=128))
            gload(ctx, kexp_sb[:, :].rearrange("p (c t) -> p c t", c=8), kexp_dram[:, t0:t0 + CH].rearrange("(c p) t -> p c t", p=128))
            gwu = ctx.n["gd"] * 16
            for oc in range(8):
                sprev = ctx.n["s"]
                def tv(te, oc=oc, gw=gw, gwu=gwu, sprev=sprev):
                    te.wait_ge(sems["gd"], max(gw, gwu))
                    if sprev:
                        te.wait_ge(sems["s"], sprev)
                    for kc in range(8):
                        inst = te.matmul(ps_mm[oc % 2][:, :],
                                         wbig[:, kc * 1024 + oc * 128:kc * 1024 + (oc + 1) * 128],
                                         wk["normed"][:, kc * CH:(kc + 1) * CH],
                                         start=(kc == 0), stop=(kc == 7))
                    mark(ctx, inst, "te")
                ctx.te(tv)
                tw = ctx.n["te"]
                def sv(s, oc=oc, tw=tw):
                    s.wait_ge(sems["te"], tw)
                    mark(ctx, s.copy(wk["mix"][:, oc * CH:(oc + 1) * CH], ps_mm[oc % 2][:, :]), "s")
                ctx.s(sv)
            swv = ctx.n["s"]
            for tsc in range(4):
                sdp = ctx.n["sd"] * 16
                for fc in range(8):
                    for which, srct, dstt in [(0, wk["mix"], vtm_sb), (1, kexp_sb, ktm_sb)]:
                        sprev = ctx.n["s"]
                        def ttr(te, tsc=tsc, fc=fc, srct=srct, swv=swv, sprev=sprev, sdp=sdp):
                            te.wait_ge(sems["s"], max(swv, sprev))
                            if sdp:
                                te.wait_ge(sems["sd"], sdp)
                            mark(ctx, te.transpose(ps_tp[:, :],
                                                   srct[:, fc * CH + tsc * 128:fc * CH + (tsc + 1) * 128],
                                                   C["ident_bf"][:, :]), "te")
                        ctx.te(ttr)
                        tw = ctx.n["te"]
                        sdpt = ctx.n["sd"] * 16
                        def str_(s, fc=fc, dstt=dstt, tw=tw, sdpt=sdpt):
                            s.wait_ge(sems["te"], tw)
                            if sdpt:
                                s.wait_ge(sems["sd"], sdpt)
                            mark(ctx, s.copy(dstt[:, fc * 128:(fc + 1) * 128], ps_tp[:, :]), "s")
                        ctx.s(str_)
                swt = ctx.n["s"]
                step = tb * 4 + tsc
                def tst(te, swt=swt, step=step):
                    te.wait_ge(sems["s"], swt)
                    for h in range(16):
                        j, hp = h % 2, h // 2
                        inst = te.matmul(ps_bl[0][j * 64:(j + 1) * 64, hp * 64:(hp + 1) * 64],
                                         ktm_sb[:, h * 64:(h + 1) * 64],
                                         vtm_sb[:, h * 64:(h + 1) * 64],
                                         start=(step == 0), stop=(step == 31))
                    mark(ctx, inst, "te")
                ctx.te(tst)
                twd = ctx.n["te"]
                def syt(sy, twd=twd):
                    sy.wait_ge(sems["te"], twd)
                ctx.sy(syt)
                sload(ctx, vtm_dram[t0 + tsc * 128:t0 + (tsc + 1) * 128, :], vtm_sb[:, :])
                sload(ctx, ktm_dram[t0 + tsc * 128:t0 + (tsc + 1) * 128, :], ktm_sb[:, :])
        ctx.barrier()
        def sst(s):
            mark(ctx, s.copy(Ssb[:, :], ps_bl[0][:, :]), "s")
        ctx.s(sst)
        ctx.barrier()
        sload(ctx, sag_in[:, 0:512], Ssb[:, :])
        sload(ctx, sag_in[:, 512:520], ksum_sb[:, 0:8])
        ctx.barrier()
        collective(ctx, "AllGather", OP.bypass, PAIRS, sag_in, sag_out)
        ctx.barrier()
        sini = ctx.sbuf("sini", [128, 520], F32)
        knf = ctx.sbuf("knf", [128, CH], F32)
        gload(ctx, sini[:, :], sag_out[0:128, :])
        for kc in range(8):
            gload(ctx, wbig[:, kc * 1024:(kc + 1) * 1024], din["woutT"][kc * 128:(kc + 1) * 128, :])
        ctx.barrier()
        def vsi(v):
            v.tensor_scalar(Ssb[:, :], sini[:, 0:512], C["flag"][:, 0:1], None, OP.mult)
            mark(ctx, v.tensor_copy(Ssb_bf[:, :], Ssb[:, :]), "v")
        ctx.v(vsi)
        _vw = ctx.n["v"]
        def ssi(s, _vw=_vw):
            s.wait_ge(sems["v"], _vw)
            mark(ctx, s.activation(kcar[:, 0:8], sini[:, 512:520], AF.Identity,
                                   scale=C["flag"][:, 0:1]), "s")
        ctx.s(ssi)
        ctx.barrier()
        for tb in range(8):
            t0 = tb * CH
            gw_consumers(te=True, v=True, s=True)
            gload(ctx, wk["normed"][:, :].rearrange("p (c t) -> p c t", c=8), p_dram[:, t0:t0 + CH].rearrange("(c p) t -> p c t", p=128))
            gload(ctx, kexp_sb[:, :].rearrange("p (c t) -> p c t", c=8), kexp_dram[:, t0:t0 + CH].rearrange("(c p) t -> p c t", p=128))
            gwp = ctx.n["gd"] * 16
            for oc in range(8):
                vprev = ctx.n["v"]
                sprev = ctx.n["s"]
                def vkn(v, oc=oc, gwp=gwp, sprev=sprev):
                    v.wait_ge(sems["gd"], gwp)
                    if sprev:
                        v.wait_ge(sems["s"], sprev)
                    mark(ctx, v.tensor_tensor_scan(knf[:, :], kexp_sb[:, oc * CH:(oc + 1) * CH],
                                         zt[:, :], kcar[:, oc:oc + 1], OP.add, OP.add), "v")
                ctx.v(vkn)
                vw = ctx.n["v"]
                def skn(s, oc=oc, vw=vw):
                    s.wait_ge(sems["v"], vw)
                    mark(ctx, s.copy(kcar[:, oc:oc + 1], knf[:, CH - 1:CH]), "s")
                ctx.s(skn)
                def vkr(v, oc=oc):
                    with ctx.nc.allow_low_precision(reason="1/knorm to bf16 is ample here"):
                        mark(ctx, v.reciprocal(rkn_sb[:, oc * CH:(oc + 1) * CH], knf[:, :]), "v")
                ctx.v(vkr)
            swd = ctx.n["s"]
            def tdn(te, gwp=gwp, swd=swd):
                te.wait_ge(sems["gd"], gwp)
                te.wait_ge(sems["s"], swd)
                for hp in range(8):
                    inst = te.matmul(ps_x[0:16, :], C["selH"][:, hp * 16:(hp + 1) * 16],
                                     wk["normed"][:, hp * CH:(hp + 1) * CH],
                                     start=(hp == 0), stop=(hp == 7))
                mark(ctx, inst, "te")
            ctx.te(tdn)
            tw = ctx.n["te"]
            def sdn(v, tw=tw):
                v.wait_ge(sems["te"], tw)
                with ctx.nc.allow_low_precision(reason="1/denom to bf16 is ample here"):
                    mark(ctx, v.reciprocal(rden_sb[:, :], ps_x[0:16, :]), "v")
            ctx.v(sdn)
            swd2 = ctx.n["s"]
            for oc in range(8):
                vprev = ctx.n["v"]
                def trb(te, oc=oc, swd2=swd2, vprev=vprev):
                    te.wait_ge(sems["s"], swd2)
                    if vprev:
                        te.wait_ge(sems["v"], vprev)
                    mark(ctx, te.matmul(ps_x[:, :], C["selHB"][:, oc * 128:(oc + 1) * 128],
                                        rden_sb[:, :], start=True, stop=True), "te")
                ctx.te(trb)
                tw = ctx.n["te"]
                def srb(s, tw=tw):
                    s.wait_ge(sems["te"], tw)
                    mark(ctx, s.copy(wk["sp1"][:, 0:CH], ps_x[:, :]), "s")
                ctx.s(srb)
                sw = ctx.n["s"]
                def vqs(v, oc=oc, sw=sw):
                    v.wait_ge(sems["s"], sw)
                    v.tensor_tensor(qs_sb[:, oc * CH:(oc + 1) * CH],
                                    wk["normed"][:, oc * CH:(oc + 1) * CH],
                                    rkn_sb[:, oc * CH:(oc + 1) * CH], OP.mult)
                    mark(ctx, v.tensor_tensor(qs_sb[:, oc * CH:(oc + 1) * CH],
                                              qs_sb[:, oc * CH:(oc + 1) * CH],
                                              wk["sp1"][:, 0:CH], OP.mult), "v")
                ctx.v(vqs)
            for tsc in range(4):
                tprev = ctx.n["te"]
                def syc(sy, tprev=tprev):
                    if tprev:
                        sy.wait_ge(sems["te"], tprev)
                ctx.sy(syc)
                sload(ctx, vtm_sb[:, :], vtm_dram[t0 + tsc * 128:t0 + (tsc + 1) * 128, :])
                sload(ctx, ktm_sb[:, :], ktm_dram[t0 + tsc * 128:t0 + (tsc + 1) * 128, :])
                sdw = ctx.n["sd"] * 16
                vw_all = ctx.n["v"]
                for hp in range(8):
                    sprev = ctx.n["s"]
                    vprev = ctx.n["v"]
                    def tsco(te, hp=hp, tsc=tsc, sdw=sdw, vw_all=vw_all, sprev=sprev, vprev=vprev):
                        te.wait_ge(sems["sd"], sdw)
                        te.wait_ge(sems["v"], max(vw_all, vprev))
                        if sprev:
                            te.wait_ge(sems["s"], sprev)
                        for j in range(2):
                            mark(ctx, te.matmul(ps_mm[j][:, 0:128],
                                                kexp_sb[j * 64:(j + 1) * 64,
                                                        hp * CH + tsc * 128:hp * CH + (tsc + 1) * 128],
                                                qs_sb[j * 64:(j + 1) * 64,
                                                      hp * CH + tsc * 128:hp * CH + (tsc + 1) * 128],
                                                start=True, stop=True), "te")
                    ctx.te(tsco)
                    tw = ctx.n["te"]
                    def vmask(v, tw=tw):
                        v.wait_ge(sems["te"], tw)
                        v.tensor_tensor(msb[:, 0:128], ps_mm[0][:, 0:128], C["cmask"][:, :], OP.mult)
                        mark(ctx, v.tensor_tensor(msb[:, 128:256], ps_mm[1][:, 0:128],
                                                  C["cmask"][:, :], OP.mult), "v")
                    ctx.v(vmask)
                    vwm = ctx.n["v"]
                    sprev3 = ctx.n["s"]
                    def tyy(te, hp=hp, tsc=tsc, vwm=vwm, sprev3=sprev3):
                        te.wait_ge(sems["v"], vwm)
                        if sprev3 > 1:
                            te.wait_ge(sems["s"], sprev3 - 1)
                        for j in range(2):
                            h = 2 * hp + j
                            te.matmul(ps_y[hp % 2][j * 64:(j + 1) * 64, 0:128],
                                      vtm_sb[:, h * 64:(h + 1) * 64], msb[:, j * 128:(j + 1) * 128],
                                      start=True, stop=False)
                            te.matmul(ps_y[hp % 2][j * 64:(j + 1) * 64, 0:128],
                                      Ssb_bf[j * 64:(j + 1) * 64, hp * 64:(hp + 1) * 64],
                                      qs_sb[j * 64:(j + 1) * 64,
                                            hp * CH + tsc * 128:hp * CH + (tsc + 1) * 128],
                                      start=False, stop=True)
                            inst = te.matmul(ps_bl[hp % 2][j * 64:(j + 1) * 64, 0:64],
                                             ktm_sb[:, h * 64:(h + 1) * 64],
                                             vtm_sb[:, h * 64:(h + 1) * 64],
                                             start=True, stop=True)
                        mark(ctx, inst, "te")
                    ctx.te(tyy)
                    tw2 = ctx.n["te"]
                    def sev(s, hp=hp, tsc=tsc, tw2=tw2):
                        s.wait_ge(sems["te"], tw2)
                        mark(ctx, s.copy(wk["mix"][:, hp * CH + tsc * 128:hp * CH + (tsc + 1) * 128],
                                         ps_y[hp % 2][:, 0:128]), "s")
                    ctx.s(sev)
                    sw2 = ctx.n["s"]
                    def vsu(v, hp=hp, sw2=sw2, tw2=tw2):
                        v.wait_ge(sems["te"], tw2)
                        for j in range(2):
                            v.tensor_tensor(Ssb[j * 64:(j + 1) * 64, hp * 64:(hp + 1) * 64],
                                            Ssb[j * 64:(j + 1) * 64, hp * 64:(hp + 1) * 64],
                                            ps_bl[hp % 2][j * 64:(j + 1) * 64, 0:64], OP.add)
                        mark(ctx, v.tensor_copy(Ssb_bf[:, hp * 64:(hp + 1) * 64],
                                                Ssb[:, hp * 64:(hp + 1) * 64]), "v")
                    ctx.v(vsu)
            if dbg_stage == "l1q":
                _v = ctx.n["v"]
                def syq2(sy, _v=_v):
                    sy.wait_ge(sems["v"], _v)
                ctx.sy(syq2)
                sload(ctx, v_dram[0:D, t0:t0 + CH].rearrange("(c p) t -> p c t", p=128),
                      qs_sb[:, :].rearrange("p (c t) -> p c t", c=8))
            if dbg_stage == "l1y":
                _s = ctx.n["s"]
                def syy2(sy, _s=_s):
                    sy.wait_ge(sems["s"], _s)
                ctx.sy(syy2)
                sload(ctx, v_dram[0:D, t0:t0 + CH].rearrange("(c p) t -> p c t", p=128),
                      wk["mix"][:, :].rearrange("p (c t) -> p c t", c=8))
            gw_consumers(te=True, v=True)
            gload(ctx, wk["normed"][:, :].rearrange("p (c t) -> p c t", c=8), stream_dram[:, t0:t0 + CH].rearrange("(c p) t -> p c t", p=128))
            gws = ctx.n["gd"] * 16
            vwc = ctx.n["v"]
            swc = ctx.n["s"]
            for oc in range(8):
                sprev = ctx.n["s"]
                def two_(te, oc=oc, swc=swc, sprev=sprev):
                    te.wait_ge(sems["s"], max(swc, sprev))
                    for hp in range(8):
                        inst = te.matmul(ps_mm[oc % 2][:, :],
                                         wbig[:, hp * 1024 + oc * 128:hp * 1024 + (oc + 1) * 128],
                                         wk["mix"][:, hp * CH:(hp + 1) * CH],
                                         start=(hp == 0), stop=(hp == 7))
                    mark(ctx, inst, "te")
                ctx.te(two_)
                tw = ctx.n["te"]
                def swo(s, oc=oc, tw=tw):
                    s.wait_ge(sems["te"], tw)
                    mark(ctx, s.copy(wk["srcx"][:, oc * CH:(oc + 1) * CH], ps_mm[oc % 2][:, :]), "s")
                ctx.s(swo)
            sww = ctx.n["s"]
            def vres(v, sww=sww, gws=gws):
                v.wait_ge(sems["s"], sww)
                v.wait_ge(sems["gd"], gws)
                for oc in range(8):
                    inst = v.tensor_tensor(wk["mix"][:, oc * CH:(oc + 1) * CH],
                                           wk["srcx"][:, oc * CH:(oc + 1) * CH],
                                           wk["normed"][:, oc * CH:(oc + 1) * CH], OP.add)
                mark(ctx, inst, "v")
            ctx.v(vres)
            mix_tiles = [wk["mix"][:, oc * CH:(oc + 1) * CH] for oc in range(8)]
            emit_norm_stats(ctx, lnp, mix_tiles, CH, ps_mm[0], ps_mm[1], C, rms=False, vwait=ctx.n["v"])
            swn = ctx.n["s"]
            for oc in range(8):
                def vap(v, oc=oc, swn=swn):
                    v.wait_ge(sems["s"], swn)
                    v.tensor_tensor(wk["sp1"][:, 0:CH], wk["mix"][:, oc * CH:(oc + 1) * CH],
                                    lnp.mu_b[:, 0:CH], OP.subtract)
                    mark(ctx, v.tensor_tensor(wk["srcx"][:, oc * CH:(oc + 1) * CH],
                                              wk["sp1"][:, 0:CH], lnp.istd_b[:, 0:CH], OP.mult), "v")
                ctx.v(vap)
            vw = ctx.n["v"]
            def syu(sy, vw=vw):
                sy.wait_ge(sems["v"], vw)
            ctx.sy(syu)
            sload(ctx, stream_dram[:, t0:t0 + CH].rearrange("(c p) t -> p c t", p=128), wk["srcx"][:, :].rearrange("p (c t) -> p c t", c=8))
            ctx.barrier()

    def emit_l1f():
        fq32 = ctx.sbuf("fq32", [128, 1024], F32)
        vq1 = ctx.sbuf("vq1", [128, 1024], F32)
        vq2 = ctx.sbuf("vq2", [128, 1024], F32)
        qcols = ctx.sbuf("qcols", [128, 24], F32)
        fout = ctx.sbuf("fout", [128, 2 * CH], F32)
        xq_sb = ctx.sbuf("xq_sb", [128, 8 * CH], BF16)
        knf2 = ctx.sbuf("knf2", [128, CH], F32)
        misc = C["misc"]
        def vz0(v):
            mark(ctx, v.memset(qcols[:, :], 0.0), "v")
        ctx.v(vz0)
        ctx.barrier()
        for kc in range(8):
            sload(ctx, fq32[:, :], din["fqT"][kc * 128:(kc + 1) * 128, :])
            sdw = ctx.n["sd"] * 16
            def vwa(v, kc=kc, sdw=sdw):
                v.wait_ge(sems["sd"], sdw)
                v.reduce_max(qcols[:, 0:1], fq32[:, :], axis=AX.X, apply_absolute_value=True)
                if kc == 0:
                    inst = v.tensor_copy(qcols[:, 1:2], qcols[:, 0:1])
                else:
                    inst = v.tensor_tensor(qcols[:, 1:2], qcols[:, 1:2], qcols[:, 0:1], OP.max)
                mark(ctx, inst, "v")
            ctx.v(vwa)
            ctx.barrier()
        colmax_bcast(qcols[:, 1:2], qcols[:, 2:3])
        ctx.barrier()
        def vsw(v):
            mark(ctx, v.tensor_scalar(qcols[:, 16:17], qcols[:, 2:3], misc[:, 7:8], None, OP.max), "v")
        ctx.v(vsw)
        _vw = ctx.n["v"]
        def ssw(s, _vw=_vw):
            s.wait_ge(sems["v"], _vw)
            mark(ctx, s.copy(qcols[:, 18:19], qcols[:, 16:17]), "s")
        ctx.s(ssw)
        _sw = ctx.n["s"]
        def vsw2(v, _sw=_sw):
            v.wait_ge(sems["s"], _sw)
            mark(ctx, v.reciprocal(qcols[:, 20:21], qcols[:, 18:19]), "v")
        ctx.v(vsw2)
        _vw = ctx.n["v"]
        def ssw2(s, _vw=_vw):
            s.wait_ge(sems["v"], _vw)
            mark(ctx, s.activation(qcols[:, 3:4], qcols[:, 20:21], AF.Identity,
                                   scale=misc[:, 6:7]), "s")
        ctx.s(ssw2)
        ctx.barrier()
        for kc in range(8):
            sload(ctx, fq32[:, :], din["fqT"][kc * 128:(kc + 1) * 128, :])
            sdw = ctx.n["sd"] * 16
            def vq(v, kc=kc, sdw=sdw):
                v.wait_ge(sems["sd"], sdw)
                v.tensor_scalar(vq1[:, :], fq32[:, :], qcols[:, 3:4], None, OP.mult)
                v.tensor_scalar(vq2[:, :], vq1[:, :], misc[:, 5:6], None, OP.mult)
                v.tensor_tensor(vq1[:, :], vq2[:, :], vq1[:, :], OP.subtract)
                mark(ctx, v.tensor_tensor(wbig[:, kc * 1024:(kc + 1) * 1024], vq2[:, :], vq1[:, :],
                                          OP.subtract), "v")
            ctx.v(vq)
            ctx.barrier()
        for tb in range(8):
            t0 = tb * CH
            gw_consumers(v=True)
            gload(ctx, wk["normed"][:, :].rearrange("p (c t) -> p c t", c=8), stream_dram[:, t0:t0 + CH].rearrange("(c p) t -> p c t", p=128))
            gw = ctx.n["gd"] * 16
            def vxa(v, tb=tb, gw=gw):
                v.wait_ge(sems["gd"], gw)
                for oc in range(8):
                    v.reduce_max(qcols[:, 4:5], wk["normed"][:, oc * CH:(oc + 1) * CH],
                                 axis=AX.X, apply_absolute_value=True)
                    if tb == 0 and oc == 0:
                        inst = v.tensor_copy(qcols[:, 5:6], qcols[:, 4:5])
                    else:
                        inst = v.tensor_tensor(qcols[:, 5:6], qcols[:, 5:6], qcols[:, 4:5], OP.max)
                mark(ctx, inst, "v")
            ctx.v(vxa)
            ctx.barrier()
        colmax_bcast(qcols[:, 5:6], qcols[:, 6:7])
        ctx.barrier()
        sload(ctx, amax_in, qcols[0:1, 6:14])
        ctx.barrier()
        collective(ctx, "AllReduce", OP.max, ALL8, amax_in, amax_out)
        ctx.barrier()
        gload(ctx, qcols[0:1, 7:8], amax_out[0:1, 0:1])
        ctx.barrier()
        def tbc2(te):
            mark(ctx, te.matmul(ps_x[:, 0:1], C["ones_f32"][:, :], qcols[0:1, 7:8],
                                start=True, stop=True), "te")
        ctx.te(tbc2)
        tw_b = ctx.n["te"]
        def vbc2(v, tw_b=tw_b):
            v.wait_ge(sems["te"], tw_b)
            mark(ctx, v.tensor_copy(qcols[:, 8:9], ps_x[:, 0:1]), "v")
        ctx.v(vbc2)
        ctx.barrier()
        def vsx(v):
            mark(ctx, v.tensor_scalar(qcols[:, 17:18], qcols[:, 8:9], misc[:, 7:8], None, OP.max), "v")
        ctx.v(vsx)
        _vw = ctx.n["v"]
        def ssx(s, _vw=_vw):
            s.wait_ge(sems["v"], _vw)
            mark(ctx, s.copy(qcols[:, 19:20], qcols[:, 17:18]), "s")
        ctx.s(ssx)
        _sw = ctx.n["s"]
        def vsx2(v, _sw=_sw):
            v.wait_ge(sems["s"], _sw)
            v.reciprocal(qcols[:, 21:22], qcols[:, 19:20])
            mark(ctx, v.scalar_tensor_tensor(qcols[:, 22:23], qcols[:, 18:19], misc[:, 8:9],
                                             qcols[:, 19:20], OP.mult, OP.mult), "v")
        ctx.v(vsx2)
        _vw = ctx.n["v"]
        def ssx2(s, _vw=_vw):
            s.wait_ge(sems["v"], _vw)
            s.activation(qcols[:, 9:10], qcols[:, 21:22], AF.Identity, scale=misc[:, 6:7])
            mark(ctx, s.copy(qcols[:, 10:11], qcols[:, 22:23]), "s")
        ctx.s(ssx2)
        ctx.barrier()
        if dbg_stage == "l1fs":
            _v = ctx.n["v"]
            _s = ctx.n["s"]
            def syd(sy, _v=_v, _s=_s):
                sy.wait_ge(sems["v"], _v)
                sy.wait_ge(sems["s"], _s)
            ctx.sy(syd)
            sload(ctx, out_dram[0:128, 0:16], qcols[:, :])
            ctx.barrier()
            return
        for tb in range(8):
            t0 = tb * CH
            gw_consumers(te=True, v=True, s=True)
            gload(ctx, wk["normed"][:, :].rearrange("p (c t) -> p c t", c=8), stream_dram[:, t0:t0 + CH].rearrange("(c p) t -> p c t", p=128))
            gw = ctx.n["gd"] * 16
            for oc in range(8):
                def vqx(v, oc=oc, gw=gw):
                    v.wait_ge(sems["gd"], gw)
                    v.tensor_scalar(vq1[:, 0:CH], wk["normed"][:, oc * CH:(oc + 1) * CH],
                                    qcols[:, 9:10], None, OP.mult)
                    v.tensor_scalar(vq2[:, 0:CH], vq1[:, 0:CH], misc[:, 5:6], None, OP.mult)
                    v.tensor_tensor(knf2[:, :], vq2[:, 0:CH], vq1[:, 0:CH], OP.subtract)
                    mark(ctx, v.tensor_tensor(xq_sb[:, oc * CH:(oc + 1) * CH], vq2[:, 0:CH],
                                              knf2[:, :], OP.subtract), "v")
                ctx.v(vqx)
            vwq = ctx.n["v"]
            for oc in range(8):
                vprev = ctx.n["v"]
                def tfq(te, oc=oc, vwq=vwq, vprev=vprev):
                    te.wait_ge(sems["v"], max(vwq, vprev))
                    for kc in range(8):
                        inst = te.matmul(ps_mm[oc % 2][:, :],
                                         wbig[:, kc * 1024 + oc * 128:kc * 1024 + (oc + 1) * 128],
                                         xq_sb[:, kc * CH:(kc + 1) * CH],
                                         start=(kc == 0), stop=(kc == 7))
                    mark(ctx, inst, "te")
                ctx.te(tfq)
                tw = ctx.n["te"]
                def vev(v, oc=oc, tw=tw):
                    v.wait_ge(sems["te"], tw)
                    v.tensor_scalar(wk["sp1"][:, 0:CH], ps_mm[oc % 2][:, :],
                                    qcols[:, 10:11], vecs["fqb"][:, oc:oc + 1], OP.mult, OP.add)
                    mark(ctx, v.tensor_tensor(wk["mix"][:, oc * CH:(oc + 1) * CH],
                                              wk["sp1"][:, 0:CH],
                                              wk["normed"][:, oc * CH:(oc + 1) * CH], OP.add), "v")
                ctx.v(vev)
            mix_tiles = [wk["mix"][:, oc * CH:(oc + 1) * CH] for oc in range(8)]
            emit_norm_stats(ctx, lnp, mix_tiles, CH, ps_mm[0], ps_mm[1], C, rms=False, vwait=ctx.n["v"])
            swn = ctx.n["s"]
            for oc in range(8):
                sdp = ctx.n["sd"] * 16
                def vapf(v, oc=oc, swn=swn, sdp=sdp):
                    v.wait_ge(sems["s"], swn)
                    if oc >= 2:
                        v.wait_ge(sems["sd"], sdp)
                    v.tensor_tensor(wk["sp1"][:, 0:CH], wk["mix"][:, oc * CH:(oc + 1) * CH],
                                    lnp.mu_b[:, 0:CH], OP.subtract)
                    mark(ctx, v.tensor_tensor(fout[:, (oc % 2) * CH:(oc % 2 + 1) * CH],
                                              wk["sp1"][:, 0:CH], lnp.istd_b[:, 0:CH], OP.mult), "v")
                ctx.v(vapf)
                vwf = ctx.n["v"]
                def syo(sy, vwf=vwf):
                    sy.wait_ge(sems["v"], vwf)
                ctx.sy(syo)
                sload(ctx, out_dram[oc * 128:(oc + 1) * 128, t0:t0 + CH],
                      fout[:, (oc % 2) * CH:(oc % 2 + 1) * CH])
            ctx.barrier()

    def dump_stream(src_dram_ap):
        ctx.barrier()
        def gdump(g):
            mark(ctx, g.dma_start(out=out_dram, in_=src_dram_ap), "gd")
        ctx.g(gdump)
        ctx.barrier()

    if dbg_stage == "l0a":
        dump_stream(res_dram)
    else:
        ctx.barrier()
        with nc.named_scope("ffn"):
            emit_ffn()
        if dbg_stage == "ffv":
            dump_stream(v_dram[0:D, :])
        elif dbg_stage == "ff2":
            dump_stream(stream_dram)
        elif dbg_stage == "ffn":
            dump_stream(stream_dram)
        else:
            ctx.push_scope()
            with nc.named_scope("moe"):
                emit_moe()
            ctx.pop_scope()
            if dbg_stage in ("moe", "moeg", "moer"):
                if dbg_stage == "moe":
                    dump_stream(stream_dram)
            else:
                ctx.push_scope()
                with nc.named_scope("l1a"):
                    emit_l1a()
                ctx.pop_scope()
                if dbg_stage in ("l1a", "l1q", "l1y"):
                    dump_stream(stream_dram if dbg_stage == "l1a" else v_dram[0:D, :])
                elif dbg_stage in (None, "l1fs"):
                    ctx.push_scope()
                    with nc.named_scope("l1f"):
                        emit_l1f()
                    ctx.pop_scope()
    ctx.barrier()
    return ctx



def kernel(**inputs):
    shards = stage_inputs(inputs)
    ctx = build_program(dbg_stage=DBG_STAGE[0])
    ctx.stack.close()
    res = run_bass_kernel_spmd(ctx.nc, shards, core_ids=list(range(8)))
    out = np.zeros((B, T, D), np.float32)
    for c in range(8):
        b, half = c // 2, c % 2
        out[b, half * TL:(half + 1) * TL, :] = res.results[c]["out"].T
    return out


# version 05
# speedup vs baseline: 1.0270x; 1.0270x over previous
import math
import contextlib
import numpy as np
import ml_dtypes
import concourse.bass as bass
import concourse.mybir as mybir
from concourse.bass_utils import run_bass_kernel_spmd
from concourse.alu_op_type import AluOpType as OP
from bass_rust import ReduceOp

F32 = mybir.dt.float32
BF16 = mybir.dt.bfloat16
I32 = mybir.dt.int32
AF = mybir.ActivationFunctionType
AX = mybir.AxisListType
BF = ml_dtypes.bfloat16

B, T, D = 4, 8192, 1024
TL = 4096
CH = 512
NCH = TL // CH
NG = 2 * NCH
DI = 2048
DH = DI // 2
DS = 16
DCV = 4
DTR = 64
FF = 4096
NH, HL = 16, 64
NE, NSH = 16, 2
EPS = 1e-5
VELT_C = float(2 ** 20 + 1)
FP8_MAX = 448.0

DBG_STAGE = [None]



def _t(a):
    return np.ascontiguousarray(np.asarray(a).T)


def _cols(vec, p=128):
    v = np.asarray(vec, np.float32).reshape(-1, p)
    return np.ascontiguousarray(v.T)


def stage_inputs(inputs):
    x = np.asarray(inputs["L0_attn_x"])
    shards = []
    shared = {}
    for li in range(2):
        shared[f"m{li}_inprojT"] = _t(inputs["L0_attn_in_proj_w"][li])
        shared[f"m{li}_convw"] = _cols(np.asarray(inputs["L0_attn_conv_w"][li]).T.reshape(-1))
        shared[f"m{li}_convb"] = _cols(inputs["L0_attn_conv_b"][li])
        shared[f"m{li}_xpT"] = _t(inputs["L0_attn_x_proj_w"][li])
        shared[f"m{li}_dtT"] = _t(inputs["L0_attn_dt_proj_w"][li])
        shared[f"m{li}_dtb"] = _cols(inputs["L0_attn_dt_proj_b"][li])
        shared[f"m{li}_outT"] = _t(inputs["L0_attn_out_proj_w"][li])
        shared[f"m{li}_D"] = _cols(inputs["L0_attn_D"][li])
        shared[f"m{li}_rmsw"] = _cols(inputs["L0_attn_rms_w"][li])
        shared[f"m{li}_lnw"] = _cols(inputs["L0_attn_ln_w"][li])
        shared[f"m{li}_lnb"] = _cols(inputs["L0_attn_ln_b"][li])
    for li in range(2):
        cw = np.asarray(inputs["L0_attn_conv_w"][li], np.float32)
        shared[f"m{li}_convw"] = np.ascontiguousarray(
            cw.reshape(16, 128, 4).transpose(1, 0, 2).reshape(128, 64))
    shared["ff1T"] = _t(inputs["L0_attn_ff_w1"])
    shared["ffb1"] = _cols(inputs["L0_attn_ff_b1"])
    shared["ff2T"] = _t(inputs["L0_attn_ff_w2"])
    shared["ffb2"] = _cols(inputs["L0_attn_ff_b2"])
    shared["centT"] = _t(inputs["L0_ffn_centroids"])
    shared["eb_row"] = np.ascontiguousarray(
        np.broadcast_to(np.asarray(inputs["L0_ffn_expert_biases"], np.float32)[None, :], (128, NE)))
    shared["WrT"] = np.ascontiguousarray(
        np.asarray(inputs["L0_ffn_Wr"], np.float32).transpose(0, 2, 1))
    shared["br"] = np.asarray(inputs["L0_ffn_br"], np.float32)
    shared["WsT"] = np.ascontiguousarray(
        np.asarray(inputs["L0_ffn_Ws"], np.float32).transpose(0, 2, 1))
    shared["bs"] = _cols(np.asarray(inputs["L0_ffn_bs"], np.float32).reshape(-1))
    shared["qkvT"] = _t(inputs["L1_attn_w"])
    shared["woutT"] = np.ascontiguousarray(np.asarray(inputs["L1_attn_w_out"], np.float32))
    shared["fqT"] = _t(inputs["L1_ffn_weight"])
    shared["fqb"] = _cols(inputs["L1_ffn_bias"])

    ident = np.eye(128, dtype=np.float32)
    consts = {
        "identity": ident,
        "ones1": np.ones((1, 128), np.float32),
        "cmask": np.triu(np.ones((128, 128), np.float32)),
        "selB": np.stack([np.eye(32, dtype=np.float32)[n].repeat(1) for n in range(16)], 0),
    }
    selB = np.zeros((16, 32, 128), np.float32)
    selC = np.zeros((16, 32, 128), np.float32)
    for n in range(16):
        selB[n, n, :] = 1.0
        selC[n, 16 + n, :] = 1.0
    consts["selB"] = np.ascontiguousarray(selB.transpose(1, 0, 2).reshape(32, 16 * 128))
    consts["selC"] = np.ascontiguousarray(selC.transpose(1, 0, 2).reshape(32, 16 * 128))
    selH = np.zeros((8, 128, 16), np.float32)
    for c in range(8):
        for j in range(2):
            selH[c, j * 64:(j + 1) * 64, 2 * c + j] = 1.0
    consts["selH"] = np.ascontiguousarray(selH.reshape(8 * 128, 16))
    selHB = np.zeros((8, 16, 128), np.float32)
    for c in range(8):
        for j in range(2):
            selHB[c, 2 * c + j, j * 64:(j + 1) * 64] = 1.0
    consts["selHB"] = np.ascontiguousarray(selHB.transpose(1, 0, 2).reshape(16, 8 * 128))
    sel16 = np.zeros((16, 16, 128), np.float32)
    for e in range(16):
        sel16[e, e, :] = 1.0
    consts["sel16"] = np.ascontiguousarray(sel16.transpose(1, 0, 2).reshape(16, 16 * 128))
    misc = np.zeros((128, 12), np.float32)
    misc[:, 0] = 1.0
    misc[:, 1] = EPS
    misc[:, 2] = 1.0 / 1024.0
    misc[:, 3] = 0.5
    misc[:, 4] = 1e-6
    misc[:, 5] = VELT_C
    misc[:, 6] = FP8_MAX
    misc[:, 7] = 1e-12
    misc[:, 8] = 1.0 / (FP8_MAX * FP8_MAX)
    consts["misc"] = misc

    for c in range(8):
        b, half = c // 2, c % 2
        sh = {}
        sh.update(shared)
        sh.update(consts)
        xs = x[b, half * TL:(half + 1) * TL, :]
        sh["x"] = _t(xs)
        f = float(half)
        sh["flag"] = np.full((128, 1), f, np.float32)
        sh["sel_lo"] = ident * (1.0 - f)
        sh["sel_hi"] = ident * f
        for li in range(2):
            al = np.asarray(inputs["L0_attn_A_log"][li], np.float32)[half * DH:(half + 1) * DH]
            sh[f"m{li}_Alog"] = np.ascontiguousarray(
                al.reshape(8, 128, 16).transpose(1, 0, 2).reshape(128, 128))
        shards.append(sh)
    return shards



class Ctx:
    def __init__(self):
        self.nc = bass.Bass("TRN2", target_bir_lowering=False)
        self.stack = contextlib.ExitStack()
        self.sb_free = {}
        self.dram = {}
        self.n = dict(te=0, v=0, s=0, g=0, gd=0, sd=0)
        self.scopes = []
        self.block = None
        self.sem = {}

    def dram_in(self, name, shape, dtype=F32):
        ap = self.nc.dram_tensor(name, list(shape), dtype, kind="ExternalInput").ap()
        self.dram[name] = ap
        return ap

    def dram_out(self, name, shape, dtype=F32):
        ap = self.nc.dram_tensor(name, list(shape), dtype, kind="ExternalOutput").ap()
        self.dram[name] = ap
        return ap

    def dram_tmp(self, name, shape, dtype=BF16):
        ap = self.nc.dram_tensor(name, list(shape), dtype, kind="Internal").ap()
        self.dram[name] = ap
        return ap

    def sbuf(self, name, shape, dtype=BF16):
        st = self.scopes[-1] if self.scopes else self.stack
        h = st.enter_context(self.nc.sbuf_tensor(name, list(shape), dtype))
        self.sb_free[name] = h
        return h

    def push_scope(self):
        self.scopes.append(contextlib.ExitStack())

    def pop_scope(self):
        self.scopes.pop().close()

    def psum(self, name, shape, dtype=F32):
        h = self.stack.enter_context(self.nc.psum_tensor(name, list(shape), dtype))
        return h

    def semaphore(self, name):
        s = self.stack.enter_context(self.nc.semaphore(name))
        self.sem[name] = s
        return s

    def te(self, f):
        self.block.tensor(f)

    def v(self, f):
        self.block.vector(f)

    def s(self, f):
        self.block.scalar(f)

    def g(self, f):
        self.block.gpsimd(f)

    def sy(self, f):
        self.block.sync(f)

    def barrier(self):
        n = dict(self.n)
        sems = self.sem

        def mk(eng_names_sem):
            ncc = self.n.get("cc", 0)
            def f(e):
                for key, scale in [("te", 1), ("v", 1), ("s", 1), ("g", 1), ("gd", 16), ("sd", 16)]:
                    if n[key] > 0:
                        e.wait_ge(sems[key], n[key] * scale)
                if ncc > 0:
                    e.wait_ge(sems["cc"], ncc)
            return f
        self.te(mk(None))
        self.v(mk(None))
        self.s(mk(None))
        self.g(mk(None))
        self.sy(mk(None))

    def inc(self, key, k=1):
        self.n[key] += k
        return self.n[key] * (16 if key in ("gd", "sd") else 1)





def mark(ctx, inst, key):
    inst.then_inc(ctx.sem[key], 16 if key in ("gd", "sd") else 1)
    return ctx.inc(key)


def gload(ctx, dst_ap, src_ap):
    out = {}
    def f(g):
        out["w"] = mark(ctx, g.dma_start(out=dst_ap, in_=src_ap), "gd")
    ctx.g(f)
    return out["w"]


def sload(ctx, dst_ap, src_ap):
    out = {}
    def f(sy):
        out["w"] = mark(ctx, sy.dma_start(out=dst_ap, in_=src_ap), "sd")
    ctx.sy(f)
    return out["w"]


def collective(ctx, kind, op, groups, in_ap, out_ap):
    sems = ctx.sem
    prev = ctx.n.get("cc", 0)
    def f(g):
        if prev:
            g.wait_ge(sems["cc"], prev)
        g.collective_compute(kind, op, replica_groups=groups, ins=[in_ap], outs=[out_ap]
                             ).then_inc(sems["cc"], 1)
    ctx.g(f)
    ctx.n["cc"] = prev + 1
    return ctx.n["cc"]


PAIRS = [[0, 1], [2, 3], [4, 5], [6, 7]]
ALL8 = [list(range(8))]


class LNParams:
    def __init__(self, ctx, tname):
        self.sq = ctx.sbuf(tname + "_sq", [128, 2 * CH], BF16)
        self.stat = ctx.sbuf(tname + "_st", [1, 4 * CH], F32)
        self.statb = ctx.sbuf(tname + "_stb", [1, 2 * CH], BF16)
        self.mu_b = ctx.sbuf(tname + "_mub", [128, CH], BF16)
        self.istd_b = ctx.sbuf(tname + "_isb", [128, CH], BF16)


def emit_norm_stats(ctx, lnp, src_tiles, t, ps_a, ps_b, consts, rms=False, vwait=0):
    sems = ctx.sem
    n = len(src_tiles)
    for i, ap in enumerate(src_tiles):
        tprev = ctx.n["te"]
        def s1(s, i=i, ap=ap, tprev=tprev):
            if i == 0 and vwait:
                s.wait_ge(sems["v"], vwait)
            if i >= 2:
                s.wait_ge(sems["te"], tprev)
            mark(ctx, s.activation(lnp.sq[:, (i % 2) * CH:(i % 2) * CH + t], ap, AF.Square), "s")
        ctx.s(s1)
        sw = ctx.n["s"]
        def t1(te, i=i, ap=ap, sw=sw):
            te.wait_ge(sems["s"], sw)
            if i == 0 and vwait:
                te.wait_ge(sems["v"], vwait)
            if not rms:
                te.matmul(ps_a[0:1, 0:t], consts["ones_col"][:, :], ap,
                          start=(i == 0), stop=(i == n - 1))
            mark(ctx, te.matmul(ps_b[0:1, 0:t], consts["ones_col"][:, :],
                                lnp.sq[:, (i % 2) * CH:(i % 2) * CH + t],
                                start=(i == 0), stop=(i == n - 1)), "te")
        ctx.te(t1)
    tw = ctx.n["te"]
    misc = consts["misc"]
    def v1(v):
        v.wait_ge(sems["te"], tw)
        v.tensor_scalar(lnp.stat[0:1, 0:t], ps_b[0:1, 0:t], misc[0:1, 2:3], None, OP.mult)
        if not rms:
            v.tensor_scalar(lnp.stat[0:1, CH:CH + t], ps_a[0:1, 0:t], misc[0:1, 2:3], None, OP.mult)
            v.tensor_tensor(lnp.stat[0:1, 2 * CH:2 * CH + t], lnp.stat[0:1, CH:CH + t],
                            lnp.stat[0:1, CH:CH + t], OP.mult)
            v.tensor_tensor(lnp.stat[0:1, 0:t], lnp.stat[0:1, 0:t],
                            lnp.stat[0:1, 2 * CH:2 * CH + t], OP.subtract)
            inst = v.tensor_copy(lnp.statb[0:1, 0:t], lnp.stat[0:1, CH:CH + t])
        else:
            inst = v.engine_nop()
        mark(ctx, inst, "v")
    ctx.v(v1)
    vw = ctx.n["v"]
    def s2(s):
        s.wait_ge(sems["v"], vw)
        s.activation(lnp.stat[0:1, 2 * CH:2 * CH + t], lnp.stat[0:1, 0:t], AF.Ln,
                     bias=misc[0:1, 1:2])
        inst = s.activation(lnp.stat[0:1, 3 * CH:3 * CH + t],
                            lnp.stat[0:1, 2 * CH:2 * CH + t], AF.Exp, scale=-0.5)
        mark(ctx, inst, "s")
    ctx.s(s2)
    sw2 = ctx.n["s"]
    def v2(v):
        v.wait_ge(sems["s"], sw2)
        inst = v.tensor_copy(lnp.statb[0:1, CH:CH + t], lnp.stat[0:1, 3 * CH:3 * CH + t])
        mark(ctx, inst, "v")
    ctx.v(v2)
    vw2 = ctx.n["v"]
    def t2(te):
        te.wait_ge(sems["v"], vw2)
        if not rms:
            mark(ctx, te.matmul(ps_a[:, 0:t], consts["ones_bf"][:, :], lnp.statb[0:1, 0:t],
                                start=True, stop=True), "te")
        mark(ctx, te.matmul(ps_b[:, 0:t], consts["ones_bf"][:, :], lnp.statb[0:1, CH:CH + t],
                            start=True, stop=True), "te")
    ctx.te(t2)
    tw2 = ctx.n["te"]
    def s3(s):
        s.wait_ge(sems["te"], tw2)
        if not rms:
            s.copy(lnp.mu_b[:, 0:t], ps_a[:, 0:t])
        inst = s.copy(lnp.istd_b[:, 0:t], ps_b[:, 0:t])
        mark(ctx, inst, "s")
    ctx.s(s3)
    return ctx.n["s"]



def build_program(dbg_stage=None):
    ctx = Ctx()
    nc = ctx.nc
    sems = ctx.sem

    din = {}
    din["x"] = ctx.dram_in("x", [D, TL])
    din["flag"] = ctx.dram_in("flag", [128, 1])
    din["sel_lo"] = ctx.dram_in("sel_lo", [128, 128])
    din["sel_hi"] = ctx.dram_in("sel_hi", [128, 128])
    din["identity"] = ctx.dram_in("identity", [128, 128])
    din["ones1"] = ctx.dram_in("ones1", [1, 128])
    din["cmask"] = ctx.dram_in("cmask", [128, 128])
    din["selB"] = ctx.dram_in("selB", [32, 16 * 128])
    din["selC"] = ctx.dram_in("selC", [32, 16 * 128])
    din["selH"] = ctx.dram_in("selH", [8 * 128, 16])
    din["selHB"] = ctx.dram_in("selHB", [16, 8 * 128])
    din["misc"] = ctx.dram_in("misc", [128, 12])
    din["sel16"] = ctx.dram_in("sel16", [16, 16 * 128])
    for li in range(2):
        din[f"m{li}_inprojT"] = ctx.dram_in(f"m{li}_inprojT", [D, 2 * DI])
        din[f"m{li}_convw"] = ctx.dram_in(f"m{li}_convw", [128, 64])
        din[f"m{li}_convb"] = ctx.dram_in(f"m{li}_convb", [128, 16])
        din[f"m{li}_xpT"] = ctx.dram_in(f"m{li}_xpT", [DI, 96])
        din[f"m{li}_dtT"] = ctx.dram_in(f"m{li}_dtT", [DTR, DI])
        din[f"m{li}_dtb"] = ctx.dram_in(f"m{li}_dtb", [128, 16])
        din[f"m{li}_outT"] = ctx.dram_in(f"m{li}_outT", [DI, D])
        din[f"m{li}_D"] = ctx.dram_in(f"m{li}_D", [128, 16])
        din[f"m{li}_rmsw"] = ctx.dram_in(f"m{li}_rmsw", [128, 8])
        din[f"m{li}_lnw"] = ctx.dram_in(f"m{li}_lnw", [128, 8])
        din[f"m{li}_lnb"] = ctx.dram_in(f"m{li}_lnb", [128, 8])
        din[f"m{li}_Alog"] = ctx.dram_in(f"m{li}_Alog", [128, 128])
    din["ff1T"] = ctx.dram_in("ff1T", [D, 2 * FF])
    din["ffb1"] = ctx.dram_in("ffb1", [128, 64])
    din["ff2T"] = ctx.dram_in("ff2T", [FF, D])
    din["ffb2"] = ctx.dram_in("ffb2", [128, 8])
    din["centT"] = ctx.dram_in("centT", [D, NE])
    din["eb_row"] = ctx.dram_in("eb_row", [128, NE])
    din["WrT"] = ctx.dram_in("WrT", [NE, D, D])
    din["br"] = ctx.dram_in("br", [NE, D])
    din["WsT"] = ctx.dram_in("WsT", [NSH, D, D])
    din["bs"] = ctx.dram_in("bs", [128, 16])
    din["qkvT"] = ctx.dram_in("qkvT", [D, 3 * D])
    din["woutT"] = ctx.dram_in("woutT", [D, D])
    din["fqT"] = ctx.dram_in("fqT", [D, D])
    din["fqb"] = ctx.dram_in("fqb", [128, 8])
    out_dram = ctx.dram_out("out", [D, TL], F32)

    bdl_in = [ctx.dram_tmp(f"bdl_in{i}", [2 * DI + 32, CH]) for i in range(2)]
    bdl_out = [ctx.dram_tmp(f"bdl_out{k}", [2 * (2 * DI + 32), CH]) for k in range(NCH)]
    yag_in = [ctx.dram_tmp(f"yag_in{i}", [DH, CH]) for i in range(2)]
    yag_out = [ctx.dram_tmp(f"yag_out{g}", [DI, CH]) for g in range(NG)]
    tail_in = ctx.dram_tmp("tail_in", [D, 4])
    tail_out = ctx.dram_tmp("tail_out", [2 * D, 4])
    z_dram = ctx.dram_tmp("z_dram", [DI, TL])
    res_dram = ctx.dram_tmp("res_dram", [D, TL])
    stream_dram = ctx.dram_tmp("stream_dram", [D, TL])
    routed_dram = ctx.dram_tmp("routed_dram", [D, TL], F32)
    xc_dram = ctx.dram_tmp("xc_dram", [DI, TL])
    v_dram = ctx.dram_tmp("v_dram", [FF, TL])
    kexp_dram = ctx.dram_tmp("kexp_dram", [D, TL])
    p_dram = ctx.dram_tmp("p_dram", [D, TL])
    vtm_dram = ctx.dram_tmp("vtm_dram", [TL, D])
    sag_in = ctx.dram_tmp("sag_in", [128, 520], F32)
    sag_out = ctx.dram_tmp("sag_out", [256, 520], F32)
    amax_in = ctx.dram_tmp("amax_in", [1, 8], F32)
    amax_out = ctx.dram_tmp("amax_out", [1, 8], F32)

    C = {}
    C["ident_bf"] = ctx.sbuf("ident_bf", [128, 128], BF16)
    C["ident_f32"] = ctx.sbuf("ident_f32", [128, 128], F32)
    C["ones_f32"] = ctx.sbuf("ones_f32", [1, 128], F32)
    C["ones_bf"] = ctx.sbuf("ones_bf", [1, 128], BF16)
    C["ones_col"] = ctx.sbuf("ones_col", [128, 1], BF16)
    C["cmask"] = ctx.sbuf("cmask_sb", [128, 128], BF16)
    C["selB"] = ctx.sbuf("selB_sb", [32, 16 * 128], BF16)
    C["selC"] = ctx.sbuf("selC_sb", [32, 16 * 128], BF16)
    C["selH"] = ctx.sbuf("selH_sb", [128, 8 * 16], BF16)
    C["selHB"] = ctx.sbuf("selHB_sb", [16, 8 * 128], BF16)
    C["sel_lo"] = ctx.sbuf("sel_lo_sb", [128, 128], BF16)
    C["sel_hi"] = ctx.sbuf("sel_hi_sb", [128, 128], BF16)
    C["sel16"] = ctx.sbuf("sel16_sb", [16, 16 * 128], BF16)
    C["misc"] = ctx.sbuf("misc_sb", [128, 12], F32)
    C["flag"] = ctx.sbuf("flag_sb", [128, 1], F32)
    C["flag_bf"] = ctx.sbuf("flag_bf", [128, 128], BF16)
    vecs = {}
    for li in range(2):
        for nm, w in [("convw", 64), ("convb", 16), ("dtb", 16), ("D", 16),
                      ("rmsw", 8), ("lnw", 8), ("lnb", 8), ("Alog", 128)]:
            vecs[f"m{li}_{nm}"] = ctx.sbuf(f"m{li}_{nm}_sb", [128, w], F32)
        vecs[f"m{li}_A"] = ctx.sbuf(f"m{li}_A_sb", [128, 128], F32)
    for nm, w in [("ffb1", 64), ("ffb2", 8), ("bs", 16), ("fqb", 8)]:
        vecs[nm] = ctx.sbuf(nm + "_sb", [128, w], F32)
    vecs["eb_row"] = ctx.sbuf("eb_row_sb", [128, NE], BF16)
    vecs["br"] = ctx.sbuf("br_sb", [NE, D], BF16)

    wbig = ctx.sbuf("wbig", [128, 16384], BF16)
    state = ctx.sbuf("state", [128, 128], F32)
    wk = {}
    wk["normed"] = ctx.sbuf("normed", [128, 8 * CH], BF16)
    wk["sp1"] = ctx.sbuf("sp1", [128, 3 * CH], BF16)
    wk["gat"] = ctx.sbuf("gat", [128, 16 * CH], BF16)
    wk["mix"] = ctx.sbuf("mix", [128, 8 * CH], BF16)
    wk["srcx"] = ctx.sbuf("srcx", [128, 8 * CH], BF16)
    wk["sz"] = ctx.sbuf("szt", [128, 2 * CH], BF16)
    lnp = LNParams(ctx, "ln")

    def alloc_mamba():
        wk["x_in"] = ctx.sbuf("x_in", [128, 3 * (CH + 4)], BF16)
        wk["halo"] = ctx.sbuf("halo", [128, 16 * 4], BF16)
        wk["x_c"] = ctx.sbuf("x_c", [128, 2 * CH], BF16)
        wk["delta"] = ctx.sbuf("delta", [128, 2 * CH], BF16)
        wk["xdbl"] = ctx.sbuf("xdbl", [96, CH], BF16)
        wk["bc"] = ctx.sbuf("bc", [32, CH], BF16)
        wk["Bb"] = ctx.sbuf("Bb", [128, 16 * CH], BF16)
        wk["Cb"] = ctx.sbuf("Cb", [128, 16 * CH], BF16)
        wk["dfeed"] = ctx.sbuf("dfeed", [128, 2 * CH], BF16)
        wk["xfeed"] = ctx.sbuf("xfeed", [128, 2 * CH], BF16)
        wk["pair"] = ctx.sbuf("pair", [128, 8 * CH], BF16)
        wk["dA"] = ctx.sbuf("dA", [128, 4 * CH], BF16)
        wk["dAm"] = ctx.sbuf("dAm", [128, 2 * CH], BF16)
        wk["in1"] = ctx.sbuf("in1", [128, 2 * CH], BF16)
        wk["hbf"] = ctx.sbuf("hbf", [128, 3 * CH], BF16)
        wk["wv"] = ctx.sbuf("wv", [128, 2 * CH], BF16)
        wk["du"] = ctx.sbuf("du", [128, CH], BF16)
        wk["ymine"] = ctx.sbuf("ymine", [128, 2 * CH], BF16)
        wk["diag"] = ctx.sbuf("diag", [128, 4 * 128], BF16)
        wk["tailh"] = ctx.sbuf("tailh", [128, 8 * 4 * 4], BF16)
        wk["yfa"] = ctx.sbuf("yfa", [128, 4 * CH], BF16)
        wk["yfb"] = ctx.sbuf("yfb", [128, 4 * CH], BF16)
        wk["wxp"] = ctx.sbuf("wxp", [128, 16 * 96], BF16)
        wk["wdt"] = ctx.sbuf("wdt", [64, DI], BF16)
        wk["hbig"] = ctx.sbuf("hbig", [128, 4 * CH], F32)

    ps_mm = [ctx.psum(f"ps_mm{i}", [128, CH], F32) for i in range(2)]
    ps_y = [ctx.psum(f"ps_y{i}", [128, CH], F32) for i in range(2)]
    ps_bl = [ctx.psum(f"ps_bl{i}", [128, CH], F32) for i in range(2)]
    ps_x = ctx.psum("ps_x", [128, CH], F32)
    ps_tp = ctx.psum("ps_tp", [128, 128], BF16)

    for s in ["te", "v", "s", "g", "gd", "sd", "cc"]:
        ctx.semaphore(s)

    blk = ctx.stack.enter_context(nc.Block())
    ctx.block = blk

    gload(ctx, C["ident_bf"][:, :], din["identity"])
    sload(ctx, C["ident_f32"][:, :], din["identity"])
    sload(ctx, C["ones_f32"][:, :], din["ones1"])
    gload(ctx, C["ones_bf"][:, :], din["ones1"])
    gload(ctx, C["cmask"][:, :], din["cmask"])
    gload(ctx, C["selB"][:, :], din["selB"])
    gload(ctx, C["selC"][:, :], din["selC"])
    gload(ctx, C["selHB"][:, :], din["selHB"])
    gload(ctx, C["sel_lo"][:, :], din["sel_lo"])
    gload(ctx, C["sel_hi"][:, :], din["sel_hi"])
    gload(ctx, vecs["eb_row"][:, :], din["eb_row"])
    gload(ctx, C["misc"][:, :], din["misc"])
    gload(ctx, C["sel16"][:, :], din["sel16"])
    gload(ctx, C["flag"][:, :], din["flag"])
    gload(ctx, C["selH"][:, :].rearrange("p (c h) -> p c h", c=8), din["selH"].rearrange("(c p) h -> p c h", p=128))
    gload(ctx, vecs["br"][:, :], din["br"])
    for li in range(2):
        for nm in ["convw", "convb", "dtb", "D", "rmsw", "lnw", "lnb", "Alog"]:
            sload(ctx, vecs[f"m{li}_{nm}"][:, :], din[f"m{li}_{nm}"])
    for nm in ["ffb1", "ffb2", "bs", "fqb"]:
        sload(ctx, vecs[nm][:, :], din[nm])
    gw0, sw0 = ctx.n["gd"] * 16, ctx.n["sd"] * 16
    def v0(v):
        v.wait_ge(sems["gd"], gw0)
        v.wait_ge(sems["sd"], sw0)
        inst = v.tensor_copy(C["ones_col"][:, :], C["misc"][:, 0:1])
        inst = v.tensor_scalar(C["flag_bf"][:, :], C["ident_bf"][:, :], C["flag"][:, 0:1], None, OP.mult)
        mark(ctx, inst, "v")
    ctx.v(v0)
    def s0(s):
        s.wait_ge(sems["sd"], sw0)
        for li in range(2):
            inst = s.activation(vecs[f"m{li}_A"][:, :], vecs[f"m{li}_Alog"][:, :], AF.Exp)
        mark(ctx, inst, "s")
    ctx.s(s0)
    def v0b(v):
        v.wait_ge(sems["s"], ctx.n["s"])
        for li in range(2):
            inst = v.tensor_scalar(vecs[f"m{li}_A"][:, :], vecs[f"m{li}_A"][:, :], -1.0, None, OP.mult)
        mark(ctx, inst, "v")
    ctx.v(v0b)
    ctx.barrier()

    def wait_all(e):
        for key in ("te", "v", "s", "g"):
            if ctx.n[key] > 0:
                e.wait_ge(sems[key], ctx.n[key])
        for key in ("gd", "sd"):
            if ctx.n[key] > 0:
                e.wait_ge(sems[key], ctx.n[key] * 16)



    def colmax_bcast(src_col, dst_col):
        vprev = ctx.n["v"]
        def t1(te, vprev=vprev):
            if vprev:
                te.wait_ge(sems["v"], vprev)
            mark(ctx, te.transpose(ps_x[0:1, 0:128], src_col, C["ident_f32"][:, :]), "te")
        ctx.te(t1)
        tw = ctx.n["te"]
        def v1(v, tw=tw):
            v.wait_ge(sems["te"], tw)
            mark(ctx, v.reduce_max(lnp.stat[0:1, 0:1], ps_x[0:1, 0:128], axis=AX.X), "v")
        ctx.v(v1)
        vw = ctx.n["v"]
        def t2(te, vw=vw):
            te.wait_ge(sems["v"], vw)
            mark(ctx, te.matmul(ps_x[:, 0:1], C["ones_f32"][:, :], lnp.stat[0:1, 0:1],
                                start=True, stop=True), "te")
        ctx.te(t2)
        tw2 = ctx.n["te"]
        def v2(v, tw2=tw2):
            v.wait_ge(sems["te"], tw2)
            mark(ctx, v.tensor_copy(dst_col, ps_x[:, 0:1]), "v")
        ctx.v(v2)

    def gw_consumers(te=True, v=False, s=False):
        nte, nv, ns = ctx.n["te"], ctx.n["v"], ctx.n["s"]
        def f(g):
            if te and nte:
                g.wait_ge(sems["te"], nte)
            if v and nv:
                g.wait_ge(sems["v"], nv)
            if s and ns:
                g.wait_ge(sems["s"], ns)
        ctx.g(f)

    XW = CH + 4

    agof = {}
    yagof = {}
    trk = {"vrms": 0, "tz": 0, "tbc": 0, "vfin": 0}

    def emit_mamba_layer(li):
        Acol = vecs[f"m{li}_A"]
        convw = vecs[f"m{li}_convw"]
        convb = vecs[f"m{li}_convb"]
        dtb = vecs[f"m{li}_dtb"]
        Dcol = vecs[f"m{li}_D"]
        rmsw = vecs[f"m{li}_rmsw"]
        lnw = vecs[f"m{li}_lnw"]
        lnb = vecs[f"m{li}_lnb"]
        misc = C["misc"]
        src_dram = din["x"] if li == 0 else res_dram

        gload(ctx, wk["wxp"][:, :].rearrange("p (c r) -> p c r", c=16), din[f"m{li}_xpT"].rearrange("(c p) r -> p c r", p=128))
        gload(ctx, wk["wdt"][:, :], din[f"m{li}_dtT"])
        def gz(g):
            mark(ctx, g.memset(state[:, :], 0.0), "g")
        ctx.g(gz)
        ctx.barrier()

        gload(ctx, wk["tailh"][:, 0:32].rearrange("p (c t) -> p c t", c=8), src_dram[:, TL - 4:TL].rearrange("(c p) t -> p c t", p=128))
        ctx.barrier()
        sload(ctx, tail_in.rearrange("(c p) t -> p c t", p=128), wk["tailh"][:, 0:32].rearrange("p (c t) -> p c t", c=8))
        def gw1(g):
            g.wait_ge(sems["sd"], ctx.n["sd"] * 16)
        ctx.g(gw1)
        collective(ctx, "AllGather", OP.bypass, PAIRS, tail_in, tail_out)
        ccw = ctx.n["cc"]
        def gt2(g, ccw=ccw):
            g.wait_ge(sems["cc"], ccw)
            mark(ctx, g.dma_start(out=wk["tailh"][:, 32:64].rearrange("p (c t) -> p c t", c=8),
                                  in_=tail_out[0:D, :].rearrange("(c p) t -> p c t", p=128)), "gd")
        ctx.g(gt2)
        ctx.barrier()
        tsl = [wk["tailh"][:, 32 + fc * 4:32 + (fc + 1) * 4] for fc in range(8)]
        emit_norm_stats(ctx, lnp, tsl, 4, ps_mm[0], ps_mm[1], C, rms=True)
        swt = ctx.n["s"]
        def vt2(v, swt=swt):
            v.wait_ge(sems["s"], swt)
            for fc in range(8):
                inst = v.scalar_tensor_tensor(wk["tailh"][:, 64 + fc * 4:64 + (fc + 1) * 4],
                                              tsl[fc], rmsw[:, fc:fc + 1],
                                              lnp.istd_b[:, 0:4], OP.mult, OP.mult)
            mark(ctx, inst, "v")
        ctx.v(vt2)
        vwt = ctx.n["v"]
        ctx.barrier()
        for kc in range(8):
            gload(ctx, wbig[:, kc * 2048:(kc + 1) * 2048],
                  din[f"m{li}_inprojT"][kc * 128:(kc + 1) * 128, 0:2048])
        ctx.barrier()
        for oc in range(16):
            sprev = ctx.n["s"]
            def th(te, oc=oc, sprev=sprev):
                if sprev:
                    te.wait_ge(sems["s"], sprev)
                for kc in range(8):
                    inst = te.matmul(ps_mm[oc % 2][:, 0:4],
                                     wbig[:, kc * 2048 + oc * 128:kc * 2048 + (oc + 1) * 128],
                                     wk["tailh"][:, 64 + kc * 4:64 + (kc + 1) * 4],
                                     start=(kc == 0), stop=(kc == 7))
                mark(ctx, inst, "te")
            ctx.te(th)
            tw = ctx.n["te"]
            def sh(s, oc=oc, tw=tw):
                s.wait_ge(sems["te"], tw)
                mark(ctx, s.copy(wk["halo"][:, oc * 4:(oc + 1) * 4], ps_mm[oc % 2][:, 0:4]), "s")
            ctx.s(sh)
        swh = ctx.n["s"]
        def vfl(v, swh=swh):
            v.wait_ge(sems["s"], swh)
            inst = v.tensor_scalar(wk["halo"][:, :], wk["halo"][:, :], C["flag"][:, 0:1], None, OP.mult)
            mark(ctx, inst, "v")
        ctx.v(vfl)
        ctx.barrier()

        def passA(k):
            t0 = k * CH
            sd_entry = ctx.n["sd"] * 16
            vg, tg = trk["vrms"], trk["tz"]
            def gpre(g, vg=vg, tg=tg):
                if vg:
                    g.wait_ge(sems["v"], vg)
                if tg:
                    g.wait_ge(sems["te"], tg)
            ctx.g(gpre)
            gload(ctx, wk["srcx"][:, :].rearrange("p (c t) -> p c t", c=8), src_dram[:, t0:t0 + CH].rearrange("(c p) t -> p c t", p=128))
            if k > 0:
                for kc in range(8):
                    gload(ctx, wbig[:, kc * 2048:(kc + 1) * 2048],
                          din[f"m{li}_inprojT"][kc * 128:(kc + 1) * 128, 0:2048])
            ccb = agof.get(k - 2, 0)
            def sypre(sy, ccb=ccb):
                if ccb:
                    sy.wait_ge(sems["cc"], ccb)
            ctx.sy(sypre)
            gdw_pa = ctx.n["gd"] * 16
            def wait_gd_pa(e, gdw_pa=gdw_pa):
                e.wait_ge(sems["gd"], gdw_pa)
            src_tiles = [wk["srcx"][:, fc * CH:(fc + 1) * CH] for fc in range(8)]
            ctx.s(wait_gd_pa)
            ctx.te(wait_gd_pa)
            emit_norm_stats(ctx, lnp, src_tiles, CH, ps_mm[0], ps_mm[1], C, rms=True)
            sw = ctx.n["s"]
            def vrms(v, sw=sw):
                v.wait_ge(sems["s"], sw)
                for fc in range(8):
                    inst = v.scalar_tensor_tensor(wk["normed"][:, fc * CH:(fc + 1) * CH],
                                                  src_tiles[fc], rmsw[:, fc:fc + 1],
                                                  lnp.istd_b[:, 0:CH], OP.mult, OP.mult)
                mark(ctx, inst, "v")
            ctx.v(vrms)
            vw = ctx.n["v"]
            trk["vrms"] = vw
            for dc in range(16):
                sprev = ctx.n["s"]
                def tip(te, dc=dc, vw=vw, sprev=sprev, gdw_pa=gdw_pa):
                    te.wait_ge(sems["v"], vw)
                    if dc == 0:
                        te.wait_ge(sems["gd"], gdw_pa)
                    if sprev:
                        te.wait_ge(sems["s"], sprev)
                    for kc in range(8):
                        inst = te.matmul(ps_mm[dc % 2][:, :],
                                         wbig[:, kc * 2048 + dc * 128:kc * 2048 + (dc + 1) * 128],
                                         wk["normed"][:, kc * CH:(kc + 1) * CH],
                                         start=(kc == 0), stop=(kc == 7))
                    mark(ctx, inst, "te")
                ctx.te(tip)
                twi = ctx.n["te"]
                slot = dc % 3
                def sxi(s, dc=dc, twi=twi, slot=slot):
                    s.wait_ge(sems["te"], twi)
                    mark(ctx, s.copy(wk["x_in"][:, slot * XW + 4:(slot + 1) * XW],
                                     ps_mm[dc % 2][:, :]), "s")
                ctx.s(sxi)
                swx = ctx.n["s"]
                def vcv(v, dc=dc, slot=slot, swx=swx):
                    v.wait_ge(sems["s"], swx)
                    v.tensor_copy(wk["x_in"][:, slot * XW:slot * XW + 4], wk["halo"][:, dc * 4:(dc + 1) * 4])
                    for j in range(4):
                        inst = v.tensor_scalar(wk["diag"][:, j * 128:(j + 1) * 128], C["ident_bf"][:, :],
                                               convw[:, dc * 4 + j:dc * 4 + j + 1], None, OP.mult)
                    mark(ctx, inst, "v")
                ctx.v(vcv)
                vwc = ctx.n["v"]
                sprev2 = ctx.n["s"]
                sdprev = ctx.n["sd"] * 16
                def tcv(te, dc=dc, slot=slot, vwc=vwc, sprev2=sprev2):
                    te.wait_ge(sems["v"], vwc)
                    for j in range(4):
                        inst = te.matmul(ps_x[:, :], wk["diag"][:, j * 128:(j + 1) * 128],
                                         wk["x_in"][:, slot * XW + j + 1:slot * XW + j + 1 + CH],
                                         start=(j == 0), stop=(j == 3))
                    mark(ctx, inst, "te")
                ctx.te(tcv)
                twc = ctx.n["te"]
                def scv(s, dc=dc, twc=twc, sdprev=sdprev):
                    s.wait_ge(sems["te"], twc)
                    if dc >= 2:
                        s.wait_ge(sems["sd"], sdprev)
                    mark(ctx, s.activation(wk["x_c"][:, (dc % 2) * CH:(dc % 2 + 1) * CH],
                                           ps_x[:, :], AF.Silu, bias=convb[:, dc:dc + 1]), "s")
                ctx.s(scv)
                swc = ctx.n["s"]
                def vht(v, dc=dc, slot=slot, swc=swc):
                    v.wait_ge(sems["s"], swc)
                    mark(ctx, v.tensor_copy(wk["halo"][:, dc * 4:(dc + 1) * 4],
                                            wk["x_in"][:, slot * XW + CH:slot * XW + CH + 4]), "v")
                ctx.v(vht)
                def txp(te, dc=dc, swc=swc):
                    te.wait_ge(sems["s"], swc)
                    mark(ctx, te.matmul(ps_bl[0][0:96, :], wk["wxp"][:, dc * 96:(dc + 1) * 96],
                                        wk["x_c"][:, (dc % 2) * CH:(dc % 2 + 1) * CH],
                                        start=(dc == 0), stop=(dc == 15)), "te")
                ctx.te(txp)
                def syxc(sy, swc=swc):
                    sy.wait_ge(sems["s"], swc)
                ctx.sy(syxc)
                sload(ctx, xc_dram[dc * 128:(dc + 1) * 128, t0:t0 + CH],
                      wk["x_c"][:, (dc % 2) * CH:(dc % 2 + 1) * CH])
                sload(ctx, bdl_in[k % 2][DI + dc * 128:DI + (dc + 1) * 128, :],
                      wk["x_c"][:, (dc % 2) * CH:(dc % 2 + 1) * CH])
            twx = ctx.n["te"]
            def sxd(s, twx=twx, sd_entry=sd_entry):
                s.wait_ge(sems["te"], twx)
                s.wait_ge(sems["sd"], sd_entry)
                mark(ctx, s.copy(wk["xdbl"][:, :], ps_bl[0][0:96, :]), "s")
            ctx.s(sxd)
            swz0 = ctx.n["s"]
            twz0 = ctx.n["te"]
            def gwz0(g, twz0=twz0):
                g.wait_ge(sems["te"], twz0)
            ctx.g(gwz0)
            for kc in range(8):
                gload(ctx, wbig[:, kc * 2048:(kc + 1) * 2048],
                      din[f"m{li}_inprojT"][kc * 128:(kc + 1) * 128, 2048:4096])
            gwz = ctx.n["gd"] * 16
            for oc in range(16):
                sprev = ctx.n["s"]
                sdprev = ctx.n["sd"] * 16
                def tz(te, oc=oc, gwz=gwz, sprev=sprev):
                    te.wait_ge(sems["gd"], gwz)
                    if sprev:
                        te.wait_ge(sems["s"], sprev)
                    for kc in range(8):
                        inst = te.matmul(ps_mm[oc % 2][:, :],
                                         wbig[:, kc * 2048 + oc * 128:kc * 2048 + (oc + 1) * 128],
                                         wk["normed"][:, kc * CH:(kc + 1) * CH],
                                         start=(kc == 0), stop=(kc == 7))
                    mark(ctx, inst, "te")
                ctx.te(tz)
                twz = ctx.n["te"]
                def sz_(s, oc=oc, twz=twz, sdprev=sdprev):
                    s.wait_ge(sems["te"], twz)
                    if oc >= 2:
                        s.wait_ge(sems["sd"], sdprev)
                    mark(ctx, s.activation(wk["sz"][:, (oc % 2) * CH:(oc % 2 + 1) * CH],
                                           ps_mm[oc % 2][:, :], AF.Silu), "s")
                ctx.s(sz_)
                swz = ctx.n["s"]
                def syz(sy, swz=swz):
                    sy.wait_ge(sems["s"], swz)
                ctx.sy(syz)
                sload(ctx, z_dram[oc * 128:(oc + 1) * 128, t0:t0 + CH],
                      wk["sz"][:, (oc % 2) * CH:(oc % 2 + 1) * CH])
            trk["tz"] = ctx.n["te"]
            swxd = ctx.n["s"]
            for oc in range(16):
                sprev = ctx.n["s"]
                def tdt(te, oc=oc, swxd=swxd, sprev=sprev):
                    te.wait_ge(sems["s"], max(swxd, sprev))
                    mark(ctx, te.matmul(ps_bl[1][:, :], wk["wdt"][:, oc * 128:(oc + 1) * 128],
                                        wk["xdbl"][0:64, :], start=True, stop=True), "te")
                ctx.te(tdt)
                twd = ctx.n["te"]
                sdprev = ctx.n["sd"] * 16
                def sdt(s, oc=oc, twd=twd, sdprev=sdprev):
                    s.wait_ge(sems["te"], twd)
                    if oc >= 2:
                        s.wait_ge(sems["sd"], sdprev)
                    s.activation(wk["sp1"][:, 0:CH], ps_bl[1][:, :], AF.Abs, bias=dtb[:, oc:oc + 1])
                    s.activation(wk["sp1"][:, CH:2 * CH], ps_bl[1][:, :], AF.Relu, bias=dtb[:, oc:oc + 1])
                    s.activation(wk["sp1"][:, 2 * CH:3 * CH], wk["sp1"][:, 0:CH], AF.Exp, scale=-1.0)
                    inst = s.activation(wk["sp1"][:, 0:CH], wk["sp1"][:, 2 * CH:3 * CH],
                                        AF.Ln, bias=misc[:, 0:1])
                    mark(ctx, inst, "s")
                ctx.s(sdt)
                sw2 = ctx.n["s"]
                sdp2 = ctx.n["sd"] * 16
                def vdt(v, oc=oc, sw2=sw2, sdp2=sdp2):
                    v.wait_ge(sems["s"], sw2)
                    if oc >= 2:
                        v.wait_ge(sems["sd"], sdp2)
                    mark(ctx, v.tensor_tensor(wk["delta"][:, (oc % 2) * CH:(oc % 2 + 1) * CH],
                                              wk["sp1"][:, 0:CH], wk["sp1"][:, CH:2 * CH],
                                              OP.add), "v")
                ctx.v(vdt)
                vwd = ctx.n["v"]
                def syd(sy, vwd=vwd):
                    sy.wait_ge(sems["v"], vwd)
                ctx.sy(syd)
                sload(ctx, bdl_in[k % 2][oc * 128:(oc + 1) * 128, :],
                      wk["delta"][:, (oc % 2) * CH:(oc % 2 + 1) * CH])
            swbc = ctx.n["s"]
            def sybc(sy, swbc=swbc):
                sy.wait_ge(sems["s"], swbc)
            ctx.sy(sybc)
            sload(ctx, bdl_in[k % 2][2 * DI:2 * DI + 32, :], wk["xdbl"][64:96, :])
            sdw = ctx.n["sd"] * 16
            def gag(g, sdw=sdw):
                g.wait_ge(sems["sd"], sdw)
            ctx.g(gag)
            agof[k] = collective(ctx, "AllGather", OP.bypass, PAIRS, bdl_in[k % 2], bdl_out[k])

        def scan_chunk(g):
            slot = g // 8
            kk = g % 8
            base = slot * (2 * DI + 32)
            ccw = agof[kk]
            ccy = yagof.get(g - 2, 0)
            teb = trk["tbc"]
            def syw(sy, ccw=ccw, ccy=ccy, teb=teb):
                sy.wait_ge(sems["cc"], ccw)
                if ccy:
                    sy.wait_ge(sems["cc"], ccy)
                if teb:
                    sy.wait_ge(sems["te"], teb)
            ctx.sy(syw)
            sload(ctx, wk["bc"][:, :], bdl_out[kk][base + 2 * DI:base + 2 * DI + 32, :])
            sdw0 = ctx.n["sd"] * 16
            for n in range(16):
                sprev = ctx.n["s"]
                def tbc(te, n=n, sprev=sprev, sdw0=sdw0):
                    te.wait_ge(sems["sd"], sdw0)
                    if sprev:
                        te.wait_ge(sems["s"], sprev)
                    te.matmul(ps_bl[0][:, :], C["selB"][:, n * 128:(n + 1) * 128],
                              wk["bc"][:, :], start=True, stop=True)
                    mark(ctx, te.matmul(ps_bl[1][:, :], C["selC"][:, n * 128:(n + 1) * 128],
                                        wk["bc"][:, :], start=True, stop=True), "te")
                ctx.te(tbc)
                if n == 15:
                    trk["tbc"] = ctx.n["te"]
                tw = ctx.n["te"]
                def sbc(s, n=n, tw=tw):
                    s.wait_ge(sems["te"], tw)
                    s.copy(wk["Bb"][:, n * CH:(n + 1) * CH], ps_bl[0][:, :])
                    mark(ctx, s.copy(wk["Cb"][:, n * CH:(n + 1) * CH], ps_bl[1][:, :]), "s")
                ctx.s(sbc)
            tfbm_prev = trk.get("tfbm", [0] * 8)
            tfbm = [0] * 8
            sdwm = {}

            def issue_pair(dcx):
                sb = (dcx % 2) * 4 * CH
                guard = tfbm[dcx - 2] if dcx >= 2 else tfbm_prev[6 + dcx]
                def syp(sy, guard=guard):
                    if guard:
                        sy.wait_ge(sems["te"], guard)
                ctx.sy(syp)
                sload(ctx, wk["pair"][:, sb:sb + CH],
                      bdl_out[kk][base + dcx * 128:base + (dcx + 1) * 128, :])
                sload(ctx, wk["pair"][:, sb + CH:sb + 2 * CH],
                      bdl_out[kk][base + (dcx + 8) * 128:base + (dcx + 9) * 128, :])
                sload(ctx, wk["pair"][:, sb + 2 * CH:sb + 3 * CH],
                      bdl_out[kk][base + DI + dcx * 128:base + DI + (dcx + 1) * 128, :])
                sload(ctx, wk["pair"][:, sb + 3 * CH:sb + 4 * CH],
                      bdl_out[kk][base + DI + (dcx + 8) * 128:base + DI + (dcx + 9) * 128, :])
                sdwm[dcx] = ctx.n["sd"] * 16

            issue_pair(0)
            for dc in range(8):
                if dc + 1 < 8:
                    issue_pair(dc + 1)
                sb = (dc % 2) * 4 * CH
                sdw = sdwm[dc]
                sprev = ctx.n["s"]
                def tfb(te, sb=sb, sdw=sdw, sprev=sprev):
                    te.wait_ge(sems["sd"], sdw)
                    if sprev:
                        te.wait_ge(sems["s"], sprev)
                    te.matmul(ps_mm[0][:, :], C["sel_lo"][:, :], wk["pair"][:, sb:sb + CH],
                              start=True, stop=False)
                    te.matmul(ps_mm[0][:, :], C["sel_hi"][:, :], wk["pair"][:, sb + CH:sb + 2 * CH],
                              start=False, stop=True)
                    te.matmul(ps_mm[1][:, :], C["sel_lo"][:, :], wk["pair"][:, sb + 2 * CH:sb + 3 * CH],
                              start=True, stop=False)
                    mark(ctx, te.matmul(ps_mm[1][:, :], C["sel_hi"][:, :], wk["pair"][:, sb + 3 * CH:sb + 4 * CH],
                                        start=False, stop=True), "te")
                ctx.te(tfb)
                tfbm[dc] = ctx.n["te"]
                tw = ctx.n["te"]
                dslot = dc % 2
                def sfb(s, tw=tw, dslot=dslot):
                    s.wait_ge(sems["te"], tw)
                    s.copy(wk["dfeed"][:, dslot * CH:(dslot + 1) * CH], ps_mm[0][:, :])
                    mark(ctx, s.copy(wk["xfeed"][:, dslot * CH:(dslot + 1) * CH], ps_mm[1][:, :]), "s")
                ctx.s(sfb)
                swf = ctx.n["s"]
                def vdu(v, dslot=dslot, swf=swf):
                    v.wait_ge(sems["s"], swf)
                    mark(ctx, v.tensor_tensor(wk["du"][:, :], wk["dfeed"][:, dslot * CH:(dslot + 1) * CH],
                                              wk["xfeed"][:, dslot * CH:(dslot + 1) * CH], OP.mult), "v")
                ctx.v(vdu)
                cast_marks = {}

                def emit_w(nn):
                    s3 = nn % 3
                    ceng, cval = cast_marks[nn]
                    tprev2 = ctx.n["te"]
                    def vw_(v, nn=nn, s3=s3, ceng=ceng, cval=cval, tprev2=tprev2):
                        v.wait_ge(sems[ceng], cval)
                        if nn >= 2 and tprev2 > 1:
                            v.wait_ge(sems["te"], tprev2 - 1)
                        mark(ctx, v.tensor_tensor(wk["wv"][:, (nn % 2) * CH:(nn % 2 + 1) * CH],
                                                  wk["hbf"][:, s3 * CH:(s3 + 1) * CH],
                                                  wk["Cb"][:, nn * CH:(nn + 1) * CH], OP.mult), "v")
                    ctx.v(vw_)
                    vw_w = ctx.n["v"]
                    sprev3 = ctx.n["s"]
                    def tym(te, dc=dc, nn=nn, vw_w=vw_w, sprev3=sprev3):
                        te.wait_ge(sems["v"], vw_w)
                        if nn == 0 and sprev3:
                            te.wait_ge(sems["s"], sprev3)
                        mark(ctx, te.matmul(ps_y[dc % 2][:, :], C["ident_bf"][:, :],
                                            wk["wv"][:, (nn % 2) * CH:(nn % 2 + 1) * CH],
                                            start=(nn == 0), stop=(nn == 15)), "te")
                    ctx.te(tym)

                exp_marks = {}
                for n in range(16):
                    s4 = n % 4
                    s2 = n % 2
                    em2 = exp_marks.get(n - 2, 0)
                    def vts(v, dc=dc, n=n, s2=s2, dslot=dslot, swf=swf, em2=em2):
                        if n == 0:
                            v.wait_ge(sems["s"], swf)
                        if em2:
                            v.wait_ge(sems["s"], em2)
                        mark(ctx, v.tensor_scalar(wk["dAm"][:, s2 * CH:(s2 + 1) * CH],
                                                  wk["dfeed"][:, dslot * CH:(dslot + 1) * CH],
                                                  Acol[:, dc * 16 + n:dc * 16 + n + 1], None,
                                                  OP.mult), "v")
                    ctx.v(vts)
                    vts_w = ctx.n["v"]
                    def sdA2(s, n=n, s4=s4, s2=s2, vts_w=vts_w):
                        s.wait_ge(sems["v"], vts_w)
                        mark(ctx, s.activation(wk["dA"][:, s4 * CH:(s4 + 1) * CH],
                                               wk["dAm"][:, s2 * CH:(s2 + 1) * CH], AF.Exp), "s")
                    ctx.s(sdA2)
                    exp_marks[n] = ctx.n["s"]
                    sw_dA = ctx.n["s"]
                    cm4 = cast_marks.get(n - 4)
                    def vscan(v, dc=dc, n=n, s4=s4, sw_dA=sw_dA, cm4=cm4):
                        v.wait_ge(sems["s"], sw_dA)
                        v.tensor_tensor(wk["in1"][:, (n % 2) * CH:(n % 2 + 1) * CH], wk["du"][:, :],
                                        wk["Bb"][:, n * CH:(n + 1) * CH], OP.mult)
                        if cm4 is not None:
                            v.wait_ge(sems[cm4[0]], cm4[1])
                        mark(ctx, v.tensor_tensor_scan(wk["hbig"][:, s4 * CH:(s4 + 1) * CH],
                                                       wk["dA"][:, s4 * CH:(s4 + 1) * CH],
                                                       wk["in1"][:, (n % 2) * CH:(n % 2 + 1) * CH],
                                                       state[:, dc * 16 + n:dc * 16 + n + 1],
                                                       OP.mult, OP.add), "v")
                    ctx.v(vscan)
                    vw_scan = ctx.n["v"]
                    s3n = n % 3
                    if True:
                        def gcast(g_, s3n=s3n, s4=s4, vw_scan=vw_scan):
                            g_.wait_ge(sems["v"], vw_scan)
                            mark(ctx, g_.tensor_copy(wk["hbf"][:, s3n * CH:(s3n + 1) * CH],
                                                     wk["hbig"][:, s4 * CH:(s4 + 1) * CH]), "g")
                        ctx.g(gcast)
                        cast_marks[n] = ("g", ctx.n["g"])
                    else:
                        def scast(s, s3n=s3n, s4=s4, vw_scan=vw_scan):
                            s.wait_ge(sems["v"], vw_scan)
                            mark(ctx, s.activation(wk["hbf"][:, s3n * CH:(s3n + 1) * CH],
                                                   wk["hbig"][:, s4 * CH:(s4 + 1) * CH],
                                                   AF.Identity), "s")
                        ctx.s(scast)
                        cast_marks[n] = ("s", ctx.n["s"])
                    if n >= 2:
                        emit_w(n - 2)
                    if n % 4 == 3:
                        ng0 = (n // 4) * 4
                        _vws = ctx.n["v"]
                        def sst_(s, dc=dc, ng0=ng0, _vws=_vws):
                            s.wait_ge(sems["v"], _vws)
                            mark(ctx, s.copy(state[:, dc * 16 + ng0:dc * 16 + ng0 + 4],
                                             wk["hbig"][:, CH - 1::CH]), "s")
                        ctx.s(sst_)
                emit_w(14)
                emit_w(15)
                tw = ctx.n["te"]
                sdpy = ctx.n["sd"] * 16
                def sy_(s, dc=dc, tw=tw, sdpy=sdpy):
                    s.wait_ge(sems["te"], tw)
                    if dc >= 2:
                        s.wait_ge(sems["sd"], sdpy)
                    mark(ctx, s.copy(wk["ymine"][:, (dc % 2) * CH:(dc % 2 + 1) * CH], ps_y[dc % 2][:, :]), "s")
                ctx.s(sy_)
                _sn = ctx.n["s"]
                def syy(sy, _sn=_sn):
                    sy.wait_ge(sems["s"], _sn)
                ctx.sy(syy)
                sload(ctx, yag_in[g % 2][dc * 128:(dc + 1) * 128, :],
                      wk["ymine"][:, (dc % 2) * CH:(dc % 2 + 1) * CH])
            trk["tfbm"] = tfbm
            sdw2 = ctx.n["sd"] * 16
            def gwy(g_, sdw2=sdw2):
                g_.wait_ge(sems["sd"], sdw2)
            ctx.g(gwy)
            yagof[g] = collective(ctx, "AllGather", OP.bypass, PAIRS, yag_in[g % 2], yag_out[g])

        def gating(k, dst_add_x):
            t0 = k * CH
            ccw = yagof[k + 8]
            def syw(sy, ccw=ccw):
                sy.wait_ge(sems["cc"], ccw)
            ctx.sy(syw)
            if dst_add_x:
                vg = trk["vfin"]
                def gpre2(g, vg=vg):
                    if vg:
                        g.wait_ge(sems["v"], vg)
                ctx.g(gpre2)
                gload(ctx, wk["srcx"][:, :].rearrange("p (c t) -> p c t", c=8), din["x"][:, t0:t0 + CH].rearrange("(c p) t -> p c t", p=128))
            gdw = ctx.n["gd"] * 16
            for grp in range(4):
                tprev = ctx.n["te"]
                vprev = ctx.n["v"]
                def syg(sy, grp=grp, tprev=tprev, vprev=vprev):
                    if tprev:
                        sy.wait_ge(sems["te"], tprev)
                    if vprev:
                        sy.wait_ge(sems["v"], vprev)
                ctx.sy(syg)
                for j in range(4):
                    c16 = grp * 4 + j
                    sload(ctx, wk["yfa"][:, j * CH:(j + 1) * CH],
                          yag_out[k][c16 * 128:(c16 + 1) * 128, :])
                    sload(ctx, wk["yfb"][:, j * CH:(j + 1) * CH],
                          yag_out[k + 8][c16 * 128:(c16 + 1) * 128, :])
                sdw = ctx.n["sd"] * 16
                for j in range(4):
                    c16 = grp * 4 + j
                    vprev2 = ctx.n["v"]
                    def syl(sy, c16=c16, j=j, vprev2=vprev2):
                        if vprev2:
                            sy.wait_ge(sems["v"], vprev2)
                    ctx.sy(syl)
                    sload(ctx, wk["x_c"][:, (j % 2) * CH:(j % 2 + 1) * CH],
                          xc_dram[c16 * 128:(c16 + 1) * 128, t0:t0 + CH])
                    sload(ctx, wk["sz"][:, (j % 2) * CH:(j % 2 + 1) * CH],
                          z_dram[c16 * 128:(c16 + 1) * 128, t0:t0 + CH])
                    sdw2 = ctx.n["sd"] * 16
                    vprev3 = ctx.n["v"]
                    def tys(te, c16=c16, j=j, sdw=sdw, vprev3=vprev3):
                        te.wait_ge(sems["sd"], sdw)
                        if vprev3:
                            te.wait_ge(sems["v"], vprev3)
                        te.matmul(ps_bl[c16 % 2][:, :], C["sel_lo"][:, :],
                                  wk["yfa"][:, j * CH:(j + 1) * CH], start=True, stop=False)
                        mark(ctx, te.matmul(ps_bl[c16 % 2][:, :], C["sel_hi"][:, :],
                                            wk["yfb"][:, j * CH:(j + 1) * CH], start=False, stop=True), "te")
                    ctx.te(tys)
                    tw = ctx.n["te"]
                    def vga(v, c16=c16, j=j, tw=tw, sdw2=sdw2):
                        v.wait_ge(sems["te"], tw)
                        v.wait_ge(sems["sd"], sdw2)
                        v.scalar_tensor_tensor(wk["gat"][:, c16 * CH:(c16 + 1) * CH],
                                               wk["x_c"][:, (j % 2) * CH:(j % 2 + 1) * CH],
                                               Dcol[:, c16:c16 + 1], ps_bl[c16 % 2][:, :],
                                               OP.mult, OP.add)
                        mark(ctx, v.tensor_tensor(wk["gat"][:, c16 * CH:(c16 + 1) * CH],
                                                  wk["gat"][:, c16 * CH:(c16 + 1) * CH],
                                                  wk["sz"][:, (j % 2) * CH:(j % 2 + 1) * CH], OP.mult), "v")
                    ctx.v(vga)
            vwg = ctx.n["v"]
            for oc in range(8):
                sprev = ctx.n["s"]
                gdw_o = trk.get("gdw_o", 0)
                def tmo(te, oc=oc, vwg=vwg, sprev=sprev, gdw_o=gdw_o):
                    te.wait_ge(sems["v"], vwg)
                    if oc == 0 and gdw_o:
                        te.wait_ge(sems["gd"], gdw_o)
                    if sprev:
                        te.wait_ge(sems["s"], sprev)
                    for kc in range(16):
                        inst = te.matmul(ps_mm[oc % 2][:, :],
                                         wbig[:, kc * 1024 + oc * 128:kc * 1024 + (oc + 1) * 128],
                                         wk["gat"][:, kc * CH:(kc + 1) * CH],
                                         start=(kc == 0), stop=(kc == 15))
                    mark(ctx, inst, "te")
                ctx.te(tmo)
                tw = ctx.n["te"]
                def smo(s, oc=oc, tw=tw):
                    s.wait_ge(sems["te"], tw)
                    mark(ctx, s.copy(wk["mix"][:, oc * CH:(oc + 1) * CH], ps_mm[oc % 2][:, :]), "s")
                ctx.s(smo)
            mix_tiles = [wk["mix"][:, oc * CH:(oc + 1) * CH] for oc in range(8)]
            emit_norm_stats(ctx, lnp, mix_tiles, CH, ps_mm[0], ps_mm[1], C, rms=False)
            swn = ctx.n["s"]
            for oc in range(8):
                def vap(v, oc=oc, swn=swn):
                    v.wait_ge(sems["s"], swn)
                    v.tensor_tensor(wk["sp1"][:, 0:CH], wk["mix"][:, oc * CH:(oc + 1) * CH],
                                    lnp.mu_b[:, 0:CH], OP.subtract)
                    mark(ctx, v.scalar_tensor_tensor(wk["sp1"][:, CH:2 * CH], wk["sp1"][:, 0:CH],
                                                     lnw[:, oc:oc + 1], lnp.istd_b[:, 0:CH],
                                                     OP.mult, OP.mult), "v")
                ctx.v(vap)
                vw2 = ctx.n["v"]
                def sap(s, oc=oc, vw2=vw2):
                    s.wait_ge(sems["v"], vw2)
                    mark(ctx, s.activation(wk["sp1"][:, 2 * CH:3 * CH], wk["sp1"][:, CH:2 * CH],
                                           AF.Identity, bias=lnb[:, oc:oc + 1]), "s")
                ctx.s(sap)
                sw3 = ctx.n["s"]
                def vfin(v, oc=oc, sw3=sw3, dst_add_x=dst_add_x, gdw=gdw):
                    v.wait_ge(sems["s"], sw3)
                    if dst_add_x:
                        v.wait_ge(sems["gd"], gdw)
                        inst = v.tensor_tensor(wk["mix"][:, oc * CH:(oc + 1) * CH],
                                               wk["sp1"][:, 2 * CH:3 * CH],
                                               wk["srcx"][:, oc * CH:(oc + 1) * CH], OP.add)
                    else:
                        inst = v.tensor_copy(wk["mix"][:, oc * CH:(oc + 1) * CH],
                                             wk["sp1"][:, 2 * CH:3 * CH])
                    mark(ctx, inst, "v")
                ctx.v(vfin)
            vwf = ctx.n["v"]
            trk["vfin"] = vwf
            def sys(sy, vwf=vwf):
                sy.wait_ge(sems["v"], vwf)
            ctx.sy(sys)
            sload(ctx, res_dram[:, t0:t0 + CH].rearrange("(c p) t -> p c t", p=128), wk["mix"][:, :].rearrange("p (c t) -> p c t", c=8))

        with nc.named_scope(f"m{li}_P1"):
            for k in range(NCH):
                with nc.named_scope(f"m{li}_pA{k}"):
                    passA(k)
                with nc.named_scope(f"m{li}_sc{k}"):
                    scan_chunk(k)
        tg = trk["tz"]
        def gwout(g, tg=tg):
            g.wait_ge(sems["te"], tg)
        ctx.g(gwout)
        for kc in range(16):
            gload(ctx, wbig[:, kc * 1024:(kc + 1) * 1024],
                  din[f"m{li}_outT"][kc * 128:(kc + 1) * 128, :])
        trk["gdw_o"] = ctx.n["gd"] * 16
        with nc.named_scope(f"m{li}_P2"):
            for k in range(NCH):
                with nc.named_scope(f"m{li}_sc{k + 8}"):
                    scan_chunk(k + 8)
                if k > 0:
                    with nc.named_scope(f"m{li}_gt{k - 1}"):
                        gating(k - 1, dst_add_x=(li == 0))
            with nc.named_scope(f"m{li}_gt7"):
                gating(NCH - 1, dst_add_x=(li == 0))

    ctx.push_scope()
    alloc_mamba()
    with nc.named_scope("mamba0"):
        emit_mamba_layer(0)
    if dbg_stage != "m0":
        with nc.named_scope("mamba1"):
            emit_mamba_layer(1)
    ctx.pop_scope()
    if dbg_stage == "m0":
        ctx.barrier()
        def gdump0(g):
            mark(ctx, g.dma_start(out=out_dram, in_=res_dram), "gd")
        ctx.g(gdump0)
        ctx.barrier()
        return ctx

    def emit_ffn():
        for blkj in range(8):
            tprev = ctx.n["te"]
            def gww(g, tprev=tprev):
                if tprev:
                    g.wait_ge(sems["te"], tprev)
            ctx.g(gww)
            for kc in range(8):
                gload(ctx, wbig[:, kc * 512:(kc + 1) * 512],
                      din["ff1T"][kc * 128:(kc + 1) * 128, blkj * 512:(blkj + 1) * 512])
                gload(ctx, wbig[:, 4096 + kc * 512:4096 + (kc + 1) * 512],
                      din["ff1T"][kc * 128:(kc + 1) * 128, FF + blkj * 512:FF + (blkj + 1) * 512])
            gw = ctx.n["gd"] * 16
            for tb in range(8):
                t0 = tb * CH
                gw_consumers(te=True)
                gload(ctx, wk["normed"][:, :].rearrange("p (c t) -> p c t", c=8), res_dram[:, t0:t0 + CH].rearrange("(c p) t -> p c t", p=128))
                gwh = ctx.n["gd"] * 16
                for j in range(4):
                    oc = blkj * 4 + j
                    sprev = ctx.n["s"]
                    def tff(te, j=j, gw=gw, gwh=gwh, sprev=sprev):
                        te.wait_ge(sems["gd"], max(gw, gwh))
                        if sprev:
                            te.wait_ge(sems["s"], sprev)
                        for kc in range(8):
                            te.matmul(ps_mm[0][:, :],
                                      wbig[:, kc * 512 + j * 128:kc * 512 + (j + 1) * 128],
                                      wk["normed"][:, kc * CH:(kc + 1) * CH],
                                      start=(kc == 0), stop=(kc == 7))
                        for kc in range(8):
                            inst = te.matmul(ps_mm[1][:, :],
                                             wbig[:, 4096 + kc * 512 + j * 128:4096 + kc * 512 + (j + 1) * 128],
                                             wk["normed"][:, kc * CH:(kc + 1) * CH],
                                             start=(kc == 0), stop=(kc == 7))
                        mark(ctx, inst, "te")
                    ctx.te(tff)
                    tw = ctx.n["te"]
                    sdp = ctx.n["sd"] * 16
                    def sff(s, oc=oc, tw=tw):
                        s.wait_ge(sems["te"], tw)
                        s.activation(wk["sp1"][:, 0:CH], ps_mm[0][:, :], AF.Identity,
                                     bias=vecs["ffb1"][:, oc:oc + 1])
                        mark(ctx, s.activation(wk["sp1"][:, CH:2 * CH], ps_mm[1][:, :], AF.Gelu,
                                               bias=vecs["ffb1"][:, 32 + oc:33 + oc]), "s")
                    ctx.s(sff)
                    sw = ctx.n["s"]
                    def vff(v, sw=sw, sdp=sdp, j=j):
                        v.wait_ge(sems["s"], sw)
                        if sdp:
                            v.wait_ge(sems["sd"], sdp)
                        mark(ctx, v.tensor_tensor(wk["sz"][:, (j % 2) * CH:(j % 2 + 1) * CH],
                                                  wk["sp1"][:, 0:CH], wk["sp1"][:, CH:2 * CH],
                                                  OP.mult), "v")
                    ctx.v(vff)
                    vw = ctx.n["v"]
                    def syv(sy, vw=vw):
                        sy.wait_ge(sems["v"], vw)
                    ctx.sy(syv)
                    sload(ctx, v_dram[oc * 128:(oc + 1) * 128, t0:t0 + CH],
                          wk["sz"][:, (j % 2) * CH:(j % 2 + 1) * CH])
        ctx.barrier()
        for ochalf in range(2):
            tprev = ctx.n["te"]
            def gww(g, tprev=tprev):
                if tprev:
                    g.wait_ge(sems["te"], tprev)
            ctx.g(gww)
            for kc in range(32):
                gload(ctx, wbig[:, kc * 512:(kc + 1) * 512],
                      din["ff2T"][kc * 128:(kc + 1) * 128, ochalf * 512:(ochalf + 1) * 512])
            gw = ctx.n["gd"] * 16
            for tb in range(8):
                t0 = tb * CH
                psl = [ps_mm[0], ps_mm[1], ps_bl[0], ps_bl[1]]
                for khalf in range(2):
                    gw_consumers(te=True)
                    gload(ctx, wk["gat"][:, :].rearrange("p (c t) -> p c t", c=16),
                          v_dram[khalf * 2048:(khalf + 1) * 2048, t0:t0 + CH].rearrange(
                              "(c p) t -> p c t", p=128))
                    gwv = ctx.n["gd"] * 16
                    sprev = ctx.n["s"]
                    def tf2(te, khalf=khalf, gw=gw, gwv=gwv, sprev=sprev):
                        te.wait_ge(sems["gd"], max(gw, gwv))
                        if khalf == 0 and sprev:
                            te.wait_ge(sems["s"], sprev)
                        for oc4 in range(4):
                            for kc in range(16):
                                kcg = khalf * 16 + kc
                                inst = te.matmul(psl[oc4][:, :],
                                                 wbig[:, kcg * 512 + oc4 * 128:kcg * 512 + (oc4 + 1) * 128],
                                                 wk["gat"][:, kc * CH:(kc + 1) * CH],
                                                 start=(kcg == 0), stop=(kcg == 31))
                        mark(ctx, inst, "te")
                    ctx.te(tf2)
                tw = ctx.n["te"]
                def sf2(s, ochalf=ochalf, tw=tw):
                    s.wait_ge(sems["te"], tw)
                    for oc4 in range(4):
                        oc = ochalf * 4 + oc4
                        inst = s.activation(wk["mix"][:, oc4 * CH:(oc4 + 1) * CH], psl[oc4][:, :],
                                            AF.Identity, bias=vecs["ffb2"][:, oc:oc + 1])
                    mark(ctx, inst, "s")
                ctx.s(sf2)
                sw = ctx.n["s"]
                def syf(sy, sw=sw):
                    sy.wait_ge(sems["s"], sw)
                ctx.sy(syf)
                for oc4 in range(4):
                    oc = ochalf * 4 + oc4
                    sload(ctx, stream_dram[oc * 128:(oc + 1) * 128, t0:t0 + CH],
                          wk["mix"][:, oc4 * CH:(oc4 + 1) * CH])
        ctx.barrier()
        if dbg_stage == "ff2":
            return
        for tb in range(8):
            t0 = tb * CH
            gw_consumers(te=True, v=True, s=True)
            _sdn = ctx.n["sd"] * 16
            def gsd(g, _sdn=_sdn):
                if _sdn:
                    g.wait_ge(sems["sd"], _sdn)
            ctx.g(gsd)
            gload(ctx, wk["mix"][:, :].rearrange("p (c t) -> p c t", c=8), stream_dram[:, t0:t0 + CH].rearrange("(c p) t -> p c t", p=128))
            gload(ctx, wk["srcx"][:, :].rearrange("p (c t) -> p c t", c=8), din["x"][:, t0:t0 + CH].rearrange("(c p) t -> p c t", p=128))
            gw = ctx.n["gd"] * 16
            def vadd(v, gw=gw):
                v.wait_ge(sems["gd"], gw)
                for oc in range(8):
                    inst = v.tensor_tensor(wk["mix"][:, oc * CH:(oc + 1) * CH],
                                           wk["mix"][:, oc * CH:(oc + 1) * CH],
                                           wk["srcx"][:, oc * CH:(oc + 1) * CH], OP.add)
                mark(ctx, inst, "v")
            ctx.v(vadd)
            mix_tiles = [wk["mix"][:, oc * CH:(oc + 1) * CH] for oc in range(8)]
            emit_norm_stats(ctx, lnp, mix_tiles, CH, ps_mm[0], ps_mm[1], C, rms=False, vwait=ctx.n["v"])
            swn = ctx.n["s"]
            for oc in range(8):
                def vap(v, oc=oc, swn=swn):
                    v.wait_ge(sems["s"], swn)
                    v.tensor_tensor(wk["sp1"][:, 0:CH], wk["mix"][:, oc * CH:(oc + 1) * CH],
                                    lnp.mu_b[:, 0:CH], OP.subtract)
                    mark(ctx, v.tensor_tensor(wk["srcx"][:, oc * CH:(oc + 1) * CH],
                                              wk["sp1"][:, 0:CH], lnp.istd_b[:, 0:CH], OP.mult), "v")
                ctx.v(vap)
            vw = ctx.n["v"]
            def syu(sy, vw=vw):
                sy.wait_ge(sems["v"], vw)
            ctx.sy(syu)
            sload(ctx, stream_dram[:, t0:t0 + CH].rearrange("(c p) t -> p c t", p=128), wk["srcx"][:, :].rearrange("p (c t) -> p c t", c=8))
        ctx.barrier()

    def emit_moe():
        centT = ctx.sbuf("centT_sb", [128, 8 * NE], BF16)
        sfm = ctx.sbuf("sfm", [16, TL], BF16)
        gfm = ctx.sbuf("gfm", [16, TL], BF16)
        stok = ctx.sbuf("stok", [128, 32 * NE], BF16)
        colsb = ctx.sbuf("colsb", [128, 10 * 32], F32)
        accf = ctx.sbuf("accf", [128, 4 * CH], F32)
        bsum2 = ctx.sbuf("bsum2", [128, 8], F32)
        gload(ctx, centT[:, :].rearrange("p (c e) -> p c e", c=8), din["centT"].rearrange("(c p) e -> p c e", p=128))
        def vbs(v):
            mark(ctx, v.tensor_tensor(colsb[:, 0:8], vecs["bs"][:, 0:8], vecs["bs"][:, 8:16], OP.add), "v")
        ctx.v(vbs)
        _vwb = ctx.n["v"]
        def sbs(s, _vwb=_vwb):
            s.wait_ge(sems["v"], _vwb)
            mark(ctx, s.activation(bsum2[:, :], colsb[:, 0:8], AF.Identity,
                                   scale=C["misc"][:, 3:4]), "s")
        ctx.s(sbs)
        ctx.barrier()
        for tb in range(8):
            t0 = tb * CH
            gw_consumers(te=True)
            gload(ctx, wk["normed"][:, :].rearrange("p (c t) -> p c t", c=8), stream_dram[:, t0:t0 + CH].rearrange("(c p) t -> p c t", p=128))
            gw = ctx.n["gd"] * 16
            sprev = ctx.n["s"]
            def tsc(te, gw=gw, sprev=sprev):
                te.wait_ge(sems["gd"], gw)
                if sprev:
                    te.wait_ge(sems["s"], sprev)
                for kc in range(8):
                    inst = te.matmul(ps_mm[0][0:16, :], centT[:, kc * NE:(kc + 1) * NE],
                                     wk["normed"][:, kc * CH:(kc + 1) * CH],
                                     start=(kc == 0), stop=(kc == 7))
                mark(ctx, inst, "te")
            ctx.te(tsc)
            tw = ctx.n["te"]
            def ssc(s, tb=tb, tw=tw):
                s.wait_ge(sems["te"], tw)
                mark(ctx, s.copy(sfm[:, tb * CH:(tb + 1) * CH], ps_mm[0][0:16, :]), "s")
            ctx.s(ssc)
        ctx.barrier()
        for sb in range(32):
            vprev = ctx.n["v"]
            def ttp(te, sb=sb, vprev=vprev):
                if vprev:
                    te.wait_ge(sems["v"], vprev)
                mark(ctx, te.transpose(ps_tp[:, 0:16],
                                       sfm[:, sb * 128:(sb + 1) * 128], C["ident_bf"][0:16, 0:16]), "te")
            ctx.te(ttp)
            tw = ctx.n["te"]
            def vtp(v, sb=sb, tw=tw):
                v.wait_ge(sems["te"], tw)
                mark(ctx, v.tensor_copy(stok[:, sb * NE:(sb + 1) * NE], ps_tp[:, 0:16]), "v")
            ctx.v(vtp)
        ctx.barrier()
        def vg1(v):
            for sb in range(32):
                v.tensor_tensor(stok[:, sb * NE:(sb + 1) * NE], stok[:, sb * NE:(sb + 1) * NE],
                                vecs["eb_row"][:, :], OP.add)
            for sb in range(32):
                v.reduce_max(colsb[:, sb:sb + 1], stok[:, sb * NE:(sb + 1) * NE], axis=AX.X)
            for sb in range(32):
                inst = v.tensor_scalar(colsb[:, 32 + sb:33 + sb], colsb[:, sb:sb + 1], -1.0, None, OP.mult)
            mark(ctx, inst, "v")
        ctx.v(vg1)
        vw = ctx.n["v"]
        def sg1(s, vw=vw):
            s.wait_ge(sems["v"], vw)
            for sb in range(32):
                inst = s.activation(stok[:, sb * NE:(sb + 1) * NE], stok[:, sb * NE:(sb + 1) * NE],
                                    AF.Exp, bias=colsb[:, 32 + sb:33 + sb],
                                    accum_out=colsb[:, 64 + sb:65 + sb])
            mark(ctx, inst, "s")
        ctx.s(sg1)
        sw = ctx.n["s"]
        def vg2a(v, sw=sw):
            v.wait_ge(sems["s"], sw)
            for sb in range(32):
                inst = v.reduce_max(colsb[:, 224 + sb:225 + sb], stok[:, sb * NE:(sb + 1) * NE], axis=AX.X)
            mark(ctx, inst, "v")
        ctx.v(vg2a)
        _vw2 = ctx.n["v"]
        def sg2a(s, _vw2=_vw2):
            s.wait_ge(sems["v"], _vw2)
            for sb in range(32):
                inst = s.copy(colsb[:, 96 + sb:97 + sb], colsb[:, 224 + sb:225 + sb])
            mark(ctx, inst, "s")
        ctx.s(sg2a)
        _sw2 = ctx.n["s"]
        def vg2(v, sw=sw, _sw2=_sw2):
            v.wait_ge(sems["s"], _sw2)
            for sb in range(32):
                v.tensor_scalar(wk["sp1"][:, sb * NE:(sb + 1) * NE] if False else lnp.sq[:, sb * NE:(sb + 1) * NE],
                                stok[:, sb * NE:(sb + 1) * NE], colsb[:, 96 + sb:97 + sb], None, OP.is_lt)
            for sb in range(32):
                v.tensor_tensor(lnp.sq[:, sb * NE:(sb + 1) * NE], stok[:, sb * NE:(sb + 1) * NE],
                                lnp.sq[:, sb * NE:(sb + 1) * NE], OP.mult)
            for sb in range(32):
                inst = v.reduce_max(colsb[:, 256 + sb:257 + sb], lnp.sq[:, sb * NE:(sb + 1) * NE], axis=AX.X)
            mark(ctx, inst, "v")
        ctx.v(vg2)
        _vw3 = ctx.n["v"]
        def sg2b(s, _vw3=_vw3):
            s.wait_ge(sems["v"], _vw3)
            for sb in range(32):
                inst = s.copy(colsb[:, 128 + sb:129 + sb], colsb[:, 256 + sb:257 + sb])
            mark(ctx, inst, "s")
        ctx.s(sg2b)
        _sw3 = ctx.n["s"]
        def vg2c(v, _sw3=_sw3):
            v.wait_ge(sems["s"], _sw3)
            for sb in range(32):
                v.tensor_scalar(lnp.sq[:, sb * NE:(sb + 1) * NE], stok[:, sb * NE:(sb + 1) * NE],
                                colsb[:, 128 + sb:129 + sb], None, OP.is_ge)
            for sb in range(32):
                v.tensor_tensor(stok[:, sb * NE:(sb + 1) * NE], stok[:, sb * NE:(sb + 1) * NE],
                                lnp.sq[:, sb * NE:(sb + 1) * NE], OP.mult)
            for sb in range(32):
                inst = v.reciprocal(colsb[:, 288 + sb:289 + sb], colsb[:, 64 + sb:65 + sb])
            mark(ctx, inst, "v")
        ctx.v(vg2c)
        _vw4 = ctx.n["v"]
        def sg2c(s, _vw4=_vw4):
            s.wait_ge(sems["v"], _vw4)
            for sb in range(32):
                inst = s.copy(colsb[:, 160 + sb:161 + sb], colsb[:, 288 + sb:289 + sb])
            mark(ctx, inst, "s")
        ctx.s(sg2c)
        _sw4 = ctx.n["s"]
        def vg2d(v, _sw4=_sw4):
            v.wait_ge(sems["s"], _sw4)
            for sb in range(32):
                inst = v.tensor_scalar(stok[:, sb * NE:(sb + 1) * NE], stok[:, sb * NE:(sb + 1) * NE],
                                       colsb[:, 160 + sb:161 + sb], None, OP.mult)
            mark(ctx, inst, "v")
        ctx.v(vg2d)
        ctx.barrier()
        for sb in range(32):
            sprev = ctx.n["s"]
            def ttb(te, sb=sb, sprev=sprev):
                if sprev:
                    te.wait_ge(sems["s"], sprev)
                mark(ctx, te.transpose(ps_tp[0:16, :], stok[:, sb * NE:(sb + 1) * NE],
                                       C["ident_bf"][:, :]), "te")
            ctx.te(ttb)
            tw = ctx.n["te"]
            def stb(s, sb=sb, tw=tw):
                s.wait_ge(sems["te"], tw)
                mark(ctx, s.copy(gfm[:, sb * 128:(sb + 1) * 128], ps_tp[0:16, :]), "s")
            ctx.s(stb)
        ctx.barrier()
        if dbg_stage == "moeg":
            _s = ctx.n["s"]
            def syg2(sy, _s=_s):
                sy.wait_ge(sems["s"], _s)
            ctx.sy(syg2)
            sload(ctx, v_dram[0:16, 0:TL], gfm[:, :])
            ctx.barrier()
            dump_stream(v_dram[0:D, :])
            return
        for eg in range(9):
            two = 2 if eg < 8 else 2
            tprev = ctx.n["te"]
            def gwe(g, tprev=tprev):
                if tprev:
                    g.wait_ge(sems["te"], tprev)
            ctx.g(gwe)
            for e2 in range(2):
                src_w = din["WrT"][eg * 2 + e2] if eg < 8 else din["WsT"][e2]
                for kc in range(8):
                    gload(ctx, wbig[:, e2 * 8192 + kc * 1024:e2 * 8192 + (kc + 1) * 1024],
                          src_w[kc * 128:(kc + 1) * 128, :])
            gw = ctx.n["gd"] * 16
            for tb in range(8):
                t0 = tb * CH
                gw_consumers(te=True, v=True)
                gload(ctx, wk["normed"][:, :].rearrange("p (c t) -> p c t", c=8), stream_dram[:, t0:t0 + CH].rearrange("(c p) t -> p c t", p=128))
                gwu = ctx.n["gd"] * 16
                for e2 in range(2):
                    sprev = ctx.n["s"]
                    vprev = ctx.n["v"]
                    if eg < 8:
                        def tgb(te, eg=eg, e2=e2, tb=tb, sprev=sprev, vprev=vprev):
                            if sprev:
                                te.wait_ge(sems["s"], sprev)
                            if vprev:
                                te.wait_ge(sems["v"], vprev)
                            e = eg * 2 + e2
                            mark(ctx, te.matmul(ps_x[:, :], C["sel16"][:, e * 128:(e + 1) * 128],
                                                gfm[:, tb * CH:(tb + 1) * CH], start=True, stop=True), "te")
                        ctx.te(tgb)
                        tw = ctx.n["te"]
                        def sgb(s, tw=tw, e2=e2):
                            s.wait_ge(sems["te"], tw)
                            mark(ctx, s.copy(wk["sz"][:, e2 * CH:(e2 + 1) * CH], ps_x[:, :]), "s")
                        ctx.s(sgb)
                        swg = ctx.n["s"]
                        def vue(v, e2=e2, swg=swg, gwu=gwu):
                            v.wait_ge(sems["s"], swg)
                            v.wait_ge(sems["gd"], gwu)
                            for kc in range(8):
                                inst = v.tensor_tensor(wk["gat"][:, (e2 * 8 + kc) * CH:(e2 * 8 + kc + 1) * CH],
                                                       wk["normed"][:, kc * CH:(kc + 1) * CH],
                                                       wk["sz"][:, e2 * CH:(e2 + 1) * CH], OP.mult)
                            mark(ctx, inst, "v")
                        ctx.v(vue)
                    else:
                        def vue(v, e2=e2, gwu=gwu, vprev=vprev):
                            v.wait_ge(sems["gd"], gwu)
                            if vprev:
                                v.wait_ge(sems["v"], vprev)
                            for kc in range(8):
                                inst = v.tensor_scalar(wk["gat"][:, (e2 * 8 + kc) * CH:(e2 * 8 + kc + 1) * CH],
                                                       wk["normed"][:, kc * CH:(kc + 1) * CH],
                                                       C["misc"][:, 3:4], None, OP.mult)
                            mark(ctx, inst, "v")
                        ctx.v(vue)
                vwu = ctx.n["v"]
                for oc in range(8):
                    sprev = ctx.n["s"]
                    def tme(te, oc=oc, gw=gw, vwu=vwu, sprev=sprev):
                        te.wait_ge(sems["gd"], gw)
                        te.wait_ge(sems["v"], vwu)
                        if sprev:
                            te.wait_ge(sems["s"], sprev)
                        for e2 in range(2):
                            for kc in range(8):
                                idx = e2 * 8 + kc
                                inst = te.matmul(ps_mm[oc % 2][:, :],
                                                 wbig[:, e2 * 8192 + kc * 1024 + oc * 128:
                                                       e2 * 8192 + kc * 1024 + (oc + 1) * 128],
                                                 wk["gat"][:, idx * CH:(idx + 1) * CH],
                                                 start=(idx == 0), stop=(idx == 15))
                        mark(ctx, inst, "te")
                    ctx.te(tme)
                    tw = ctx.n["te"]
                    def sme(s, oc=oc, tw=tw):
                        s.wait_ge(sems["te"], tw)
                        mark(ctx, s.copy(wk["mix"][:, oc * CH:(oc + 1) * CH], ps_mm[oc % 2][:, :]), "s")
                    ctx.s(sme)
                sw = ctx.n["s"]
                def gacc(g, eg=eg, t0=t0, sw=sw):
                    g.wait_ge(sems["s"], sw)
                    if eg == 0:
                        inst = g.dma_start(out=routed_dram[:, t0:t0 + CH].rearrange("(c p) t -> p c t", p=128),
                                           in_=wk["mix"][:, :].rearrange("p (c t) -> p c t", c=8))
                    else:
                        inst = g.dma_start(out=routed_dram[:, t0:t0 + CH].rearrange("(c p) t -> p c t", p=128),
                                           in_=wk["mix"][:, :].rearrange("p (c t) -> p c t", c=8),
                                           accum_op=OP.add)
                    mark(ctx, inst, "gd")
                ctx.g(gacc)
            ctx.barrier()
        if dbg_stage == "moer":
            def gdr(g):
                mark(ctx, g.dma_start(out=out_dram, in_=routed_dram), "gd")
            ctx.g(gdr)
            ctx.barrier()
            return
        for tb in range(8):
            t0 = tb * CH
            gw_consumers(te=True, v=True, s=True)
            _sdn = ctx.n["sd"] * 16
            def gsd2(g, _sdn=_sdn):
                if _sdn:
                    g.wait_ge(sems["sd"], _sdn)
            ctx.g(gsd2)
            gload(ctx, wk["normed"][:, :].rearrange("p (c t) -> p c t", c=8), stream_dram[:, t0:t0 + CH].rearrange("(c p) t -> p c t", p=128))
            for half in range(2):
                vprev = ctx.n["v"]
                def gacl(g, vprev=vprev):
                    if vprev:
                        g.wait_ge(sems["v"], vprev)
                ctx.g(gacl)
                gload(ctx, accf[:, :].rearrange("p (c t) -> p c t", c=4),
                      routed_dram[half * 512:half * 512 + 512, t0:t0 + CH].rearrange(
                          "(c p) t -> p c t", p=128))
                gw2 = ctx.n["gd"] * 16
                for j in range(4):
                    oc = half * 4 + j
                    sprev = ctx.n["s"]
                    vprev2 = ctx.n["v"]
                    def tbr(te, oc=oc, tb=tb, sprev=sprev, vprev2=vprev2):
                        if sprev:
                            te.wait_ge(sems["s"], sprev)
                        if vprev2:
                            te.wait_ge(sems["v"], vprev2)
                        mark(ctx, te.matmul(ps_x[:, :], vecs["br"][:, oc * 128:(oc + 1) * 128],
                                            gfm[:, tb * CH:(tb + 1) * CH], start=True, stop=True), "te")
                    ctx.te(tbr)
                    tw = ctx.n["te"]
                    def vas(v, oc=oc, j=j, tw=tw, gw2=gw2):
                        v.wait_ge(sems["te"], tw)
                        v.wait_ge(sems["gd"], gw2)
                        v.tensor_tensor(wk["sp1"][:, 0:CH], accf[:, j * CH:(j + 1) * CH], ps_x[:, :], OP.add)
                        v.tensor_tensor(wk["sp1"][:, 0:CH], wk["sp1"][:, 0:CH],
                                        wk["normed"][:, oc * CH:(oc + 1) * CH], OP.add)
                        mark(ctx, v.scalar_tensor_tensor(wk["mix"][:, oc * CH:(oc + 1) * CH],
                                                         wk["sp1"][:, 0:CH], bsum2[:, oc:oc + 1],
                                                         wk["normed"][:, oc * CH:(oc + 1) * CH],
                                                         OP.add, OP.add), "v")
                    ctx.v(vas)
            mix_tiles = [wk["mix"][:, oc * CH:(oc + 1) * CH] for oc in range(8)]
            emit_norm_stats(ctx, lnp, mix_tiles, CH, ps_mm[0], ps_mm[1], C, rms=False, vwait=ctx.n["v"])
            swn = ctx.n["s"]
            for oc in range(8):
                def vap(v, oc=oc, swn=swn):
                    v.wait_ge(sems["s"], swn)
                    v.tensor_tensor(wk["sp1"][:, 0:CH], wk["mix"][:, oc * CH:(oc + 1) * CH],
                                    lnp.mu_b[:, 0:CH], OP.subtract)
                    mark(ctx, v.tensor_tensor(wk["srcx"][:, oc * CH:(oc + 1) * CH],
                                              wk["sp1"][:, 0:CH], lnp.istd_b[:, 0:CH], OP.mult), "v")
                ctx.v(vap)
            vw = ctx.n["v"]
            def syu(sy, vw=vw):
                sy.wait_ge(sems["v"], vw)
            ctx.sy(syu)
            sload(ctx, stream_dram[:, t0:t0 + CH].rearrange("(c p) t -> p c t", p=128), wk["srcx"][:, :].rearrange("p (c t) -> p c t", c=8))
        ctx.barrier()

    ktm_dram = ctx.dram_tmp("ktm_dram", [TL, D])

    def emit_l1a():
        qs_sb = ctx.sbuf("qs_sb", [128, 8 * CH], BF16)
        kexp_sb = ctx.sbuf("kexp_sb", [128, 8 * CH], BF16)
        rkn_sb = ctx.sbuf("rkn_sb", [128, 8 * CH], BF16)
        Ssb = ctx.sbuf("Ssb", [128, 8 * 64], F32)
        Ssb_bf = ctx.sbuf("Ssb_bf", [128, 8 * 64], BF16)
        ksum_sb = ctx.sbuf("ksum_sb", [128, 10], F32)
        kcar = ctx.sbuf("kcar", [128, 9], F32)
        ktmp = ctx.sbuf("ktmp", [128, 64], F32)
        zt = ctx.sbuf("zt", [128, CH], BF16)
        rden_sb = ctx.sbuf("rden_sb", [16, CH], BF16)
        vtm_sb = ctx.sbuf("vtm_sb", [128, 1024], BF16)
        ktm_sb = ctx.sbuf("ktm_sb", [128, 1024], BF16)
        msb = ctx.sbuf("msb", [128, 2 * 128], BF16)
        def vz(v):
            v.memset(zt[:, :], 0.0)
            v.memset(ksum_sb[:, :], 0.0)
            mark(ctx, v.memset(Ssb[:, :], 0.0), "v")
        ctx.v(vz)
        for kc in range(8):
            gload(ctx, wbig[:, kc * 2048:(kc + 1) * 2048],
                  din["qkvT"][kc * 128:(kc + 1) * 128, 0:2048])
        gw = ctx.n["gd"] * 16
        ctx.barrier()
        for tb in range(8):
            t0 = tb * CH
            gw_consumers(te=True, v=True)
            gload(ctx, wk["normed"][:, :].rearrange("p (c t) -> p c t", c=8), stream_dram[:, t0:t0 + CH].rearrange("(c p) t -> p c t", p=128))
            gwu = ctx.n["gd"] * 16
            for oc in range(16):
                sprev = ctx.n["s"]
                def tqk(te, oc=oc, gw=gw, gwu=gwu, sprev=sprev):
                    te.wait_ge(sems["gd"], max(gw, gwu))
                    if sprev:
                        te.wait_ge(sems["s"], sprev)
                    for kc in range(8):
                        inst = te.matmul(ps_mm[oc % 2][:, :],
                                         wbig[:, kc * 2048 + oc * 128:kc * 2048 + (oc + 1) * 128],
                                         wk["normed"][:, kc * CH:(kc + 1) * CH],
                                         start=(kc == 0), stop=(kc == 7))
                    mark(ctx, inst, "te")
                ctx.te(tqk)
                tw = ctx.n["te"]
                def sqk(s, oc=oc, tw=tw):
                    s.wait_ge(sems["te"], tw)
                    mark(ctx, s.activation(wk["sp1"][:, (oc % 2) * CH:(oc % 2 + 1) * CH],
                                           ps_mm[oc % 2][:, :], AF.Exp, scale=0.125), "s")
                ctx.s(sqk)
                sw = ctx.n["s"]
                sdp = ctx.n["sd"] * 16
                if oc < 8:
                    def syq(sy, sw=sw):
                        sy.wait_ge(sems["s"], sw)
                    ctx.sy(syq)
                    sload(ctx, p_dram[oc * 128:(oc + 1) * 128, t0:t0 + CH],
                          wk["sp1"][:, (oc % 2) * CH:(oc % 2 + 1) * CH])
                else:
                    ko = oc - 8
                    def vke(v, ko=ko, oc=oc, tb=tb, sw=sw, sdp=sdp):
                        v.wait_ge(sems["s"], sw)
                        if sdp:
                            v.wait_ge(sems["sd"], sdp)
                        v.tensor_scalar(kexp_sb[:, ko * CH:(ko + 1) * CH],
                                        wk["sp1"][:, (oc % 2) * CH:(oc % 2 + 1) * CH],
                                        C["misc"][:, 4:5], None, OP.add)
                        mark(ctx, v.reduce_sum(ktmp[:, tb * 8 + ko:tb * 8 + ko + 1],
                                               kexp_sb[:, ko * CH:(ko + 1) * CH], axis=AX.X), "v")
                    ctx.v(vke)
                    vw = ctx.n["v"]
                    def syk(sy, vw=vw):
                        sy.wait_ge(sems["v"], vw)
                    ctx.sy(syk)
                    sload(ctx, kexp_dram[ko * 128:(ko + 1) * 128, t0:t0 + CH],
                          kexp_sb[:, ko * CH:(ko + 1) * CH])
        ctx.barrier()
        def vks(v):
            for ko in range(8):
                inst = v.reduce_sum(ksum_sb[:, ko:ko + 1], ktmp[:, ko::8], axis=AX.X)
            mark(ctx, inst, "v")
        ctx.v(vks)
        ctx.barrier()
        for kc in range(8):
            gload(ctx, wbig[:, kc * 1024:(kc + 1) * 1024],
                  din["qkvT"][kc * 128:(kc + 1) * 128, 2048:3072])
        gw = ctx.n["gd"] * 16
        ctx.barrier()
        nst_tot = [0]
        for tb in range(8):
            t0 = tb * CH
            gw_consumers(te=True, v=True, s=True)
            gload(ctx, wk["normed"][:, :].rearrange("p (c t) -> p c t", c=8), stream_dram[:, t0:t0 + CH].rearrange("(c p) t -> p c t", p=128))
            gload(ctx, kexp_sb[:, :].rearrange("p (c t) -> p c t", c=8), kexp_dram[:, t0:t0 + CH].rearrange("(c p) t -> p c t", p=128))
            gwu = ctx.n["gd"] * 16
            for oc in range(8):
                sprev = ctx.n["s"]
                def tv(te, oc=oc, gw=gw, gwu=gwu, sprev=sprev):
                    te.wait_ge(sems["gd"], max(gw, gwu))
                    if sprev:
                        te.wait_ge(sems["s"], sprev)
                    for kc in range(8):
                        inst = te.matmul(ps_mm[oc % 2][:, :],
                                         wbig[:, kc * 1024 + oc * 128:kc * 1024 + (oc + 1) * 128],
                                         wk["normed"][:, kc * CH:(kc + 1) * CH],
                                         start=(kc == 0), stop=(kc == 7))
                    mark(ctx, inst, "te")
                ctx.te(tv)
                tw = ctx.n["te"]
                def sv(s, oc=oc, tw=tw):
                    s.wait_ge(sems["te"], tw)
                    mark(ctx, s.copy(wk["mix"][:, oc * CH:(oc + 1) * CH], ps_mm[oc % 2][:, :]), "s")
                ctx.s(sv)
            swv = ctx.n["s"]
            for tsc in range(4):
                sdp = ctx.n["sd"] * 16
                for fc in range(8):
                    for which, srct, dstt in [(0, wk["mix"], vtm_sb), (1, kexp_sb, ktm_sb)]:
                        sprev = ctx.n["s"]
                        def ttr(te, tsc=tsc, fc=fc, srct=srct, swv=swv, sprev=sprev, sdp=sdp):
                            te.wait_ge(sems["s"], max(swv, sprev))
                            if sdp:
                                te.wait_ge(sems["sd"], sdp)
                            mark(ctx, te.transpose(ps_tp[:, :],
                                                   srct[:, fc * CH + tsc * 128:fc * CH + (tsc + 1) * 128],
                                                   C["ident_bf"][:, :]), "te")
                        ctx.te(ttr)
                        tw = ctx.n["te"]
                        sdpt = ctx.n["sd"] * 16
                        def str_(s, fc=fc, dstt=dstt, tw=tw, sdpt=sdpt):
                            s.wait_ge(sems["te"], tw)
                            if sdpt:
                                s.wait_ge(sems["sd"], sdpt)
                            mark(ctx, s.copy(dstt[:, fc * 128:(fc + 1) * 128], ps_tp[:, :]), "s")
                        ctx.s(str_)
                swt = ctx.n["s"]
                step = tb * 4 + tsc
                def tst(te, swt=swt, step=step):
                    te.wait_ge(sems["s"], swt)
                    for h in range(16):
                        j, hp = h % 2, h // 2
                        inst = te.matmul(ps_bl[0][j * 64:(j + 1) * 64, hp * 64:(hp + 1) * 64],
                                         ktm_sb[:, h * 64:(h + 1) * 64],
                                         vtm_sb[:, h * 64:(h + 1) * 64],
                                         start=(step == 0), stop=(step == 31))
                    mark(ctx, inst, "te")
                ctx.te(tst)
                twd = ctx.n["te"]
                def syt(sy, twd=twd):
                    sy.wait_ge(sems["te"], twd)
                ctx.sy(syt)
                sload(ctx, vtm_dram[t0 + tsc * 128:t0 + (tsc + 1) * 128, :], vtm_sb[:, :])
                sload(ctx, ktm_dram[t0 + tsc * 128:t0 + (tsc + 1) * 128, :], ktm_sb[:, :])
        ctx.barrier()
        def sst(s):
            mark(ctx, s.copy(Ssb[:, :], ps_bl[0][:, :]), "s")
        ctx.s(sst)
        ctx.barrier()
        sload(ctx, sag_in[:, 0:512], Ssb[:, :])
        sload(ctx, sag_in[:, 512:520], ksum_sb[:, 0:8])
        ctx.barrier()
        collective(ctx, "AllGather", OP.bypass, PAIRS, sag_in, sag_out)
        ctx.barrier()
        sini = ctx.sbuf("sini", [128, 520], F32)
        knf = ctx.sbuf("knf", [128, CH], F32)
        gload(ctx, sini[:, :], sag_out[0:128, :])
        for kc in range(8):
            gload(ctx, wbig[:, kc * 1024:(kc + 1) * 1024], din["woutT"][kc * 128:(kc + 1) * 128, :])
        ctx.barrier()
        def vsi(v):
            v.tensor_scalar(Ssb[:, :], sini[:, 0:512], C["flag"][:, 0:1], None, OP.mult)
            mark(ctx, v.tensor_copy(Ssb_bf[:, :], Ssb[:, :]), "v")
        ctx.v(vsi)
        _vw = ctx.n["v"]
        def ssi(s, _vw=_vw):
            s.wait_ge(sems["v"], _vw)
            mark(ctx, s.activation(kcar[:, 0:8], sini[:, 512:520], AF.Identity,
                                   scale=C["flag"][:, 0:1]), "s")
        ctx.s(ssi)
        ctx.barrier()
        for tb in range(8):
            t0 = tb * CH
            gw_consumers(te=True, v=True, s=True)
            gload(ctx, wk["normed"][:, :].rearrange("p (c t) -> p c t", c=8), p_dram[:, t0:t0 + CH].rearrange("(c p) t -> p c t", p=128))
            gload(ctx, kexp_sb[:, :].rearrange("p (c t) -> p c t", c=8), kexp_dram[:, t0:t0 + CH].rearrange("(c p) t -> p c t", p=128))
            gwp = ctx.n["gd"] * 16
            for oc in range(8):
                vprev = ctx.n["v"]
                sprev = ctx.n["s"]
                def vkn(v, oc=oc, gwp=gwp, sprev=sprev):
                    v.wait_ge(sems["gd"], gwp)
                    if sprev:
                        v.wait_ge(sems["s"], sprev)
                    mark(ctx, v.tensor_tensor_scan(knf[:, :], kexp_sb[:, oc * CH:(oc + 1) * CH],
                                         zt[:, :], kcar[:, oc:oc + 1], OP.add, OP.add), "v")
                ctx.v(vkn)
                vw = ctx.n["v"]
                def skn(s, oc=oc, vw=vw):
                    s.wait_ge(sems["v"], vw)
                    mark(ctx, s.copy(kcar[:, oc:oc + 1], knf[:, CH - 1:CH]), "s")
                ctx.s(skn)
                def vkr(v, oc=oc):
                    with ctx.nc.allow_low_precision(reason="1/knorm to bf16 is ample here"):
                        mark(ctx, v.reciprocal(rkn_sb[:, oc * CH:(oc + 1) * CH], knf[:, :]), "v")
                ctx.v(vkr)
            swd = ctx.n["s"]
            def tdn(te, gwp=gwp, swd=swd):
                te.wait_ge(sems["gd"], gwp)
                te.wait_ge(sems["s"], swd)
                for hp in range(8):
                    inst = te.matmul(ps_x[0:16, :], C["selH"][:, hp * 16:(hp + 1) * 16],
                                     wk["normed"][:, hp * CH:(hp + 1) * CH],
                                     start=(hp == 0), stop=(hp == 7))
                mark(ctx, inst, "te")
            ctx.te(tdn)
            tw = ctx.n["te"]
            def sdn(v, tw=tw):
                v.wait_ge(sems["te"], tw)
                with ctx.nc.allow_low_precision(reason="1/denom to bf16 is ample here"):
                    mark(ctx, v.reciprocal(rden_sb[:, :], ps_x[0:16, :]), "v")
            ctx.v(sdn)
            swd2 = ctx.n["s"]
            for oc in range(8):
                vprev = ctx.n["v"]
                def trb(te, oc=oc, swd2=swd2, vprev=vprev):
                    te.wait_ge(sems["s"], swd2)
                    if vprev:
                        te.wait_ge(sems["v"], vprev)
                    mark(ctx, te.matmul(ps_x[:, :], C["selHB"][:, oc * 128:(oc + 1) * 128],
                                        rden_sb[:, :], start=True, stop=True), "te")
                ctx.te(trb)
                tw = ctx.n["te"]
                def srb(s, tw=tw):
                    s.wait_ge(sems["te"], tw)
                    mark(ctx, s.copy(wk["sp1"][:, 0:CH], ps_x[:, :]), "s")
                ctx.s(srb)
                sw = ctx.n["s"]
                def vqs(v, oc=oc, sw=sw):
                    v.wait_ge(sems["s"], sw)
                    v.tensor_tensor(qs_sb[:, oc * CH:(oc + 1) * CH],
                                    wk["normed"][:, oc * CH:(oc + 1) * CH],
                                    rkn_sb[:, oc * CH:(oc + 1) * CH], OP.mult)
                    mark(ctx, v.tensor_tensor(qs_sb[:, oc * CH:(oc + 1) * CH],
                                              qs_sb[:, oc * CH:(oc + 1) * CH],
                                              wk["sp1"][:, 0:CH], OP.mult), "v")
                ctx.v(vqs)
            for tsc in range(4):
                tprev = ctx.n["te"]
                def syc(sy, tprev=tprev):
                    if tprev:
                        sy.wait_ge(sems["te"], tprev)
                ctx.sy(syc)
                sload(ctx, vtm_sb[:, :], vtm_dram[t0 + tsc * 128:t0 + (tsc + 1) * 128, :])
                sload(ctx, ktm_sb[:, :], ktm_dram[t0 + tsc * 128:t0 + (tsc + 1) * 128, :])
                sdw = ctx.n["sd"] * 16
                vw_all = ctx.n["v"]
                for hp in range(8):
                    sprev = ctx.n["s"]
                    vprev = ctx.n["v"]
                    def tsco(te, hp=hp, tsc=tsc, sdw=sdw, vw_all=vw_all, sprev=sprev, vprev=vprev):
                        te.wait_ge(sems["sd"], sdw)
                        te.wait_ge(sems["v"], max(vw_all, vprev))
                        if sprev:
                            te.wait_ge(sems["s"], sprev)
                        for j in range(2):
                            mark(ctx, te.matmul(ps_mm[j][:, 0:128],
                                                kexp_sb[j * 64:(j + 1) * 64,
                                                        hp * CH + tsc * 128:hp * CH + (tsc + 1) * 128],
                                                qs_sb[j * 64:(j + 1) * 64,
                                                      hp * CH + tsc * 128:hp * CH + (tsc + 1) * 128],
                                                start=True, stop=True), "te")
                    ctx.te(tsco)
                    tw = ctx.n["te"]
                    def vmask(v, tw=tw):
                        v.wait_ge(sems["te"], tw)
                        v.tensor_tensor(msb[:, 0:128], ps_mm[0][:, 0:128], C["cmask"][:, :], OP.mult)
                        mark(ctx, v.tensor_tensor(msb[:, 128:256], ps_mm[1][:, 0:128],
                                                  C["cmask"][:, :], OP.mult), "v")
                    ctx.v(vmask)
                    vwm = ctx.n["v"]
                    sprev3 = ctx.n["s"]
                    def tyy(te, hp=hp, tsc=tsc, vwm=vwm, sprev3=sprev3):
                        te.wait_ge(sems["v"], vwm)
                        if sprev3 > 1:
                            te.wait_ge(sems["s"], sprev3 - 1)
                        for j in range(2):
                            h = 2 * hp + j
                            te.matmul(ps_y[hp % 2][j * 64:(j + 1) * 64, 0:128],
                                      vtm_sb[:, h * 64:(h + 1) * 64], msb[:, j * 128:(j + 1) * 128],
                                      start=True, stop=False)
                            te.matmul(ps_y[hp % 2][j * 64:(j + 1) * 64, 0:128],
                                      Ssb_bf[j * 64:(j + 1) * 64, hp * 64:(hp + 1) * 64],
                                      qs_sb[j * 64:(j + 1) * 64,
                                            hp * CH + tsc * 128:hp * CH + (tsc + 1) * 128],
                                      start=False, stop=True)
                            inst = te.matmul(ps_bl[hp % 2][j * 64:(j + 1) * 64, 0:64],
                                             ktm_sb[:, h * 64:(h + 1) * 64],
                                             vtm_sb[:, h * 64:(h + 1) * 64],
                                             start=True, stop=True)
                        mark(ctx, inst, "te")
                    ctx.te(tyy)
                    tw2 = ctx.n["te"]
                    def sev(s, hp=hp, tsc=tsc, tw2=tw2):
                        s.wait_ge(sems["te"], tw2)
                        mark(ctx, s.copy(wk["mix"][:, hp * CH + tsc * 128:hp * CH + (tsc + 1) * 128],
                                         ps_y[hp % 2][:, 0:128]), "s")
                    ctx.s(sev)
                    sw2 = ctx.n["s"]
                    def vsu(v, hp=hp, sw2=sw2, tw2=tw2):
                        v.wait_ge(sems["te"], tw2)
                        for j in range(2):
                            v.tensor_tensor(Ssb[j * 64:(j + 1) * 64, hp * 64:(hp + 1) * 64],
                                            Ssb[j * 64:(j + 1) * 64, hp * 64:(hp + 1) * 64],
                                            ps_bl[hp % 2][j * 64:(j + 1) * 64, 0:64], OP.add)
                        mark(ctx, v.tensor_copy(Ssb_bf[:, hp * 64:(hp + 1) * 64],
                                                Ssb[:, hp * 64:(hp + 1) * 64]), "v")
                    ctx.v(vsu)
            if dbg_stage == "l1q":
                _v = ctx.n["v"]
                def syq2(sy, _v=_v):
                    sy.wait_ge(sems["v"], _v)
                ctx.sy(syq2)
                sload(ctx, v_dram[0:D, t0:t0 + CH].rearrange("(c p) t -> p c t", p=128),
                      qs_sb[:, :].rearrange("p (c t) -> p c t", c=8))
            if dbg_stage == "l1y":
                _s = ctx.n["s"]
                def syy2(sy, _s=_s):
                    sy.wait_ge(sems["s"], _s)
                ctx.sy(syy2)
                sload(ctx, v_dram[0:D, t0:t0 + CH].rearrange("(c p) t -> p c t", p=128),
                      wk["mix"][:, :].rearrange("p (c t) -> p c t", c=8))
            gw_consumers(te=True, v=True)
            gload(ctx, wk["normed"][:, :].rearrange("p (c t) -> p c t", c=8), stream_dram[:, t0:t0 + CH].rearrange("(c p) t -> p c t", p=128))
            gws = ctx.n["gd"] * 16
            vwc = ctx.n["v"]
            swc = ctx.n["s"]
            for oc in range(8):
                sprev = ctx.n["s"]
                def two_(te, oc=oc, swc=swc, sprev=sprev):
                    te.wait_ge(sems["s"], max(swc, sprev))
                    for hp in range(8):
                        inst = te.matmul(ps_mm[oc % 2][:, :],
                                         wbig[:, hp * 1024 + oc * 128:hp * 1024 + (oc + 1) * 128],
                                         wk["mix"][:, hp * CH:(hp + 1) * CH],
                                         start=(hp == 0), stop=(hp == 7))
                    mark(ctx, inst, "te")
                ctx.te(two_)
                tw = ctx.n["te"]
                def swo(s, oc=oc, tw=tw):
                    s.wait_ge(sems["te"], tw)
                    mark(ctx, s.copy(wk["srcx"][:, oc * CH:(oc + 1) * CH], ps_mm[oc % 2][:, :]), "s")
                ctx.s(swo)
            sww = ctx.n["s"]
            def vres(v, sww=sww, gws=gws):
                v.wait_ge(sems["s"], sww)
                v.wait_ge(sems["gd"], gws)
                for oc in range(8):
                    inst = v.tensor_tensor(wk["mix"][:, oc * CH:(oc + 1) * CH],
                                           wk["srcx"][:, oc * CH:(oc + 1) * CH],
                                           wk["normed"][:, oc * CH:(oc + 1) * CH], OP.add)
                mark(ctx, inst, "v")
            ctx.v(vres)
            mix_tiles = [wk["mix"][:, oc * CH:(oc + 1) * CH] for oc in range(8)]
            emit_norm_stats(ctx, lnp, mix_tiles, CH, ps_mm[0], ps_mm[1], C, rms=False, vwait=ctx.n["v"])
            swn = ctx.n["s"]
            for oc in range(8):
                def vap(v, oc=oc, swn=swn):
                    v.wait_ge(sems["s"], swn)
                    v.tensor_tensor(wk["sp1"][:, 0:CH], wk["mix"][:, oc * CH:(oc + 1) * CH],
                                    lnp.mu_b[:, 0:CH], OP.subtract)
                    mark(ctx, v.tensor_tensor(wk["srcx"][:, oc * CH:(oc + 1) * CH],
                                              wk["sp1"][:, 0:CH], lnp.istd_b[:, 0:CH], OP.mult), "v")
                ctx.v(vap)
            vw = ctx.n["v"]
            def syu(sy, vw=vw):
                sy.wait_ge(sems["v"], vw)
            ctx.sy(syu)
            sload(ctx, stream_dram[:, t0:t0 + CH].rearrange("(c p) t -> p c t", p=128), wk["srcx"][:, :].rearrange("p (c t) -> p c t", c=8))
            ctx.barrier()

    def emit_l1f():
        fq32 = ctx.sbuf("fq32", [128, 1024], F32)
        vq1 = ctx.sbuf("vq1", [128, 1024], F32)
        vq2 = ctx.sbuf("vq2", [128, 1024], F32)
        qcols = ctx.sbuf("qcols", [128, 24], F32)
        fout = ctx.sbuf("fout", [128, 2 * CH], F32)
        xq_sb = ctx.sbuf("xq_sb", [128, 8 * CH], BF16)
        knf2 = ctx.sbuf("knf2", [128, CH], F32)
        misc = C["misc"]
        def vz0(v):
            mark(ctx, v.memset(qcols[:, :], 0.0), "v")
        ctx.v(vz0)
        ctx.barrier()
        for kc in range(8):
            sload(ctx, fq32[:, :], din["fqT"][kc * 128:(kc + 1) * 128, :])
            sdw = ctx.n["sd"] * 16
            def vwa(v, kc=kc, sdw=sdw):
                v.wait_ge(sems["sd"], sdw)
                v.reduce_max(qcols[:, 0:1], fq32[:, :], axis=AX.X, apply_absolute_value=True)
                if kc == 0:
                    inst = v.tensor_copy(qcols[:, 1:2], qcols[:, 0:1])
                else:
                    inst = v.tensor_tensor(qcols[:, 1:2], qcols[:, 1:2], qcols[:, 0:1], OP.max)
                mark(ctx, inst, "v")
            ctx.v(vwa)
            ctx.barrier()
        colmax_bcast(qcols[:, 1:2], qcols[:, 2:3])
        ctx.barrier()
        def vsw(v):
            mark(ctx, v.tensor_scalar(qcols[:, 16:17], qcols[:, 2:3], misc[:, 7:8], None, OP.max), "v")
        ctx.v(vsw)
        _vw = ctx.n["v"]
        def ssw(s, _vw=_vw):
            s.wait_ge(sems["v"], _vw)
            mark(ctx, s.copy(qcols[:, 18:19], qcols[:, 16:17]), "s")
        ctx.s(ssw)
        _sw = ctx.n["s"]
        def vsw2(v, _sw=_sw):
            v.wait_ge(sems["s"], _sw)
            mark(ctx, v.reciprocal(qcols[:, 20:21], qcols[:, 18:19]), "v")
        ctx.v(vsw2)
        _vw = ctx.n["v"]
        def ssw2(s, _vw=_vw):
            s.wait_ge(sems["v"], _vw)
            mark(ctx, s.activation(qcols[:, 3:4], qcols[:, 20:21], AF.Identity,
                                   scale=misc[:, 6:7]), "s")
        ctx.s(ssw2)
        ctx.barrier()
        for kc in range(8):
            sload(ctx, fq32[:, :], din["fqT"][kc * 128:(kc + 1) * 128, :])
            sdw = ctx.n["sd"] * 16
            def vq(v, kc=kc, sdw=sdw):
                v.wait_ge(sems["sd"], sdw)
                v.tensor_scalar(vq1[:, :], fq32[:, :], qcols[:, 3:4], None, OP.mult)
                v.tensor_scalar(vq2[:, :], vq1[:, :], misc[:, 5:6], None, OP.mult)
                v.tensor_tensor(vq1[:, :], vq2[:, :], vq1[:, :], OP.subtract)
                mark(ctx, v.tensor_tensor(wbig[:, kc * 1024:(kc + 1) * 1024], vq2[:, :], vq1[:, :],
                                          OP.subtract), "v")
            ctx.v(vq)
            ctx.barrier()
        for tb in range(8):
            t0 = tb * CH
            gw_consumers(v=True)
            gload(ctx, wk["normed"][:, :].rearrange("p (c t) -> p c t", c=8), stream_dram[:, t0:t0 + CH].rearrange("(c p) t -> p c t", p=128))
            gw = ctx.n["gd"] * 16
            def vxa(v, tb=tb, gw=gw):
                v.wait_ge(sems["gd"], gw)
                for oc in range(8):
                    v.reduce_max(qcols[:, 4:5], wk["normed"][:, oc * CH:(oc + 1) * CH],
                                 axis=AX.X, apply_absolute_value=True)
                    if tb == 0 and oc == 0:
                        inst = v.tensor_copy(qcols[:, 5:6], qcols[:, 4:5])
                    else:
                        inst = v.tensor_tensor(qcols[:, 5:6], qcols[:, 5:6], qcols[:, 4:5], OP.max)
                mark(ctx, inst, "v")
            ctx.v(vxa)
            ctx.barrier()
        colmax_bcast(qcols[:, 5:6], qcols[:, 6:7])
        ctx.barrier()
        sload(ctx, amax_in, qcols[0:1, 6:14])
        ctx.barrier()
        collective(ctx, "AllReduce", OP.max, ALL8, amax_in, amax_out)
        ctx.barrier()
        gload(ctx, qcols[0:1, 7:8], amax_out[0:1, 0:1])
        ctx.barrier()
        def tbc2(te):
            mark(ctx, te.matmul(ps_x[:, 0:1], C["ones_f32"][:, :], qcols[0:1, 7:8],
                                start=True, stop=True), "te")
        ctx.te(tbc2)
        tw_b = ctx.n["te"]
        def vbc2(v, tw_b=tw_b):
            v.wait_ge(sems["te"], tw_b)
            mark(ctx, v.tensor_copy(qcols[:, 8:9], ps_x[:, 0:1]), "v")
        ctx.v(vbc2)
        ctx.barrier()
        def vsx(v):
            mark(ctx, v.tensor_scalar(qcols[:, 17:18], qcols[:, 8:9], misc[:, 7:8], None, OP.max), "v")
        ctx.v(vsx)
        _vw = ctx.n["v"]
        def ssx(s, _vw=_vw):
            s.wait_ge(sems["v"], _vw)
            mark(ctx, s.copy(qcols[:, 19:20], qcols[:, 17:18]), "s")
        ctx.s(ssx)
        _sw = ctx.n["s"]
        def vsx2(v, _sw=_sw):
            v.wait_ge(sems["s"], _sw)
            v.reciprocal(qcols[:, 21:22], qcols[:, 19:20])
            mark(ctx, v.scalar_tensor_tensor(qcols[:, 22:23], qcols[:, 18:19], misc[:, 8:9],
                                             qcols[:, 19:20], OP.mult, OP.mult), "v")
        ctx.v(vsx2)
        _vw = ctx.n["v"]
        def ssx2(s, _vw=_vw):
            s.wait_ge(sems["v"], _vw)
            s.activation(qcols[:, 9:10], qcols[:, 21:22], AF.Identity, scale=misc[:, 6:7])
            mark(ctx, s.copy(qcols[:, 10:11], qcols[:, 22:23]), "s")
        ctx.s(ssx2)
        ctx.barrier()
        if dbg_stage == "l1fs":
            _v = ctx.n["v"]
            _s = ctx.n["s"]
            def syd(sy, _v=_v, _s=_s):
                sy.wait_ge(sems["v"], _v)
                sy.wait_ge(sems["s"], _s)
            ctx.sy(syd)
            sload(ctx, out_dram[0:128, 0:16], qcols[:, :])
            ctx.barrier()
            return
        for tb in range(8):
            t0 = tb * CH
            gw_consumers(te=True, v=True, s=True)
            gload(ctx, wk["normed"][:, :].rearrange("p (c t) -> p c t", c=8), stream_dram[:, t0:t0 + CH].rearrange("(c p) t -> p c t", p=128))
            gw = ctx.n["gd"] * 16
            for oc in range(8):
                def vqx(v, oc=oc, gw=gw):
                    v.wait_ge(sems["gd"], gw)
                    v.tensor_scalar(vq1[:, 0:CH], wk["normed"][:, oc * CH:(oc + 1) * CH],
                                    qcols[:, 9:10], None, OP.mult)
                    v.tensor_scalar(vq2[:, 0:CH], vq1[:, 0:CH], misc[:, 5:6], None, OP.mult)
                    v.tensor_tensor(knf2[:, :], vq2[:, 0:CH], vq1[:, 0:CH], OP.subtract)
                    mark(ctx, v.tensor_tensor(xq_sb[:, oc * CH:(oc + 1) * CH], vq2[:, 0:CH],
                                              knf2[:, :], OP.subtract), "v")
                ctx.v(vqx)
            vwq = ctx.n["v"]
            for oc in range(8):
                vprev = ctx.n["v"]
                def tfq(te, oc=oc, vwq=vwq, vprev=vprev):
                    te.wait_ge(sems["v"], max(vwq, vprev))
                    for kc in range(8):
                        inst = te.matmul(ps_mm[oc % 2][:, :],
                                         wbig[:, kc * 1024 + oc * 128:kc * 1024 + (oc + 1) * 128],
                                         xq_sb[:, kc * CH:(kc + 1) * CH],
                                         start=(kc == 0), stop=(kc == 7))
                    mark(ctx, inst, "te")
                ctx.te(tfq)
                tw = ctx.n["te"]
                def vev(v, oc=oc, tw=tw):
                    v.wait_ge(sems["te"], tw)
                    v.tensor_scalar(wk["sp1"][:, 0:CH], ps_mm[oc % 2][:, :],
                                    qcols[:, 10:11], vecs["fqb"][:, oc:oc + 1], OP.mult, OP.add)
                    mark(ctx, v.tensor_tensor(wk["mix"][:, oc * CH:(oc + 1) * CH],
                                              wk["sp1"][:, 0:CH],
                                              wk["normed"][:, oc * CH:(oc + 1) * CH], OP.add), "v")
                ctx.v(vev)
            mix_tiles = [wk["mix"][:, oc * CH:(oc + 1) * CH] for oc in range(8)]
            emit_norm_stats(ctx, lnp, mix_tiles, CH, ps_mm[0], ps_mm[1], C, rms=False, vwait=ctx.n["v"])
            swn = ctx.n["s"]
            for oc in range(8):
                sdp = ctx.n["sd"] * 16
                def vapf(v, oc=oc, swn=swn, sdp=sdp):
                    v.wait_ge(sems["s"], swn)
                    if oc >= 2:
                        v.wait_ge(sems["sd"], sdp)
                    v.tensor_tensor(wk["sp1"][:, 0:CH], wk["mix"][:, oc * CH:(oc + 1) * CH],
                                    lnp.mu_b[:, 0:CH], OP.subtract)
                    mark(ctx, v.tensor_tensor(fout[:, (oc % 2) * CH:(oc % 2 + 1) * CH],
                                              wk["sp1"][:, 0:CH], lnp.istd_b[:, 0:CH], OP.mult), "v")
                ctx.v(vapf)
                vwf = ctx.n["v"]
                def syo(sy, vwf=vwf):
                    sy.wait_ge(sems["v"], vwf)
                ctx.sy(syo)
                sload(ctx, out_dram[oc * 128:(oc + 1) * 128, t0:t0 + CH],
                      fout[:, (oc % 2) * CH:(oc % 2 + 1) * CH])
            ctx.barrier()

    def dump_stream(src_dram_ap):
        ctx.barrier()
        def gdump(g):
            mark(ctx, g.dma_start(out=out_dram, in_=src_dram_ap), "gd")
        ctx.g(gdump)
        ctx.barrier()

    if dbg_stage == "l0a":
        dump_stream(res_dram)
    else:
        ctx.barrier()
        with nc.named_scope("ffn"):
            emit_ffn()
        if dbg_stage == "ffv":
            dump_stream(v_dram[0:D, :])
        elif dbg_stage == "ff2":
            dump_stream(stream_dram)
        elif dbg_stage == "ffn":
            dump_stream(stream_dram)
        else:
            ctx.push_scope()
            with nc.named_scope("moe"):
                emit_moe()
            ctx.pop_scope()
            if dbg_stage in ("moe", "moeg", "moer"):
                if dbg_stage == "moe":
                    dump_stream(stream_dram)
            else:
                ctx.push_scope()
                with nc.named_scope("l1a"):
                    emit_l1a()
                ctx.pop_scope()
                if dbg_stage in ("l1a", "l1q", "l1y"):
                    dump_stream(stream_dram if dbg_stage == "l1a" else v_dram[0:D, :])
                elif dbg_stage in (None, "l1fs"):
                    ctx.push_scope()
                    with nc.named_scope("l1f"):
                        emit_l1f()
                    ctx.pop_scope()
    ctx.barrier()
    return ctx



def kernel(**inputs):
    shards = stage_inputs(inputs)
    ctx = build_program(dbg_stage=DBG_STAGE[0])
    ctx.stack.close()
    res = run_bass_kernel_spmd(ctx.nc, shards, core_ids=list(range(8)))
    out = np.zeros((B, T, D), np.float32)
    for c in range(8):
        b, half = c // 2, c % 2
        out[b, half * TL:(half + 1) * TL, :] = res.results[c]["out"].T
    return out


# version 07
# speedup vs baseline: 1.2139x; 1.1820x over previous
import math
import contextlib
import numpy as np
import ml_dtypes
import concourse.bass as bass
import concourse.mybir as mybir
from concourse.bass_utils import run_bass_kernel_spmd
from concourse.alu_op_type import AluOpType as OP
from bass_rust import ReduceOp

F32 = mybir.dt.float32
BF16 = mybir.dt.bfloat16
I32 = mybir.dt.int32
AF = mybir.ActivationFunctionType
AX = mybir.AxisListType
BF = ml_dtypes.bfloat16

B, T, D = 4, 8192, 1024
TL = 4096
CH = 512
NCH = TL // CH
NG = 2 * NCH
DI = 2048
DH = DI // 2
DS = 16
DCV = 4
DTR = 64
FF = 4096
NH, HL = 16, 64
NE, NSH = 16, 2
EPS = 1e-5
VELT_C = float(2 ** 20 + 1)
FP8_MAX = 448.0

DBG_STAGE = [None]



def _t(a):
    return np.ascontiguousarray(np.asarray(a).T)


def _cols(vec, p=128):
    v = np.asarray(vec, np.float32).reshape(-1, p)
    return np.ascontiguousarray(v.T)


def stage_inputs(inputs):
    x = np.asarray(inputs["L0_attn_x"])
    shards = []
    shared = {}
    for li in range(2):
        shared[f"m{li}_inprojT"] = _t(inputs["L0_attn_in_proj_w"][li])
        shared[f"m{li}_convw"] = _cols(np.asarray(inputs["L0_attn_conv_w"][li]).T.reshape(-1))
        shared[f"m{li}_convb"] = _cols(inputs["L0_attn_conv_b"][li])
        shared[f"m{li}_xpT"] = _t(inputs["L0_attn_x_proj_w"][li])
        shared[f"m{li}_dtT"] = _t(inputs["L0_attn_dt_proj_w"][li])
        shared[f"m{li}_dtb"] = _cols(inputs["L0_attn_dt_proj_b"][li])
        shared[f"m{li}_outT"] = _t(inputs["L0_attn_out_proj_w"][li])
        shared[f"m{li}_D"] = _cols(inputs["L0_attn_D"][li])
        shared[f"m{li}_rmsw"] = _cols(inputs["L0_attn_rms_w"][li])
        shared[f"m{li}_lnw"] = _cols(inputs["L0_attn_ln_w"][li])
        shared[f"m{li}_lnb"] = _cols(inputs["L0_attn_ln_b"][li])
    for li in range(2):
        cw = np.asarray(inputs["L0_attn_conv_w"][li], np.float32)
        shared[f"m{li}_convw"] = np.ascontiguousarray(
            cw.reshape(16, 128, 4).transpose(1, 0, 2).reshape(128, 64))
    shared["ff1T"] = _t(inputs["L0_attn_ff_w1"])
    shared["ffb1"] = _cols(inputs["L0_attn_ff_b1"])
    shared["ff2T"] = _t(inputs["L0_attn_ff_w2"])
    shared["ffb2"] = _cols(inputs["L0_attn_ff_b2"])
    shared["centT"] = _t(inputs["L0_ffn_centroids"])
    shared["eb_row"] = np.ascontiguousarray(
        np.broadcast_to(np.asarray(inputs["L0_ffn_expert_biases"], np.float32)[None, :], (128, NE)))
    shared["WrT"] = np.ascontiguousarray(
        np.asarray(inputs["L0_ffn_Wr"], np.float32).transpose(0, 2, 1))
    shared["br"] = np.asarray(inputs["L0_ffn_br"], np.float32)
    shared["WsT"] = np.ascontiguousarray(
        np.asarray(inputs["L0_ffn_Ws"], np.float32).transpose(0, 2, 1))
    shared["bs"] = _cols(np.asarray(inputs["L0_ffn_bs"], np.float32).reshape(-1))
    shared["qkvT"] = _t(inputs["L1_attn_w"])
    shared["woutT"] = np.ascontiguousarray(np.asarray(inputs["L1_attn_w_out"], np.float32))
    shared["fqT"] = _t(inputs["L1_ffn_weight"])
    shared["fqb"] = _cols(inputs["L1_ffn_bias"])

    ident = np.eye(128, dtype=np.float32)
    consts = {
        "identity": ident,
        "ones1": np.ones((1, 128), np.float32),
        "cmask": np.triu(np.ones((128, 128), np.float32)),
        "selB": np.stack([np.eye(32, dtype=np.float32)[n].repeat(1) for n in range(16)], 0),
    }
    selB = np.zeros((16, 32, 128), np.float32)
    selC = np.zeros((16, 32, 128), np.float32)
    for n in range(16):
        selB[n, n, :] = 1.0
        selC[n, 16 + n, :] = 1.0
    consts["selB"] = np.ascontiguousarray(selB.transpose(1, 0, 2).reshape(32, 16 * 128))
    consts["selC"] = np.ascontiguousarray(selC.transpose(1, 0, 2).reshape(32, 16 * 128))
    selH = np.zeros((8, 128, 16), np.float32)
    for c in range(8):
        for j in range(2):
            selH[c, j * 64:(j + 1) * 64, 2 * c + j] = 1.0
    consts["selH"] = np.ascontiguousarray(selH.reshape(8 * 128, 16))
    selHB = np.zeros((8, 16, 128), np.float32)
    for c in range(8):
        for j in range(2):
            selHB[c, 2 * c + j, j * 64:(j + 1) * 64] = 1.0
    consts["selHB"] = np.ascontiguousarray(selHB.transpose(1, 0, 2).reshape(16, 8 * 128))
    sel16 = np.zeros((16, 16, 128), np.float32)
    for e in range(16):
        sel16[e, e, :] = 1.0
    consts["sel16"] = np.ascontiguousarray(sel16.transpose(1, 0, 2).reshape(16, 16 * 128))
    misc = np.zeros((128, 12), np.float32)
    misc[:, 0] = 1.0
    misc[:, 1] = EPS
    misc[:, 2] = 1.0 / 1024.0
    misc[:, 3] = 0.5
    misc[:, 4] = 1e-6
    misc[:, 5] = VELT_C
    misc[:, 6] = FP8_MAX
    misc[:, 7] = 1e-12
    misc[:, 8] = 1.0 / (FP8_MAX * FP8_MAX)
    consts["misc"] = misc

    for c in range(8):
        b, half = c // 2, c % 2
        sh = {}
        sh.update(shared)
        sh.update(consts)
        xs = x[b, half * TL:(half + 1) * TL, :]
        sh["x"] = _t(xs)
        f = float(half)
        sh["flag"] = np.full((128, 1), f, np.float32)
        sh["sel_lo"] = ident * (1.0 - f)
        sh["sel_hi"] = ident * f
        for li in range(2):
            al = np.asarray(inputs["L0_attn_A_log"][li], np.float32)[half * DH:(half + 1) * DH]
            sh[f"m{li}_Alog"] = np.ascontiguousarray(
                al.reshape(8, 128, 16).transpose(1, 0, 2).reshape(128, 128))
        shards.append(sh)
    return shards



class Ctx:
    def __init__(self):
        self.nc = bass.Bass("TRN2", target_bir_lowering=False)
        self.stack = contextlib.ExitStack()
        self.sb_free = {}
        self.dram = {}
        self.n = dict(te=0, v=0, s=0, g=0, gd=0, sd=0)
        self.scopes = []
        self.block = None
        self.sem = {}

    def dram_in(self, name, shape, dtype=F32):
        ap = self.nc.dram_tensor(name, list(shape), dtype, kind="ExternalInput").ap()
        self.dram[name] = ap
        return ap

    def dram_out(self, name, shape, dtype=F32):
        ap = self.nc.dram_tensor(name, list(shape), dtype, kind="ExternalOutput").ap()
        self.dram[name] = ap
        return ap

    def dram_tmp(self, name, shape, dtype=BF16):
        ap = self.nc.dram_tensor(name, list(shape), dtype, kind="Internal").ap()
        self.dram[name] = ap
        return ap

    def sbuf(self, name, shape, dtype=BF16):
        st = self.scopes[-1] if self.scopes else self.stack
        h = st.enter_context(self.nc.sbuf_tensor(name, list(shape), dtype))
        self.sb_free[name] = h
        return h

    def push_scope(self):
        self.scopes.append(contextlib.ExitStack())

    def pop_scope(self):
        self.scopes.pop().close()

    def psum(self, name, shape, dtype=F32):
        h = self.stack.enter_context(self.nc.psum_tensor(name, list(shape), dtype))
        return h

    def semaphore(self, name):
        s = self.stack.enter_context(self.nc.semaphore(name))
        self.sem[name] = s
        return s

    def te(self, f):
        self.block.tensor(f)

    def v(self, f):
        self.block.vector(f)

    def s(self, f):
        self.block.scalar(f)

    def g(self, f):
        self.block.gpsimd(f)

    def sy(self, f):
        self.block.sync(f)

    def barrier(self):
        n = dict(self.n)
        sems = self.sem

        def mk(eng_names_sem):
            ncc = self.n.get("cc", 0)
            def f(e):
                for key, scale in [("te", 1), ("v", 1), ("s", 1), ("g", 1), ("gd", 16), ("sd", 16)]:
                    if n[key] > 0:
                        e.wait_ge(sems[key], n[key] * scale)
                if ncc > 0:
                    e.wait_ge(sems["cc"], ncc)
            return f
        self.te(mk(None))
        self.v(mk(None))
        self.s(mk(None))
        self.g(mk(None))
        self.sy(mk(None))

    def inc(self, key, k=1):
        self.n[key] += k
        return self.n[key] * (16 if key in ("gd", "sd") else 1)





def mark(ctx, inst, key):
    inst.then_inc(ctx.sem[key], 16 if key in ("gd", "sd") else 1)
    return ctx.inc(key)


def gload(ctx, dst_ap, src_ap):
    out = {}
    def f(g):
        out["w"] = mark(ctx, g.dma_start(out=dst_ap, in_=src_ap), "gd")
    ctx.g(f)
    return out["w"]


def sload(ctx, dst_ap, src_ap):
    out = {}
    def f(sy):
        out["w"] = mark(ctx, sy.dma_start(out=dst_ap, in_=src_ap), "sd")
    ctx.sy(f)
    return out["w"]


def collective(ctx, kind, op, groups, in_ap, out_ap):
    sems = ctx.sem
    prev = ctx.n.get("cc", 0)
    def f(g):
        if prev:
            g.wait_ge(sems["cc"], prev)
        g.collective_compute(kind, op, replica_groups=groups, ins=[in_ap], outs=[out_ap]
                             ).then_inc(sems["cc"], 1)
    ctx.g(f)
    ctx.n["cc"] = prev + 1
    return ctx.n["cc"]


PAIRS = [[0, 1], [2, 3], [4, 5], [6, 7]]
ALL8 = [list(range(8))]


class LNParams:
    def __init__(self, ctx, tname):
        self.sq = ctx.sbuf(tname + "_sq", [128, 2 * CH], BF16)
        self.stat = ctx.sbuf(tname + "_st", [1, 4 * CH], F32)
        self.statb = ctx.sbuf(tname + "_stb", [1, 2 * CH], BF16)
        self.mu_b = ctx.sbuf(tname + "_mub", [128, CH], BF16)
        self.istd_b = ctx.sbuf(tname + "_isb", [128, CH], BF16)


def emit_norm_stats(ctx, lnp, src_tiles, t, ps_a, ps_b, consts, rms=False, vwait=0):
    sems = ctx.sem
    n = len(src_tiles)
    for i, ap in enumerate(src_tiles):
        tprev = ctx.n["te"]
        def s1(s, i=i, ap=ap, tprev=tprev):
            if i == 0 and vwait:
                s.wait_ge(sems["v"], vwait)
            if i >= 2:
                s.wait_ge(sems["te"], tprev)
            mark(ctx, s.activation(lnp.sq[:, (i % 2) * CH:(i % 2) * CH + t], ap, AF.Square), "s")
        ctx.s(s1)
        sw = ctx.n["s"]
        def t1(te, i=i, ap=ap, sw=sw):
            te.wait_ge(sems["s"], sw)
            if i == 0 and vwait:
                te.wait_ge(sems["v"], vwait)
            if not rms:
                te.matmul(ps_a[0:1, 0:t], consts["ones_col"][:, :], ap,
                          start=(i == 0), stop=(i == n - 1))
            mark(ctx, te.matmul(ps_b[0:1, 0:t], consts["ones_col"][:, :],
                                lnp.sq[:, (i % 2) * CH:(i % 2) * CH + t],
                                start=(i == 0), stop=(i == n - 1)), "te")
        ctx.te(t1)
    tw = ctx.n["te"]
    misc = consts["misc"]
    def v1(v):
        v.wait_ge(sems["te"], tw)
        v.tensor_scalar(lnp.stat[0:1, 0:t], ps_b[0:1, 0:t], misc[0:1, 2:3], None, OP.mult)
        if not rms:
            v.tensor_scalar(lnp.stat[0:1, CH:CH + t], ps_a[0:1, 0:t], misc[0:1, 2:3], None, OP.mult)
            v.tensor_tensor(lnp.stat[0:1, 2 * CH:2 * CH + t], lnp.stat[0:1, CH:CH + t],
                            lnp.stat[0:1, CH:CH + t], OP.mult)
            v.tensor_tensor(lnp.stat[0:1, 0:t], lnp.stat[0:1, 0:t],
                            lnp.stat[0:1, 2 * CH:2 * CH + t], OP.subtract)
            inst = v.tensor_copy(lnp.statb[0:1, 0:t], lnp.stat[0:1, CH:CH + t])
        else:
            inst = v.engine_nop()
        mark(ctx, inst, "v")
    ctx.v(v1)
    vw = ctx.n["v"]
    def s2(s):
        s.wait_ge(sems["v"], vw)
        s.activation(lnp.stat[0:1, 2 * CH:2 * CH + t], lnp.stat[0:1, 0:t], AF.Ln,
                     bias=misc[0:1, 1:2])
        inst = s.activation(lnp.stat[0:1, 3 * CH:3 * CH + t],
                            lnp.stat[0:1, 2 * CH:2 * CH + t], AF.Exp, scale=-0.5)
        mark(ctx, inst, "s")
    ctx.s(s2)
    sw2 = ctx.n["s"]
    def v2(v):
        v.wait_ge(sems["s"], sw2)
        inst = v.tensor_copy(lnp.statb[0:1, CH:CH + t], lnp.stat[0:1, 3 * CH:3 * CH + t])
        mark(ctx, inst, "v")
    ctx.v(v2)
    vw2 = ctx.n["v"]
    def t2(te):
        te.wait_ge(sems["v"], vw2)
        if not rms:
            mark(ctx, te.matmul(ps_a[:, 0:t], consts["ones_bf"][:, :], lnp.statb[0:1, 0:t],
                                start=True, stop=True), "te")
        mark(ctx, te.matmul(ps_b[:, 0:t], consts["ones_bf"][:, :], lnp.statb[0:1, CH:CH + t],
                            start=True, stop=True), "te")
    ctx.te(t2)
    tw2 = ctx.n["te"]
    def s3(s):
        s.wait_ge(sems["te"], tw2)
        if not rms:
            s.copy(lnp.mu_b[:, 0:t], ps_a[:, 0:t])
        inst = s.copy(lnp.istd_b[:, 0:t], ps_b[:, 0:t])
        mark(ctx, inst, "s")
    ctx.s(s3)
    return ctx.n["s"]



def build_program(dbg_stage=None):
    ctx = Ctx()
    nc = ctx.nc
    sems = ctx.sem

    din = {}
    din["x"] = ctx.dram_in("x", [D, TL])
    din["flag"] = ctx.dram_in("flag", [128, 1])
    din["sel_lo"] = ctx.dram_in("sel_lo", [128, 128])
    din["sel_hi"] = ctx.dram_in("sel_hi", [128, 128])
    din["identity"] = ctx.dram_in("identity", [128, 128])
    din["ones1"] = ctx.dram_in("ones1", [1, 128])
    din["cmask"] = ctx.dram_in("cmask", [128, 128])
    din["selB"] = ctx.dram_in("selB", [32, 16 * 128])
    din["selC"] = ctx.dram_in("selC", [32, 16 * 128])
    din["selH"] = ctx.dram_in("selH", [8 * 128, 16])
    din["selHB"] = ctx.dram_in("selHB", [16, 8 * 128])
    din["misc"] = ctx.dram_in("misc", [128, 12])
    din["sel16"] = ctx.dram_in("sel16", [16, 16 * 128])
    for li in range(2):
        din[f"m{li}_inprojT"] = ctx.dram_in(f"m{li}_inprojT", [D, 2 * DI])
        din[f"m{li}_convw"] = ctx.dram_in(f"m{li}_convw", [128, 64])
        din[f"m{li}_convb"] = ctx.dram_in(f"m{li}_convb", [128, 16])
        din[f"m{li}_xpT"] = ctx.dram_in(f"m{li}_xpT", [DI, 96])
        din[f"m{li}_dtT"] = ctx.dram_in(f"m{li}_dtT", [DTR, DI])
        din[f"m{li}_dtb"] = ctx.dram_in(f"m{li}_dtb", [128, 16])
        din[f"m{li}_outT"] = ctx.dram_in(f"m{li}_outT", [DI, D])
        din[f"m{li}_D"] = ctx.dram_in(f"m{li}_D", [128, 16])
        din[f"m{li}_rmsw"] = ctx.dram_in(f"m{li}_rmsw", [128, 8])
        din[f"m{li}_lnw"] = ctx.dram_in(f"m{li}_lnw", [128, 8])
        din[f"m{li}_lnb"] = ctx.dram_in(f"m{li}_lnb", [128, 8])
        din[f"m{li}_Alog"] = ctx.dram_in(f"m{li}_Alog", [128, 128])
    din["ff1T"] = ctx.dram_in("ff1T", [D, 2 * FF])
    din["ffb1"] = ctx.dram_in("ffb1", [128, 64])
    din["ff2T"] = ctx.dram_in("ff2T", [FF, D])
    din["ffb2"] = ctx.dram_in("ffb2", [128, 8])
    din["centT"] = ctx.dram_in("centT", [D, NE])
    din["eb_row"] = ctx.dram_in("eb_row", [128, NE])
    din["WrT"] = ctx.dram_in("WrT", [NE, D, D])
    din["br"] = ctx.dram_in("br", [NE, D])
    din["WsT"] = ctx.dram_in("WsT", [NSH, D, D])
    din["bs"] = ctx.dram_in("bs", [128, 16])
    din["qkvT"] = ctx.dram_in("qkvT", [D, 3 * D])
    din["woutT"] = ctx.dram_in("woutT", [D, D])
    din["fqT"] = ctx.dram_in("fqT", [D, D])
    din["fqb"] = ctx.dram_in("fqb", [128, 8])
    out_dram = ctx.dram_out("out", [D, TL], F32)

    bdl_in = [ctx.dram_tmp(f"bdl_in{i}", [2 * DI + 32, CH]) for i in range(2)]
    bdl_out = [ctx.dram_tmp(f"bdl_out{k}", [2 * (2 * DI + 32), CH]) for k in range(NCH)]
    yag_in = [ctx.dram_tmp(f"yag_in{i}", [DH, CH]) for i in range(2)]
    yag_out = [ctx.dram_tmp(f"yag_out{g}", [DI, CH]) for g in range(NG)]
    tail_in = ctx.dram_tmp("tail_in", [D, 4])
    tail_out = ctx.dram_tmp("tail_out", [2 * D, 4])
    z_dram = ctx.dram_tmp("z_dram", [DI, TL])
    res_dram = ctx.dram_tmp("res_dram", [D, TL])
    stream_dram = ctx.dram_tmp("stream_dram", [D, TL])
    routed_dram = ctx.dram_tmp("routed_dram", [D, TL], F32)
    xc_dram = ctx.dram_tmp("xc_dram", [DI, TL])
    v_dram = ctx.dram_tmp("v_dram", [FF, TL])
    kexp_dram = ctx.dram_tmp("kexp_dram", [D, TL])
    p_dram = ctx.dram_tmp("p_dram", [D, TL])
    vtm_dram = ctx.dram_tmp("vtm_dram", [TL, D])
    sag_in = ctx.dram_tmp("sag_in", [128, 520], F32)
    sag_out = ctx.dram_tmp("sag_out", [256, 520], F32)
    amax_in = ctx.dram_tmp("amax_in", [1, 8], F32)
    amax_out = ctx.dram_tmp("amax_out", [1, 8], F32)

    C = {}
    C["ident_bf"] = ctx.sbuf("ident_bf", [128, 128], BF16)
    C["ident_f32"] = ctx.sbuf("ident_f32", [128, 128], F32)
    C["ones_f32"] = ctx.sbuf("ones_f32", [1, 128], F32)
    C["ones_bf"] = ctx.sbuf("ones_bf", [1, 128], BF16)
    C["ones_col"] = ctx.sbuf("ones_col", [128, 1], BF16)
    C["cmask"] = ctx.sbuf("cmask_sb", [128, 128], BF16)
    C["selB"] = ctx.sbuf("selB_sb", [32, 16 * 128], BF16)
    C["selC"] = ctx.sbuf("selC_sb", [32, 16 * 128], BF16)
    C["selH"] = ctx.sbuf("selH_sb", [128, 8 * 16], BF16)
    C["selHB"] = ctx.sbuf("selHB_sb", [16, 8 * 128], BF16)
    C["sel_lo"] = ctx.sbuf("sel_lo_sb", [128, 128], BF16)
    C["sel_hi"] = ctx.sbuf("sel_hi_sb", [128, 128], BF16)
    C["sel16"] = ctx.sbuf("sel16_sb", [16, 16 * 128], BF16)
    C["misc"] = ctx.sbuf("misc_sb", [128, 12], F32)
    C["flag"] = ctx.sbuf("flag_sb", [128, 1], F32)
    C["flag_bf"] = ctx.sbuf("flag_bf", [128, 128], BF16)
    vecs = {}
    for li in range(2):
        for nm, w in [("convw", 64), ("convb", 16), ("dtb", 16), ("D", 16),
                      ("rmsw", 8), ("lnw", 8), ("lnb", 8), ("Alog", 128)]:
            vecs[f"m{li}_{nm}"] = ctx.sbuf(f"m{li}_{nm}_sb", [128, w], F32)
        vecs[f"m{li}_A"] = ctx.sbuf(f"m{li}_A_sb", [128, 128], F32)
    for nm, w in [("ffb1", 64), ("ffb2", 8), ("bs", 16), ("fqb", 8)]:
        vecs[nm] = ctx.sbuf(nm + "_sb", [128, w], F32)
    vecs["eb_row"] = ctx.sbuf("eb_row_sb", [128, NE], BF16)
    vecs["br"] = ctx.sbuf("br_sb", [NE, D], BF16)

    wbig = ctx.sbuf("wbig", [128, 16384], BF16)
    state = ctx.sbuf("state", [128, 128], F32)
    wk = {}
    wk["normed"] = ctx.sbuf("normed", [128, 8 * CH], BF16)
    wk["sp1"] = ctx.sbuf("sp1", [128, 3 * CH], BF16)
    wk["gat"] = ctx.sbuf("gat", [128, 16 * CH], BF16)
    wk["mix"] = ctx.sbuf("mix", [128, 8 * CH], BF16)
    wk["srcx"] = ctx.sbuf("srcx", [128, 8 * CH], BF16)
    wk["sz"] = ctx.sbuf("szt", [128, 2 * CH], BF16)
    lnp = LNParams(ctx, "ln")

    def alloc_mamba():
        wk["x_in"] = ctx.sbuf("x_in", [128, 3 * (CH + 4)], BF16)
        wk["halo"] = ctx.sbuf("halo", [128, 16 * 4], BF16)
        wk["x_c"] = ctx.sbuf("x_c", [128, 2 * CH], BF16)
        wk["delta"] = ctx.sbuf("delta", [128, 2 * CH], BF16)
        wk["xdbl"] = ctx.sbuf("xdbl", [96, CH], BF16)
        wk["bc"] = ctx.sbuf("bc", [32, CH], BF16)
        wk["Bb"] = ctx.sbuf("Bb", [128, 16 * CH], BF16)
        wk["Cb"] = ctx.sbuf("Cb", [128, 16 * CH], BF16)
        wk["dfeed"] = ctx.sbuf("dfeed", [128, 2 * CH], BF16)
        wk["xfeed"] = ctx.sbuf("xfeed", [128, 2 * CH], BF16)
        wk["pair"] = ctx.sbuf("pair", [128, 8 * CH], BF16)
        wk["dA"] = ctx.sbuf("dA", [128, 2 * CH], BF16)
        wk["in1"] = ctx.sbuf("in1", [128, 2 * CH], BF16)
        wk["hbf"] = ctx.sbuf("hbf", [128, 2 * CH], BF16)
        wk["wv"] = ctx.sbuf("wv", [128, 2 * CH], BF16)
        wk["du"] = ctx.sbuf("du", [128, CH], BF16)
        wk["ymine"] = ctx.sbuf("ymine", [128, 2 * CH], BF16)
        wk["diag"] = ctx.sbuf("diag", [128, 4 * 128], BF16)
        wk["tailh"] = ctx.sbuf("tailh", [128, 8 * 4 * 4], BF16)
        wk["yfa"] = ctx.sbuf("yfa", [128, 4 * CH], BF16)
        wk["yfb"] = ctx.sbuf("yfb", [128, 4 * CH], BF16)
        wk["wxp"] = ctx.sbuf("wxp", [128, 16 * 96], BF16)
        wk["wdt"] = ctx.sbuf("wdt", [64, DI], BF16)
        wk["hbig"] = ctx.sbuf("hbig", [128, 4 * CH], F32)

    ps_mm = [ctx.psum(f"ps_mm{i}", [128, CH], F32) for i in range(2)]
    ps_y = [ctx.psum(f"ps_y{i}", [128, CH], F32) for i in range(2)]
    ps_bl = [ctx.psum(f"ps_bl{i}", [128, CH], F32) for i in range(2)]
    ps_x = ctx.psum("ps_x", [128, CH], F32)
    ps_tp = ctx.psum("ps_tp", [128, 128], BF16)

    for s in ["te", "v", "s", "g", "gd", "sd", "cc"]:
        ctx.semaphore(s)

    blk = ctx.stack.enter_context(nc.Block())
    ctx.block = blk

    gload(ctx, C["ident_bf"][:, :], din["identity"])
    sload(ctx, C["ident_f32"][:, :], din["identity"])
    sload(ctx, C["ones_f32"][:, :], din["ones1"])
    gload(ctx, C["ones_bf"][:, :], din["ones1"])
    gload(ctx, C["cmask"][:, :], din["cmask"])
    gload(ctx, C["selB"][:, :], din["selB"])
    gload(ctx, C["selC"][:, :], din["selC"])
    gload(ctx, C["selHB"][:, :], din["selHB"])
    gload(ctx, C["sel_lo"][:, :], din["sel_lo"])
    gload(ctx, C["sel_hi"][:, :], din["sel_hi"])
    gload(ctx, vecs["eb_row"][:, :], din["eb_row"])
    gload(ctx, C["misc"][:, :], din["misc"])
    gload(ctx, C["sel16"][:, :], din["sel16"])
    gload(ctx, C["flag"][:, :], din["flag"])
    gload(ctx, C["selH"][:, :].rearrange("p (c h) -> p c h", c=8), din["selH"].rearrange("(c p) h -> p c h", p=128))
    gload(ctx, vecs["br"][:, :], din["br"])
    for li in range(2):
        for nm in ["convw", "convb", "dtb", "D", "rmsw", "lnw", "lnb", "Alog"]:
            sload(ctx, vecs[f"m{li}_{nm}"][:, :], din[f"m{li}_{nm}"])
    for nm in ["ffb1", "ffb2", "bs", "fqb"]:
        sload(ctx, vecs[nm][:, :], din[nm])
    gw0, sw0 = ctx.n["gd"] * 16, ctx.n["sd"] * 16
    def v0(v):
        v.wait_ge(sems["gd"], gw0)
        v.wait_ge(sems["sd"], sw0)
        inst = v.tensor_copy(C["ones_col"][:, :], C["misc"][:, 0:1])
        inst = v.tensor_scalar(C["flag_bf"][:, :], C["ident_bf"][:, :], C["flag"][:, 0:1], None, OP.mult)
        mark(ctx, inst, "v")
    ctx.v(v0)
    def s0(s):
        s.wait_ge(sems["sd"], sw0)
        for li in range(2):
            inst = s.activation(vecs[f"m{li}_A"][:, :], vecs[f"m{li}_Alog"][:, :], AF.Exp)
        mark(ctx, inst, "s")
    ctx.s(s0)
    def v0b(v):
        v.wait_ge(sems["s"], ctx.n["s"])
        for li in range(2):
            inst = v.tensor_scalar(vecs[f"m{li}_A"][:, :], vecs[f"m{li}_A"][:, :], -1.0, None, OP.mult)
        mark(ctx, inst, "v")
    ctx.v(v0b)
    ctx.barrier()

    def wait_all(e):
        for key in ("te", "v", "s", "g"):
            if ctx.n[key] > 0:
                e.wait_ge(sems[key], ctx.n[key])
        for key in ("gd", "sd"):
            if ctx.n[key] > 0:
                e.wait_ge(sems[key], ctx.n[key] * 16)



    def colmax_bcast(src_col, dst_col):
        vprev = ctx.n["v"]
        def t1(te, vprev=vprev):
            if vprev:
                te.wait_ge(sems["v"], vprev)
            mark(ctx, te.transpose(ps_x[0:1, 0:128], src_col, C["ident_f32"][:, :]), "te")
        ctx.te(t1)
        tw = ctx.n["te"]
        def v1(v, tw=tw):
            v.wait_ge(sems["te"], tw)
            mark(ctx, v.reduce_max(lnp.stat[0:1, 0:1], ps_x[0:1, 0:128], axis=AX.X), "v")
        ctx.v(v1)
        vw = ctx.n["v"]
        def t2(te, vw=vw):
            te.wait_ge(sems["v"], vw)
            mark(ctx, te.matmul(ps_x[:, 0:1], C["ones_f32"][:, :], lnp.stat[0:1, 0:1],
                                start=True, stop=True), "te")
        ctx.te(t2)
        tw2 = ctx.n["te"]
        def v2(v, tw2=tw2):
            v.wait_ge(sems["te"], tw2)
            mark(ctx, v.tensor_copy(dst_col, ps_x[:, 0:1]), "v")
        ctx.v(v2)

    def gw_consumers(te=True, v=False, s=False):
        nte, nv, ns = ctx.n["te"], ctx.n["v"], ctx.n["s"]
        def f(g):
            if te and nte:
                g.wait_ge(sems["te"], nte)
            if v and nv:
                g.wait_ge(sems["v"], nv)
            if s and ns:
                g.wait_ge(sems["s"], ns)
        ctx.g(f)

    XW = CH + 4

    agof = {}
    yagof = {}
    trk = {"vrms": 0, "tz": 0, "tbc": 0, "vfin": 0}

    def emit_mamba_layer(li):
        Acol = vecs[f"m{li}_A"]
        convw = vecs[f"m{li}_convw"]
        convb = vecs[f"m{li}_convb"]
        dtb = vecs[f"m{li}_dtb"]
        Dcol = vecs[f"m{li}_D"]
        rmsw = vecs[f"m{li}_rmsw"]
        lnw = vecs[f"m{li}_lnw"]
        lnb = vecs[f"m{li}_lnb"]
        misc = C["misc"]
        src_dram = din["x"] if li == 0 else res_dram

        gload(ctx, wk["wxp"][:, :].rearrange("p (c r) -> p c r", c=16), din[f"m{li}_xpT"].rearrange("(c p) r -> p c r", p=128))
        gload(ctx, wk["wdt"][:, :], din[f"m{li}_dtT"])
        def gz(g):
            mark(ctx, g.memset(state[:, :], 0.0), "g")
        ctx.g(gz)
        ctx.barrier()

        gload(ctx, wk["tailh"][:, 0:32].rearrange("p (c t) -> p c t", c=8), src_dram[:, TL - 4:TL].rearrange("(c p) t -> p c t", p=128))
        ctx.barrier()
        sload(ctx, tail_in.rearrange("(c p) t -> p c t", p=128), wk["tailh"][:, 0:32].rearrange("p (c t) -> p c t", c=8))
        def gw1(g):
            g.wait_ge(sems["sd"], ctx.n["sd"] * 16)
        ctx.g(gw1)
        collective(ctx, "AllGather", OP.bypass, PAIRS, tail_in, tail_out)
        ccw = ctx.n["cc"]
        def gt2(g, ccw=ccw):
            g.wait_ge(sems["cc"], ccw)
            mark(ctx, g.dma_start(out=wk["tailh"][:, 32:64].rearrange("p (c t) -> p c t", c=8),
                                  in_=tail_out[0:D, :].rearrange("(c p) t -> p c t", p=128)), "gd")
        ctx.g(gt2)
        ctx.barrier()
        tsl = [wk["tailh"][:, 32 + fc * 4:32 + (fc + 1) * 4] for fc in range(8)]
        emit_norm_stats(ctx, lnp, tsl, 4, ps_mm[0], ps_mm[1], C, rms=True)
        swt = ctx.n["s"]
        def vt2(v, swt=swt):
            v.wait_ge(sems["s"], swt)
            for fc in range(8):
                inst = v.scalar_tensor_tensor(wk["tailh"][:, 64 + fc * 4:64 + (fc + 1) * 4],
                                              tsl[fc], rmsw[:, fc:fc + 1],
                                              lnp.istd_b[:, 0:4], OP.mult, OP.mult)
            mark(ctx, inst, "v")
        ctx.v(vt2)
        vwt = ctx.n["v"]
        ctx.barrier()
        for kc in range(8):
            gload(ctx, wbig[:, kc * 2048:(kc + 1) * 2048],
                  din[f"m{li}_inprojT"][kc * 128:(kc + 1) * 128, 0:2048])
        ctx.barrier()
        for oc in range(16):
            sprev = ctx.n["s"]
            def th(te, oc=oc, sprev=sprev):
                if sprev:
                    te.wait_ge(sems["s"], sprev)
                for kc in range(8):
                    inst = te.matmul(ps_mm[oc % 2][:, 0:4],
                                     wbig[:, kc * 2048 + oc * 128:kc * 2048 + (oc + 1) * 128],
                                     wk["tailh"][:, 64 + kc * 4:64 + (kc + 1) * 4],
                                     start=(kc == 0), stop=(kc == 7))
                mark(ctx, inst, "te")
            ctx.te(th)
            tw = ctx.n["te"]
            def sh(s, oc=oc, tw=tw):
                s.wait_ge(sems["te"], tw)
                mark(ctx, s.copy(wk["halo"][:, oc * 4:(oc + 1) * 4], ps_mm[oc % 2][:, 0:4]), "s")
            ctx.s(sh)
        swh = ctx.n["s"]
        def vfl(v, swh=swh):
            v.wait_ge(sems["s"], swh)
            inst = v.tensor_scalar(wk["halo"][:, :], wk["halo"][:, :], C["flag"][:, 0:1], None, OP.mult)
            mark(ctx, inst, "v")
        ctx.v(vfl)
        ctx.barrier()

        def passA(k):
            t0 = k * CH
            sd_entry = ctx.n["sd"] * 16
            vg, tg = trk["vrms"], trk["tz"]
            def gpre(g, vg=vg, tg=tg):
                if vg:
                    g.wait_ge(sems["v"], vg)
                if tg:
                    g.wait_ge(sems["te"], tg)
            ctx.g(gpre)
            gload(ctx, wk["srcx"][:, :].rearrange("p (c t) -> p c t", c=8), src_dram[:, t0:t0 + CH].rearrange("(c p) t -> p c t", p=128))
            if k > 0:
                for kc in range(8):
                    gload(ctx, wbig[:, kc * 2048:(kc + 1) * 2048],
                          din[f"m{li}_inprojT"][kc * 128:(kc + 1) * 128, 0:2048])
            ccb = agof.get(k - 2, 0)
            def sypre(sy, ccb=ccb):
                if ccb:
                    sy.wait_ge(sems["cc"], ccb)
            ctx.sy(sypre)
            gdw_pa = ctx.n["gd"] * 16
            def wait_gd_pa(e, gdw_pa=gdw_pa):
                e.wait_ge(sems["gd"], gdw_pa)
            src_tiles = [wk["srcx"][:, fc * CH:(fc + 1) * CH] for fc in range(8)]
            ctx.s(wait_gd_pa)
            ctx.te(wait_gd_pa)
            emit_norm_stats(ctx, lnp, src_tiles, CH, ps_mm[0], ps_mm[1], C, rms=True)
            sw = ctx.n["s"]
            def vrms(v, sw=sw):
                v.wait_ge(sems["s"], sw)
                for fc in range(8):
                    inst = v.scalar_tensor_tensor(wk["normed"][:, fc * CH:(fc + 1) * CH],
                                                  src_tiles[fc], rmsw[:, fc:fc + 1],
                                                  lnp.istd_b[:, 0:CH], OP.mult, OP.mult)
                mark(ctx, inst, "v")
            ctx.v(vrms)
            vw = ctx.n["v"]
            trk["vrms"] = vw
            for dc in range(16):
                sprev = ctx.n["s"]
                def tip(te, dc=dc, vw=vw, sprev=sprev, gdw_pa=gdw_pa):
                    te.wait_ge(sems["v"], vw)
                    if dc == 0:
                        te.wait_ge(sems["gd"], gdw_pa)
                    if sprev:
                        te.wait_ge(sems["s"], sprev)
                    for kc in range(8):
                        inst = te.matmul(ps_mm[dc % 2][:, :],
                                         wbig[:, kc * 2048 + dc * 128:kc * 2048 + (dc + 1) * 128],
                                         wk["normed"][:, kc * CH:(kc + 1) * CH],
                                         start=(kc == 0), stop=(kc == 7))
                    mark(ctx, inst, "te")
                ctx.te(tip)
                twi = ctx.n["te"]
                slot = dc % 3
                def sxi(s, dc=dc, twi=twi, slot=slot):
                    s.wait_ge(sems["te"], twi)
                    mark(ctx, s.copy(wk["x_in"][:, slot * XW + 4:(slot + 1) * XW],
                                     ps_mm[dc % 2][:, :]), "s")
                ctx.s(sxi)
                swx = ctx.n["s"]
                def vcv(v, dc=dc, slot=slot, swx=swx):
                    v.wait_ge(sems["s"], swx)
                    v.tensor_copy(wk["x_in"][:, slot * XW:slot * XW + 4], wk["halo"][:, dc * 4:(dc + 1) * 4])
                    for j in range(4):
                        inst = v.tensor_scalar(wk["diag"][:, j * 128:(j + 1) * 128], C["ident_bf"][:, :],
                                               convw[:, dc * 4 + j:dc * 4 + j + 1], None, OP.mult)
                    mark(ctx, inst, "v")
                ctx.v(vcv)
                vwc = ctx.n["v"]
                sprev2 = ctx.n["s"]
                sdprev = ctx.n["sd"] * 16
                def tcv(te, dc=dc, slot=slot, vwc=vwc, sprev2=sprev2):
                    te.wait_ge(sems["v"], vwc)
                    for j in range(4):
                        inst = te.matmul(ps_x[:, :], wk["diag"][:, j * 128:(j + 1) * 128],
                                         wk["x_in"][:, slot * XW + j + 1:slot * XW + j + 1 + CH],
                                         start=(j == 0), stop=(j == 3))
                    mark(ctx, inst, "te")
                ctx.te(tcv)
                twc = ctx.n["te"]
                def scv(s, dc=dc, twc=twc, sdprev=sdprev):
                    s.wait_ge(sems["te"], twc)
                    if dc >= 2:
                        s.wait_ge(sems["sd"], sdprev)
                    mark(ctx, s.activation(wk["x_c"][:, (dc % 2) * CH:(dc % 2 + 1) * CH],
                                           ps_x[:, :], AF.Silu, bias=convb[:, dc:dc + 1]), "s")
                ctx.s(scv)
                swc = ctx.n["s"]
                def vht(v, dc=dc, slot=slot, swc=swc):
                    v.wait_ge(sems["s"], swc)
                    mark(ctx, v.tensor_copy(wk["halo"][:, dc * 4:(dc + 1) * 4],
                                            wk["x_in"][:, slot * XW + CH:slot * XW + CH + 4]), "v")
                ctx.v(vht)
                def txp(te, dc=dc, swc=swc):
                    te.wait_ge(sems["s"], swc)
                    mark(ctx, te.matmul(ps_bl[0][0:96, :], wk["wxp"][:, dc * 96:(dc + 1) * 96],
                                        wk["x_c"][:, (dc % 2) * CH:(dc % 2 + 1) * CH],
                                        start=(dc == 0), stop=(dc == 15)), "te")
                ctx.te(txp)
                def syxc(sy, swc=swc):
                    sy.wait_ge(sems["s"], swc)
                ctx.sy(syxc)
                sload(ctx, xc_dram[dc * 128:(dc + 1) * 128, t0:t0 + CH],
                      wk["x_c"][:, (dc % 2) * CH:(dc % 2 + 1) * CH])
                sload(ctx, bdl_in[k % 2][DI + dc * 128:DI + (dc + 1) * 128, :],
                      wk["x_c"][:, (dc % 2) * CH:(dc % 2 + 1) * CH])
            twx = ctx.n["te"]
            def sxd(s, twx=twx, sd_entry=sd_entry):
                s.wait_ge(sems["te"], twx)
                s.wait_ge(sems["sd"], sd_entry)
                mark(ctx, s.copy(wk["xdbl"][:, :], ps_bl[0][0:96, :]), "s")
            ctx.s(sxd)
            swz0 = ctx.n["s"]
            twz0 = ctx.n["te"]
            def gwz0(g, twz0=twz0):
                g.wait_ge(sems["te"], twz0)
            ctx.g(gwz0)
            for kc in range(8):
                gload(ctx, wbig[:, kc * 2048:(kc + 1) * 2048],
                      din[f"m{li}_inprojT"][kc * 128:(kc + 1) * 128, 2048:4096])
            gwz = ctx.n["gd"] * 16
            for oc in range(16):
                sprev = ctx.n["s"]
                sdprev = ctx.n["sd"] * 16
                def tz(te, oc=oc, gwz=gwz, sprev=sprev):
                    te.wait_ge(sems["gd"], gwz)
                    if sprev:
                        te.wait_ge(sems["s"], sprev)
                    for kc in range(8):
                        inst = te.matmul(ps_mm[oc % 2][:, :],
                                         wbig[:, kc * 2048 + oc * 128:kc * 2048 + (oc + 1) * 128],
                                         wk["normed"][:, kc * CH:(kc + 1) * CH],
                                         start=(kc == 0), stop=(kc == 7))
                    mark(ctx, inst, "te")
                ctx.te(tz)
                twz = ctx.n["te"]
                def sz_(s, oc=oc, twz=twz, sdprev=sdprev):
                    s.wait_ge(sems["te"], twz)
                    if oc >= 2:
                        s.wait_ge(sems["sd"], sdprev)
                    mark(ctx, s.activation(wk["sz"][:, (oc % 2) * CH:(oc % 2 + 1) * CH],
                                           ps_mm[oc % 2][:, :], AF.Silu), "s")
                ctx.s(sz_)
                swz = ctx.n["s"]
                def syz(sy, swz=swz):
                    sy.wait_ge(sems["s"], swz)
                ctx.sy(syz)
                sload(ctx, z_dram[oc * 128:(oc + 1) * 128, t0:t0 + CH],
                      wk["sz"][:, (oc % 2) * CH:(oc % 2 + 1) * CH])
            trk["tz"] = ctx.n["te"]
            swxd = ctx.n["s"]
            for oc in range(16):
                sprev = ctx.n["s"]
                def tdt(te, oc=oc, swxd=swxd, sprev=sprev):
                    te.wait_ge(sems["s"], max(swxd, sprev))
                    mark(ctx, te.matmul(ps_bl[1][:, :], wk["wdt"][:, oc * 128:(oc + 1) * 128],
                                        wk["xdbl"][0:64, :], start=True, stop=True), "te")
                ctx.te(tdt)
                twd = ctx.n["te"]
                sdprev = ctx.n["sd"] * 16
                def sdt(s, oc=oc, twd=twd, sdprev=sdprev):
                    s.wait_ge(sems["te"], twd)
                    if oc >= 2:
                        s.wait_ge(sems["sd"], sdprev)
                    s.activation(wk["sp1"][:, 0:CH], ps_bl[1][:, :], AF.Abs, bias=dtb[:, oc:oc + 1])
                    s.activation(wk["sp1"][:, CH:2 * CH], ps_bl[1][:, :], AF.Relu, bias=dtb[:, oc:oc + 1])
                    s.activation(wk["sp1"][:, 2 * CH:3 * CH], wk["sp1"][:, 0:CH], AF.Exp, scale=-1.0)
                    inst = s.activation(wk["sp1"][:, 0:CH], wk["sp1"][:, 2 * CH:3 * CH],
                                        AF.Ln, bias=misc[:, 0:1])
                    mark(ctx, inst, "s")
                ctx.s(sdt)
                sw2 = ctx.n["s"]
                sdp2 = ctx.n["sd"] * 16
                def vdt(v, oc=oc, sw2=sw2, sdp2=sdp2):
                    v.wait_ge(sems["s"], sw2)
                    if oc >= 2:
                        v.wait_ge(sems["sd"], sdp2)
                    mark(ctx, v.tensor_tensor(wk["delta"][:, (oc % 2) * CH:(oc % 2 + 1) * CH],
                                              wk["sp1"][:, 0:CH], wk["sp1"][:, CH:2 * CH],
                                              OP.add), "v")
                ctx.v(vdt)
                vwd = ctx.n["v"]
                def syd(sy, vwd=vwd):
                    sy.wait_ge(sems["v"], vwd)
                ctx.sy(syd)
                sload(ctx, bdl_in[k % 2][oc * 128:(oc + 1) * 128, :],
                      wk["delta"][:, (oc % 2) * CH:(oc % 2 + 1) * CH])
            swbc = ctx.n["s"]
            def sybc(sy, swbc=swbc):
                sy.wait_ge(sems["s"], swbc)
            ctx.sy(sybc)
            sload(ctx, bdl_in[k % 2][2 * DI:2 * DI + 32, :], wk["xdbl"][64:96, :])
            sdw = ctx.n["sd"] * 16
            def gag(g, sdw=sdw):
                g.wait_ge(sems["sd"], sdw)
            ctx.g(gag)
            agof[k] = collective(ctx, "AllGather", OP.bypass, PAIRS, bdl_in[k % 2], bdl_out[k])

        def scan_chunk(g):
            slot = g // 8
            kk = g % 8
            base = slot * (2 * DI + 32)
            ccw = agof[kk]
            ccy = yagof.get(g - 2, 0)
            teb = trk["tbc"]
            def syw(sy, ccw=ccw, ccy=ccy, teb=teb):
                sy.wait_ge(sems["cc"], ccw)
                if ccy:
                    sy.wait_ge(sems["cc"], ccy)
                if teb:
                    sy.wait_ge(sems["te"], teb)
            ctx.sy(syw)
            sload(ctx, wk["bc"][:, :], bdl_out[kk][base + 2 * DI:base + 2 * DI + 32, :])
            sdw0 = ctx.n["sd"] * 16
            for n in range(16):
                sprev = ctx.n["s"]
                def tbc(te, n=n, sprev=sprev, sdw0=sdw0):
                    te.wait_ge(sems["sd"], sdw0)
                    if sprev:
                        te.wait_ge(sems["s"], sprev)
                    te.matmul(ps_bl[0][:, :], C["selB"][:, n * 128:(n + 1) * 128],
                              wk["bc"][:, :], start=True, stop=True)
                    mark(ctx, te.matmul(ps_bl[1][:, :], C["selC"][:, n * 128:(n + 1) * 128],
                                        wk["bc"][:, :], start=True, stop=True), "te")
                ctx.te(tbc)
                if n == 15:
                    trk["tbc"] = ctx.n["te"]
                tw = ctx.n["te"]
                def sbc(s, n=n, tw=tw):
                    s.wait_ge(sems["te"], tw)
                    s.copy(wk["Bb"][:, n * CH:(n + 1) * CH], ps_bl[0][:, :])
                    mark(ctx, s.copy(wk["Cb"][:, n * CH:(n + 1) * CH], ps_bl[1][:, :]), "s")
                ctx.s(sbc)
            tfbm_prev = trk.get("tfbm", [0] * 8)
            tfbm = [0] * 8
            sdwm = {}

            def issue_pair(dcx):
                sb = (dcx % 2) * 4 * CH
                guard = tfbm[dcx - 2] if dcx >= 2 else tfbm_prev[6 + dcx]
                def syp(sy, guard=guard):
                    if guard:
                        sy.wait_ge(sems["te"], guard)
                ctx.sy(syp)
                sload(ctx, wk["pair"][:, sb:sb + CH],
                      bdl_out[kk][base + dcx * 128:base + (dcx + 1) * 128, :])
                sload(ctx, wk["pair"][:, sb + CH:sb + 2 * CH],
                      bdl_out[kk][base + (dcx + 8) * 128:base + (dcx + 9) * 128, :])
                sload(ctx, wk["pair"][:, sb + 2 * CH:sb + 3 * CH],
                      bdl_out[kk][base + DI + dcx * 128:base + DI + (dcx + 1) * 128, :])
                sload(ctx, wk["pair"][:, sb + 3 * CH:sb + 4 * CH],
                      bdl_out[kk][base + DI + (dcx + 8) * 128:base + DI + (dcx + 9) * 128, :])
                sdwm[dcx] = ctx.n["sd"] * 16

            issue_pair(0)
            for dc in range(8):
                if dc + 1 < 8:
                    issue_pair(dc + 1)
                sb = (dc % 2) * 4 * CH
                sdw = sdwm[dc]
                sprev = ctx.n["s"]
                def tfb(te, sb=sb, sdw=sdw, sprev=sprev):
                    te.wait_ge(sems["sd"], sdw)
                    if sprev:
                        te.wait_ge(sems["s"], sprev)
                    te.matmul(ps_mm[0][:, :], C["sel_lo"][:, :], wk["pair"][:, sb:sb + CH],
                              start=True, stop=False)
                    te.matmul(ps_mm[0][:, :], C["sel_hi"][:, :], wk["pair"][:, sb + CH:sb + 2 * CH],
                              start=False, stop=True)
                    te.matmul(ps_mm[1][:, :], C["sel_lo"][:, :], wk["pair"][:, sb + 2 * CH:sb + 3 * CH],
                              start=True, stop=False)
                    mark(ctx, te.matmul(ps_mm[1][:, :], C["sel_hi"][:, :], wk["pair"][:, sb + 3 * CH:sb + 4 * CH],
                                        start=False, stop=True), "te")
                ctx.te(tfb)
                tfbm[dc] = ctx.n["te"]
                tw = ctx.n["te"]
                dslot = dc % 2
                def sfb(s, tw=tw, dslot=dslot):
                    s.wait_ge(sems["te"], tw)
                    s.copy(wk["dfeed"][:, dslot * CH:(dslot + 1) * CH], ps_mm[0][:, :])
                    mark(ctx, s.copy(wk["xfeed"][:, dslot * CH:(dslot + 1) * CH], ps_mm[1][:, :]), "s")
                ctx.s(sfb)
                swf = ctx.n["s"]
                def vdu(v, dslot=dslot, swf=swf):
                    v.wait_ge(sems["s"], swf)
                    mark(ctx, v.tensor_tensor(wk["du"][:, :], wk["dfeed"][:, dslot * CH:(dslot + 1) * CH],
                                              wk["xfeed"][:, dslot * CH:(dslot + 1) * CH], OP.mult), "v")
                ctx.v(vdu)
                def emit_w(nn):
                    pbn = nn % 2
                    gw = gmarks[nn]
                    tprev2 = ctx.n["te"]
                    def vw_(v, nn=nn, pbn=pbn, gw=gw, tprev2=tprev2):
                        v.wait_ge(sems["g"], gw)
                        if nn >= 2 and tprev2 > 1:
                            v.wait_ge(sems["te"], tprev2 - 1)
                        mark(ctx, v.tensor_tensor(wk["wv"][:, pbn * CH:(pbn + 1) * CH],
                                                  wk["hbf"][:, pbn * CH:(pbn + 1) * CH],
                                                  wk["Cb"][:, nn * CH:(nn + 1) * CH], OP.mult), "v")
                    ctx.v(vw_)
                    vw_w = ctx.n["v"]
                    sprev3 = ctx.n["s"]
                    def tym(te, dc=dc, nn=nn, pbn=pbn, vw_w=vw_w, sprev3=sprev3):
                        te.wait_ge(sems["v"], vw_w)
                        if nn == 0 and sprev3:
                            te.wait_ge(sems["s"], sprev3)
                        mark(ctx, te.matmul(ps_y[dc % 2][:, :], C["ident_bf"][:, :],
                                            wk["wv"][:, pbn * CH:(pbn + 1) * CH],
                                            start=(nn == 0), stop=(nn == 15)), "te")
                    ctx.te(tym)
                gmarks = {}
                for n in range(16):
                    pb = n % 2
                    hslot = n % 4
                    vprev = ctx.n["v"]
                    def sdA(s, dc=dc, n=n, pb=pb, vprev=vprev, dslot=dslot, swf=swf):
                        s.wait_ge(sems["s"], swf)
                        if n >= 2 and vprev > 3:
                            s.wait_ge(sems["v"], vprev - 3)
                    ctx.s(sdA)
                    def sdA2(s, dc=dc, n=n, pb=pb, dslot=dslot):
                        mark(ctx, s.activation(wk["dA"][:, pb * CH:(pb + 1) * CH],
                                               wk["dfeed"][:, dslot * CH:(dslot + 1) * CH], AF.Exp,
                                               scale=Acol[:, dc * 16 + n:dc * 16 + n + 1]), "s")
                    ctx.s(sdA2)
                    sw_dA = ctx.n["s"]
                    gprev = ctx.n["g"]
                    def vscan(v, dc=dc, n=n, pb=pb, hslot=hslot, sw_dA=sw_dA, gprev=gprev):
                        v.wait_ge(sems["s"], sw_dA)
                        v.tensor_tensor(wk["in1"][:, pb * CH:(pb + 1) * CH], wk["du"][:, :],
                                        wk["Bb"][:, n * CH:(n + 1) * CH], OP.mult)
                        if n >= 4 and gprev > 3:
                            v.wait_ge(sems["g"], gprev - 3)
                        mark(ctx, v.tensor_tensor_scan(wk["hbig"][:, hslot * CH:(hslot + 1) * CH],
                                                       wk["dA"][:, pb * CH:(pb + 1) * CH],
                                                       wk["in1"][:, pb * CH:(pb + 1) * CH],
                                                       state[:, dc * 16 + n:dc * 16 + n + 1],
                                                       OP.mult, OP.add), "v")
                    ctx.v(vscan)
                    vw_scan = ctx.n["v"]
                    def gcast(g_, pb=pb, hslot=hslot, vw_scan=vw_scan):
                        g_.wait_ge(sems["v"], vw_scan)
                        mark(ctx, g_.tensor_copy(wk["hbf"][:, pb * CH:(pb + 1) * CH],
                                                 wk["hbig"][:, hslot * CH:(hslot + 1) * CH]), "g")
                    ctx.g(gcast)
                    gmarks[n] = ctx.n["g"]
                    if n >= 1:
                        emit_w(n - 1)
                    if n % 4 == 3:
                        ng0 = (n // 4) * 4
                        _vws = ctx.n["v"]
                        def sst_(s, dc=dc, ng0=ng0, _vws=_vws):
                            s.wait_ge(sems["v"], _vws)
                            mark(ctx, s.copy(state[:, dc * 16 + ng0:dc * 16 + ng0 + 4],
                                             wk["hbig"][:, CH - 1::CH]), "s")
                        ctx.s(sst_)
                emit_w(15)
                tw = ctx.n["te"]
                sdpy = ctx.n["sd"] * 16
                def sy_(s, dc=dc, tw=tw, sdpy=sdpy):
                    s.wait_ge(sems["te"], tw)
                    if dc >= 2:
                        s.wait_ge(sems["sd"], sdpy)
                    mark(ctx, s.copy(wk["ymine"][:, (dc % 2) * CH:(dc % 2 + 1) * CH], ps_y[dc % 2][:, :]), "s")
                ctx.s(sy_)
                _sn = ctx.n["s"]
                def syy(sy, _sn=_sn):
                    sy.wait_ge(sems["s"], _sn)
                ctx.sy(syy)
                sload(ctx, yag_in[g % 2][dc * 128:(dc + 1) * 128, :],
                      wk["ymine"][:, (dc % 2) * CH:(dc % 2 + 1) * CH])
            trk["tfbm"] = tfbm
            sdw2 = ctx.n["sd"] * 16
            def gwy(g_, sdw2=sdw2):
                g_.wait_ge(sems["sd"], sdw2)
            ctx.g(gwy)
            yagof[g] = collective(ctx, "AllGather", OP.bypass, PAIRS, yag_in[g % 2], yag_out[g])

        def gating(k, dst_add_x):
            t0 = k * CH
            ccw = yagof[k + 8]
            def syw(sy, ccw=ccw):
                sy.wait_ge(sems["cc"], ccw)
            ctx.sy(syw)
            if dst_add_x:
                vg = trk["vfin"]
                def gpre2(g, vg=vg):
                    if vg:
                        g.wait_ge(sems["v"], vg)
                ctx.g(gpre2)
                gload(ctx, wk["srcx"][:, :].rearrange("p (c t) -> p c t", c=8), din["x"][:, t0:t0 + CH].rearrange("(c p) t -> p c t", p=128))
            gdw = ctx.n["gd"] * 16
            for grp in range(4):
                tprev = ctx.n["te"]
                vprev = ctx.n["v"]
                def syg(sy, grp=grp, tprev=tprev, vprev=vprev):
                    if tprev:
                        sy.wait_ge(sems["te"], tprev)
                    if vprev:
                        sy.wait_ge(sems["v"], vprev)
                ctx.sy(syg)
                for j in range(4):
                    c16 = grp * 4 + j
                    sload(ctx, wk["yfa"][:, j * CH:(j + 1) * CH],
                          yag_out[k][c16 * 128:(c16 + 1) * 128, :])
                    sload(ctx, wk["yfb"][:, j * CH:(j + 1) * CH],
                          yag_out[k + 8][c16 * 128:(c16 + 1) * 128, :])
                sdw = ctx.n["sd"] * 16
                for j in range(4):
                    c16 = grp * 4 + j
                    vprev2 = ctx.n["v"]
                    def syl(sy, c16=c16, j=j, vprev2=vprev2):
                        if vprev2:
                            sy.wait_ge(sems["v"], vprev2)
                    ctx.sy(syl)
                    sload(ctx, wk["x_c"][:, (j % 2) * CH:(j % 2 + 1) * CH],
                          xc_dram[c16 * 128:(c16 + 1) * 128, t0:t0 + CH])
                    sload(ctx, wk["sz"][:, (j % 2) * CH:(j % 2 + 1) * CH],
                          z_dram[c16 * 128:(c16 + 1) * 128, t0:t0 + CH])
                    sdw2 = ctx.n["sd"] * 16
                    vprev3 = ctx.n["v"]
                    def tys(te, c16=c16, j=j, sdw=sdw, vprev3=vprev3):
                        te.wait_ge(sems["sd"], sdw)
                        if vprev3:
                            te.wait_ge(sems["v"], vprev3)
                        te.matmul(ps_bl[c16 % 2][:, :], C["sel_lo"][:, :],
                                  wk["yfa"][:, j * CH:(j + 1) * CH], start=True, stop=False)
                        mark(ctx, te.matmul(ps_bl[c16 % 2][:, :], C["sel_hi"][:, :],
                                            wk["yfb"][:, j * CH:(j + 1) * CH], start=False, stop=True), "te")
                    ctx.te(tys)
                    tw = ctx.n["te"]
                    def vga(v, c16=c16, j=j, tw=tw, sdw2=sdw2):
                        v.wait_ge(sems["te"], tw)
                        v.wait_ge(sems["sd"], sdw2)
                        v.scalar_tensor_tensor(wk["gat"][:, c16 * CH:(c16 + 1) * CH],
                                               wk["x_c"][:, (j % 2) * CH:(j % 2 + 1) * CH],
                                               Dcol[:, c16:c16 + 1], ps_bl[c16 % 2][:, :],
                                               OP.mult, OP.add)
                        mark(ctx, v.tensor_tensor(wk["gat"][:, c16 * CH:(c16 + 1) * CH],
                                                  wk["gat"][:, c16 * CH:(c16 + 1) * CH],
                                                  wk["sz"][:, (j % 2) * CH:(j % 2 + 1) * CH], OP.mult), "v")
                    ctx.v(vga)
            vwg = ctx.n["v"]
            for oc in range(8):
                sprev = ctx.n["s"]
                gdw_o = trk.get("gdw_o", 0)
                def tmo(te, oc=oc, vwg=vwg, sprev=sprev, gdw_o=gdw_o):
                    te.wait_ge(sems["v"], vwg)
                    if oc == 0 and gdw_o:
                        te.wait_ge(sems["gd"], gdw_o)
                    if sprev:
                        te.wait_ge(sems["s"], sprev)
                    for kc in range(16):
                        inst = te.matmul(ps_mm[oc % 2][:, :],
                                         wbig[:, kc * 1024 + oc * 128:kc * 1024 + (oc + 1) * 128],
                                         wk["gat"][:, kc * CH:(kc + 1) * CH],
                                         start=(kc == 0), stop=(kc == 15))
                    mark(ctx, inst, "te")
                ctx.te(tmo)
                tw = ctx.n["te"]
                def smo(s, oc=oc, tw=tw):
                    s.wait_ge(sems["te"], tw)
                    mark(ctx, s.copy(wk["mix"][:, oc * CH:(oc + 1) * CH], ps_mm[oc % 2][:, :]), "s")
                ctx.s(smo)
            mix_tiles = [wk["mix"][:, oc * CH:(oc + 1) * CH] for oc in range(8)]
            emit_norm_stats(ctx, lnp, mix_tiles, CH, ps_mm[0], ps_mm[1], C, rms=False)
            swn = ctx.n["s"]
            for oc in range(8):
                def vap(v, oc=oc, swn=swn):
                    v.wait_ge(sems["s"], swn)
                    v.tensor_tensor(wk["sp1"][:, 0:CH], wk["mix"][:, oc * CH:(oc + 1) * CH],
                                    lnp.mu_b[:, 0:CH], OP.subtract)
                    mark(ctx, v.scalar_tensor_tensor(wk["sp1"][:, CH:2 * CH], wk["sp1"][:, 0:CH],
                                                     lnw[:, oc:oc + 1], lnp.istd_b[:, 0:CH],
                                                     OP.mult, OP.mult), "v")
                ctx.v(vap)
                vw2 = ctx.n["v"]
                def sap(s, oc=oc, vw2=vw2):
                    s.wait_ge(sems["v"], vw2)
                    mark(ctx, s.activation(wk["sp1"][:, 2 * CH:3 * CH], wk["sp1"][:, CH:2 * CH],
                                           AF.Identity, bias=lnb[:, oc:oc + 1]), "s")
                ctx.s(sap)
                sw3 = ctx.n["s"]
                def vfin(v, oc=oc, sw3=sw3, dst_add_x=dst_add_x, gdw=gdw):
                    v.wait_ge(sems["s"], sw3)
                    if dst_add_x:
                        v.wait_ge(sems["gd"], gdw)
                        inst = v.tensor_tensor(wk["mix"][:, oc * CH:(oc + 1) * CH],
                                               wk["sp1"][:, 2 * CH:3 * CH],
                                               wk["srcx"][:, oc * CH:(oc + 1) * CH], OP.add)
                    else:
                        inst = v.tensor_copy(wk["mix"][:, oc * CH:(oc + 1) * CH],
                                             wk["sp1"][:, 2 * CH:3 * CH])
                    mark(ctx, inst, "v")
                ctx.v(vfin)
            vwf = ctx.n["v"]
            trk["vfin"] = vwf
            def sys(sy, vwf=vwf):
                sy.wait_ge(sems["v"], vwf)
            ctx.sy(sys)
            sload(ctx, res_dram[:, t0:t0 + CH].rearrange("(c p) t -> p c t", p=128), wk["mix"][:, :].rearrange("p (c t) -> p c t", c=8))

        for k in range(NCH):
            passA(k)
            scan_chunk(k)
        tg = trk["tz"]
        def gwout(g, tg=tg):
            g.wait_ge(sems["te"], tg)
        ctx.g(gwout)
        for kc in range(16):
            gload(ctx, wbig[:, kc * 1024:(kc + 1) * 1024],
                  din[f"m{li}_outT"][kc * 128:(kc + 1) * 128, :])
        trk["gdw_o"] = ctx.n["gd"] * 16
        for k in range(NCH):
            scan_chunk(k + 8)
            if k > 0:
                gating(k - 1, dst_add_x=(li == 0))
        gating(NCH - 1, dst_add_x=(li == 0))

    ctx.push_scope()
    alloc_mamba()
    emit_mamba_layer(0)
    if dbg_stage != "m0":
        emit_mamba_layer(1)
    ctx.pop_scope()
    if dbg_stage == "m0":
        ctx.barrier()
        def gdump0(g):
            mark(ctx, g.dma_start(out=out_dram, in_=res_dram), "gd")
        ctx.g(gdump0)
        ctx.barrier()
        return ctx

    def emit_ffn():
        tffm = {}
        gwm = {}
        nlm = {}

        def issue_wload(bj):
            bw = (bj % 2) * 8192
            guard = tffm.get((bj - 2) * 8 + 7, 0)
            def gww(g, guard=guard):
                if guard:
                    g.wait_ge(sems["te"], guard)
            ctx.g(gww)
            for kc in range(8):
                gload(ctx, wbig[:, bw + kc * 512:bw + (kc + 1) * 512],
                      din["ff1T"][kc * 128:(kc + 1) * 128, bj * 512:(bj + 1) * 512])
                gload(ctx, wbig[:, bw + 4096 + kc * 512:bw + 4096 + (kc + 1) * 512],
                      din["ff1T"][kc * 128:(kc + 1) * 128, FF + bj * 512:FF + (bj + 1) * 512])
            gwm[bj] = ctx.n["gd"] * 16

        def issue_nload(a):
            tile = wk["normed"] if a % 2 == 0 else wk["srcx"]
            guard = tffm.get(a - 2, 0)
            def gwn(g, guard=guard):
                if guard:
                    g.wait_ge(sems["te"], guard)
            ctx.g(gwn)
            t0a = (a % 8) * CH
            gload(ctx, tile[:, :].rearrange("p (c t) -> p c t", c=8),
                  res_dram[:, t0a:t0a + CH].rearrange("(c p) t -> p c t", p=128))
            nlm[a] = ctx.n["gd"] * 16

        issue_wload(0)
        issue_nload(0)
        for blkj in range(8):
            if blkj + 1 < 8:
                issue_wload(blkj + 1)
            bw = (blkj % 2) * 8192
            gw = gwm[blkj]
            for tb in range(8):
                a = blkj * 8 + tb
                if a + 1 < 64:
                    issue_nload(a + 1)
                tile = wk["normed"] if a % 2 == 0 else wk["srcx"]
                gwh = nlm[a]
                t0 = tb * CH
                for j in range(4):
                    oc = blkj * 4 + j
                    sprev = ctx.n["s"]
                    def tff(te, j=j, bw=bw, tile=tile, gw=gw, gwh=gwh, sprev=sprev):
                        te.wait_ge(sems["gd"], max(gw, gwh))
                        if sprev:
                            te.wait_ge(sems["s"], sprev)
                        for kc in range(8):
                            te.matmul(ps_mm[0][:, :],
                                      wbig[:, bw + kc * 512 + j * 128:bw + kc * 512 + (j + 1) * 128],
                                      tile[:, kc * CH:(kc + 1) * CH],
                                      start=(kc == 0), stop=(kc == 7))
                        for kc in range(8):
                            inst = te.matmul(ps_mm[1][:, :],
                                             wbig[:, bw + 4096 + kc * 512 + j * 128:bw + 4096 + kc * 512 + (j + 1) * 128],
                                             tile[:, kc * CH:(kc + 1) * CH],
                                             start=(kc == 0), stop=(kc == 7))
                        mark(ctx, inst, "te")
                    ctx.te(tff)
                    tw = ctx.n["te"]
                    sdp = ctx.n["sd"] * 16
                    def sff(s, oc=oc, tw=tw):
                        s.wait_ge(sems["te"], tw)
                        s.activation(wk["sp1"][:, 0:CH], ps_mm[0][:, :], AF.Identity,
                                     bias=vecs["ffb1"][:, oc:oc + 1])
                        mark(ctx, s.activation(wk["sp1"][:, CH:2 * CH], ps_mm[1][:, :], AF.Gelu,
                                               bias=vecs["ffb1"][:, 32 + oc:33 + oc]), "s")
                    ctx.s(sff)
                    sw = ctx.n["s"]
                    def vff(v, sw=sw, sdp=sdp, j=j):
                        v.wait_ge(sems["s"], sw)
                        if sdp:
                            v.wait_ge(sems["sd"], sdp)
                        mark(ctx, v.tensor_tensor(wk["sz"][:, (j % 2) * CH:(j % 2 + 1) * CH],
                                                  wk["sp1"][:, 0:CH], wk["sp1"][:, CH:2 * CH],
                                                  OP.mult), "v")
                    ctx.v(vff)
                    vw = ctx.n["v"]
                    def syv(sy, vw=vw):
                        sy.wait_ge(sems["v"], vw)
                    ctx.sy(syv)
                    sload(ctx, v_dram[oc * 128:(oc + 1) * 128, t0:t0 + CH],
                          wk["sz"][:, (j % 2) * CH:(j % 2 + 1) * CH])
                tffm[a] = ctx.n["te"]
        ctx.barrier()
        for ochalf in range(2):
            tprev = ctx.n["te"]
            def gww(g, tprev=tprev):
                if tprev:
                    g.wait_ge(sems["te"], tprev)
            ctx.g(gww)
            for kc in range(32):
                gload(ctx, wbig[:, kc * 512:(kc + 1) * 512],
                      din["ff2T"][kc * 128:(kc + 1) * 128, ochalf * 512:(ochalf + 1) * 512])
            gw = ctx.n["gd"] * 16
            for tb in range(8):
                t0 = tb * CH
                psl = [ps_mm[0], ps_mm[1], ps_bl[0], ps_bl[1]]
                for khalf in range(2):
                    gw_consumers(te=True)
                    gload(ctx, wk["gat"][:, :].rearrange("p (c t) -> p c t", c=16),
                          v_dram[khalf * 2048:(khalf + 1) * 2048, t0:t0 + CH].rearrange(
                              "(c p) t -> p c t", p=128))
                    gwv = ctx.n["gd"] * 16
                    sprev = ctx.n["s"]
                    def tf2(te, khalf=khalf, gw=gw, gwv=gwv, sprev=sprev):
                        te.wait_ge(sems["gd"], max(gw, gwv))
                        if khalf == 0 and sprev:
                            te.wait_ge(sems["s"], sprev)
                        for oc4 in range(4):
                            for kc in range(16):
                                kcg = khalf * 16 + kc
                                inst = te.matmul(psl[oc4][:, :],
                                                 wbig[:, kcg * 512 + oc4 * 128:kcg * 512 + (oc4 + 1) * 128],
                                                 wk["gat"][:, kc * CH:(kc + 1) * CH],
                                                 start=(kcg == 0), stop=(kcg == 31))
                        mark(ctx, inst, "te")
                    ctx.te(tf2)
                tw = ctx.n["te"]
                def sf2(s, ochalf=ochalf, tw=tw):
                    s.wait_ge(sems["te"], tw)
                    for oc4 in range(4):
                        oc = ochalf * 4 + oc4
                        inst = s.activation(wk["mix"][:, oc4 * CH:(oc4 + 1) * CH], psl[oc4][:, :],
                                            AF.Identity, bias=vecs["ffb2"][:, oc:oc + 1])
                    mark(ctx, inst, "s")
                ctx.s(sf2)
                sw = ctx.n["s"]
                def syf(sy, sw=sw):
                    sy.wait_ge(sems["s"], sw)
                ctx.sy(syf)
                for oc4 in range(4):
                    oc = ochalf * 4 + oc4
                    sload(ctx, stream_dram[oc * 128:(oc + 1) * 128, t0:t0 + CH],
                          wk["mix"][:, oc4 * CH:(oc4 + 1) * CH])
        ctx.barrier()
        if dbg_stage == "ff2":
            return
        for tb in range(8):
            t0 = tb * CH
            gw_consumers(te=True, v=True, s=True)
            _sdn = ctx.n["sd"] * 16
            def gsd(g, _sdn=_sdn):
                if _sdn:
                    g.wait_ge(sems["sd"], _sdn)
            ctx.g(gsd)
            gload(ctx, wk["mix"][:, :].rearrange("p (c t) -> p c t", c=8), stream_dram[:, t0:t0 + CH].rearrange("(c p) t -> p c t", p=128))
            gload(ctx, wk["srcx"][:, :].rearrange("p (c t) -> p c t", c=8), din["x"][:, t0:t0 + CH].rearrange("(c p) t -> p c t", p=128))
            gw = ctx.n["gd"] * 16
            def vadd(v, gw=gw):
                v.wait_ge(sems["gd"], gw)
                for oc in range(8):
                    inst = v.tensor_tensor(wk["mix"][:, oc * CH:(oc + 1) * CH],
                                           wk["mix"][:, oc * CH:(oc + 1) * CH],
                                           wk["srcx"][:, oc * CH:(oc + 1) * CH], OP.add)
                mark(ctx, inst, "v")
            ctx.v(vadd)
            mix_tiles = [wk["mix"][:, oc * CH:(oc + 1) * CH] for oc in range(8)]
            emit_norm_stats(ctx, lnp, mix_tiles, CH, ps_mm[0], ps_mm[1], C, rms=False, vwait=ctx.n["v"])
            swn = ctx.n["s"]
            for oc in range(8):
                def vap(v, oc=oc, swn=swn):
                    v.wait_ge(sems["s"], swn)
                    v.tensor_tensor(wk["sp1"][:, 0:CH], wk["mix"][:, oc * CH:(oc + 1) * CH],
                                    lnp.mu_b[:, 0:CH], OP.subtract)
                    mark(ctx, v.tensor_tensor(wk["srcx"][:, oc * CH:(oc + 1) * CH],
                                              wk["sp1"][:, 0:CH], lnp.istd_b[:, 0:CH], OP.mult), "v")
                ctx.v(vap)
            vw = ctx.n["v"]
            def syu(sy, vw=vw):
                sy.wait_ge(sems["v"], vw)
            ctx.sy(syu)
            sload(ctx, stream_dram[:, t0:t0 + CH].rearrange("(c p) t -> p c t", p=128), wk["srcx"][:, :].rearrange("p (c t) -> p c t", c=8))
        ctx.barrier()

    def emit_moe():
        centT = ctx.sbuf("centT_sb", [128, 8 * NE], BF16)
        sfm = ctx.sbuf("sfm", [16, TL], BF16)
        gfm = ctx.sbuf("gfm", [16, TL], BF16)
        stok = ctx.sbuf("stok", [128, 32 * NE], BF16)
        colsb = ctx.sbuf("colsb", [128, 10 * 32], F32)
        accf = ctx.sbuf("accf", [128, 4 * CH], F32)
        bsum2 = ctx.sbuf("bsum2", [128, 8], F32)
        gload(ctx, centT[:, :].rearrange("p (c e) -> p c e", c=8), din["centT"].rearrange("(c p) e -> p c e", p=128))
        def vbs(v):
            mark(ctx, v.tensor_tensor(colsb[:, 0:8], vecs["bs"][:, 0:8], vecs["bs"][:, 8:16], OP.add), "v")
        ctx.v(vbs)
        _vwb = ctx.n["v"]
        def sbs(s, _vwb=_vwb):
            s.wait_ge(sems["v"], _vwb)
            mark(ctx, s.activation(bsum2[:, :], colsb[:, 0:8], AF.Identity,
                                   scale=C["misc"][:, 3:4]), "s")
        ctx.s(sbs)
        ctx.barrier()
        for tb in range(8):
            t0 = tb * CH
            gw_consumers(te=True)
            gload(ctx, wk["normed"][:, :].rearrange("p (c t) -> p c t", c=8), stream_dram[:, t0:t0 + CH].rearrange("(c p) t -> p c t", p=128))
            gw = ctx.n["gd"] * 16
            sprev = ctx.n["s"]
            def tsc(te, gw=gw, sprev=sprev):
                te.wait_ge(sems["gd"], gw)
                if sprev:
                    te.wait_ge(sems["s"], sprev)
                for kc in range(8):
                    inst = te.matmul(ps_mm[0][0:16, :], centT[:, kc * NE:(kc + 1) * NE],
                                     wk["normed"][:, kc * CH:(kc + 1) * CH],
                                     start=(kc == 0), stop=(kc == 7))
                mark(ctx, inst, "te")
            ctx.te(tsc)
            tw = ctx.n["te"]
            def ssc(s, tb=tb, tw=tw):
                s.wait_ge(sems["te"], tw)
                mark(ctx, s.copy(sfm[:, tb * CH:(tb + 1) * CH], ps_mm[0][0:16, :]), "s")
            ctx.s(ssc)
        ctx.barrier()
        for sb in range(32):
            vprev = ctx.n["v"]
            def ttp(te, sb=sb, vprev=vprev):
                if vprev:
                    te.wait_ge(sems["v"], vprev)
                mark(ctx, te.transpose(ps_tp[:, 0:16],
                                       sfm[:, sb * 128:(sb + 1) * 128], C["ident_bf"][0:16, 0:16]), "te")
            ctx.te(ttp)
            tw = ctx.n["te"]
            def vtp(v, sb=sb, tw=tw):
                v.wait_ge(sems["te"], tw)
                mark(ctx, v.tensor_copy(stok[:, sb * NE:(sb + 1) * NE], ps_tp[:, 0:16]), "v")
            ctx.v(vtp)
        ctx.barrier()
        def vg1(v):
            for sb in range(32):
                v.tensor_tensor(stok[:, sb * NE:(sb + 1) * NE], stok[:, sb * NE:(sb + 1) * NE],
                                vecs["eb_row"][:, :], OP.add)
            for sb in range(32):
                v.reduce_max(colsb[:, sb:sb + 1], stok[:, sb * NE:(sb + 1) * NE], axis=AX.X)
            for sb in range(32):
                inst = v.tensor_scalar(colsb[:, 32 + sb:33 + sb], colsb[:, sb:sb + 1], -1.0, None, OP.mult)
            mark(ctx, inst, "v")
        ctx.v(vg1)
        vw = ctx.n["v"]
        def sg1(s, vw=vw):
            s.wait_ge(sems["v"], vw)
            for sb in range(32):
                inst = s.activation(stok[:, sb * NE:(sb + 1) * NE], stok[:, sb * NE:(sb + 1) * NE],
                                    AF.Exp, bias=colsb[:, 32 + sb:33 + sb],
                                    accum_out=colsb[:, 64 + sb:65 + sb])
            mark(ctx, inst, "s")
        ctx.s(sg1)
        sw = ctx.n["s"]
        def vg2a(v, sw=sw):
            v.wait_ge(sems["s"], sw)
            for sb in range(32):
                inst = v.reduce_max(colsb[:, 224 + sb:225 + sb], stok[:, sb * NE:(sb + 1) * NE], axis=AX.X)
            mark(ctx, inst, "v")
        ctx.v(vg2a)
        _vw2 = ctx.n["v"]
        def sg2a(s, _vw2=_vw2):
            s.wait_ge(sems["v"], _vw2)
            for sb in range(32):
                inst = s.copy(colsb[:, 96 + sb:97 + sb], colsb[:, 224 + sb:225 + sb])
            mark(ctx, inst, "s")
        ctx.s(sg2a)
        _sw2 = ctx.n["s"]
        def vg2(v, sw=sw, _sw2=_sw2):
            v.wait_ge(sems["s"], _sw2)
            for sb in range(32):
                v.tensor_scalar(wk["sp1"][:, sb * NE:(sb + 1) * NE] if False else lnp.sq[:, sb * NE:(sb + 1) * NE],
                                stok[:, sb * NE:(sb + 1) * NE], colsb[:, 96 + sb:97 + sb], None, OP.is_lt)
            for sb in range(32):
                v.tensor_tensor(lnp.sq[:, sb * NE:(sb + 1) * NE], stok[:, sb * NE:(sb + 1) * NE],
                                lnp.sq[:, sb * NE:(sb + 1) * NE], OP.mult)
            for sb in range(32):
                inst = v.reduce_max(colsb[:, 256 + sb:257 + sb], lnp.sq[:, sb * NE:(sb + 1) * NE], axis=AX.X)
            mark(ctx, inst, "v")
        ctx.v(vg2)
        _vw3 = ctx.n["v"]
        def sg2b(s, _vw3=_vw3):
            s.wait_ge(sems["v"], _vw3)
            for sb in range(32):
                inst = s.copy(colsb[:, 128 + sb:129 + sb], colsb[:, 256 + sb:257 + sb])
            mark(ctx, inst, "s")
        ctx.s(sg2b)
        _sw3 = ctx.n["s"]
        def vg2c(v, _sw3=_sw3):
            v.wait_ge(sems["s"], _sw3)
            for sb in range(32):
                v.tensor_scalar(lnp.sq[:, sb * NE:(sb + 1) * NE], stok[:, sb * NE:(sb + 1) * NE],
                                colsb[:, 128 + sb:129 + sb], None, OP.is_ge)
            for sb in range(32):
                v.tensor_tensor(stok[:, sb * NE:(sb + 1) * NE], stok[:, sb * NE:(sb + 1) * NE],
                                lnp.sq[:, sb * NE:(sb + 1) * NE], OP.mult)
            for sb in range(32):
                inst = v.reciprocal(colsb[:, 288 + sb:289 + sb], colsb[:, 64 + sb:65 + sb])
            mark(ctx, inst, "v")
        ctx.v(vg2c)
        _vw4 = ctx.n["v"]
        def sg2c(s, _vw4=_vw4):
            s.wait_ge(sems["v"], _vw4)
            for sb in range(32):
                inst = s.copy(colsb[:, 160 + sb:161 + sb], colsb[:, 288 + sb:289 + sb])
            mark(ctx, inst, "s")
        ctx.s(sg2c)
        _sw4 = ctx.n["s"]
        def vg2d(v, _sw4=_sw4):
            v.wait_ge(sems["s"], _sw4)
            for sb in range(32):
                inst = v.tensor_scalar(stok[:, sb * NE:(sb + 1) * NE], stok[:, sb * NE:(sb + 1) * NE],
                                       colsb[:, 160 + sb:161 + sb], None, OP.mult)
            mark(ctx, inst, "v")
        ctx.v(vg2d)
        ctx.barrier()
        for sb in range(32):
            sprev = ctx.n["s"]
            def ttb(te, sb=sb, sprev=sprev):
                if sprev:
                    te.wait_ge(sems["s"], sprev)
                mark(ctx, te.transpose(ps_tp[0:16, :], stok[:, sb * NE:(sb + 1) * NE],
                                       C["ident_bf"][:, :]), "te")
            ctx.te(ttb)
            tw = ctx.n["te"]
            def stb(s, sb=sb, tw=tw):
                s.wait_ge(sems["te"], tw)
                mark(ctx, s.copy(gfm[:, sb * 128:(sb + 1) * 128], ps_tp[0:16, :]), "s")
            ctx.s(stb)
        ctx.barrier()
        if dbg_stage == "moeg":
            _s = ctx.n["s"]
            def syg2(sy, _s=_s):
                sy.wait_ge(sems["s"], _s)
            ctx.sy(syg2)
            sload(ctx, v_dram[0:16, 0:TL], gfm[:, :])
            ctx.barrier()
            dump_stream(v_dram[0:D, :])
            return
        for eg in range(9):
            two = 2 if eg < 8 else 2
            tprev = ctx.n["te"]
            def gwe(g, tprev=tprev):
                if tprev:
                    g.wait_ge(sems["te"], tprev)
            ctx.g(gwe)
            for e2 in range(2):
                src_w = din["WrT"][eg * 2 + e2] if eg < 8 else din["WsT"][e2]
                for kc in range(8):
                    gload(ctx, wbig[:, e2 * 8192 + kc * 1024:e2 * 8192 + (kc + 1) * 1024],
                          src_w[kc * 128:(kc + 1) * 128, :])
            gw = ctx.n["gd"] * 16
            for tb in range(8):
                t0 = tb * CH
                gw_consumers(te=True, v=True)
                gload(ctx, wk["normed"][:, :].rearrange("p (c t) -> p c t", c=8), stream_dram[:, t0:t0 + CH].rearrange("(c p) t -> p c t", p=128))
                gwu = ctx.n["gd"] * 16
                for e2 in range(2):
                    sprev = ctx.n["s"]
                    vprev = ctx.n["v"]
                    if eg < 8:
                        def tgb(te, eg=eg, e2=e2, tb=tb, sprev=sprev, vprev=vprev):
                            if sprev:
                                te.wait_ge(sems["s"], sprev)
                            if vprev:
                                te.wait_ge(sems["v"], vprev)
                            e = eg * 2 + e2
                            mark(ctx, te.matmul(ps_x[:, :], C["sel16"][:, e * 128:(e + 1) * 128],
                                                gfm[:, tb * CH:(tb + 1) * CH], start=True, stop=True), "te")
                        ctx.te(tgb)
                        tw = ctx.n["te"]
                        def sgb(s, tw=tw, e2=e2):
                            s.wait_ge(sems["te"], tw)
                            mark(ctx, s.copy(wk["sz"][:, e2 * CH:(e2 + 1) * CH], ps_x[:, :]), "s")
                        ctx.s(sgb)
                        swg = ctx.n["s"]
                        def vue(v, e2=e2, swg=swg, gwu=gwu):
                            v.wait_ge(sems["s"], swg)
                            v.wait_ge(sems["gd"], gwu)
                            for kc in range(8):
                                inst = v.tensor_tensor(wk["gat"][:, (e2 * 8 + kc) * CH:(e2 * 8 + kc + 1) * CH],
                                                       wk["normed"][:, kc * CH:(kc + 1) * CH],
                                                       wk["sz"][:, e2 * CH:(e2 + 1) * CH], OP.mult)
                            mark(ctx, inst, "v")
                        ctx.v(vue)
                    else:
                        def vue(v, e2=e2, gwu=gwu, vprev=vprev):
                            v.wait_ge(sems["gd"], gwu)
                            if vprev:
                                v.wait_ge(sems["v"], vprev)
                            for kc in range(8):
                                inst = v.tensor_scalar(wk["gat"][:, (e2 * 8 + kc) * CH:(e2 * 8 + kc + 1) * CH],
                                                       wk["normed"][:, kc * CH:(kc + 1) * CH],
                                                       C["misc"][:, 3:4], None, OP.mult)
                            mark(ctx, inst, "v")
                        ctx.v(vue)
                vwu = ctx.n["v"]
                for oc in range(8):
                    sprev = ctx.n["s"]
                    def tme(te, oc=oc, gw=gw, vwu=vwu, sprev=sprev):
                        te.wait_ge(sems["gd"], gw)
                        te.wait_ge(sems["v"], vwu)
                        if sprev:
                            te.wait_ge(sems["s"], sprev)
                        for e2 in range(2):
                            for kc in range(8):
                                idx = e2 * 8 + kc
                                inst = te.matmul(ps_mm[oc % 2][:, :],
                                                 wbig[:, e2 * 8192 + kc * 1024 + oc * 128:
                                                       e2 * 8192 + kc * 1024 + (oc + 1) * 128],
                                                 wk["gat"][:, idx * CH:(idx + 1) * CH],
                                                 start=(idx == 0), stop=(idx == 15))
                        mark(ctx, inst, "te")
                    ctx.te(tme)
                    tw = ctx.n["te"]
                    def sme(s, oc=oc, tw=tw):
                        s.wait_ge(sems["te"], tw)
                        mark(ctx, s.copy(wk["mix"][:, oc * CH:(oc + 1) * CH], ps_mm[oc % 2][:, :]), "s")
                    ctx.s(sme)
                sw = ctx.n["s"]
                def gacc(g, eg=eg, t0=t0, sw=sw):
                    g.wait_ge(sems["s"], sw)
                    if eg == 0:
                        inst = g.dma_start(out=routed_dram[:, t0:t0 + CH].rearrange("(c p) t -> p c t", p=128),
                                           in_=wk["mix"][:, :].rearrange("p (c t) -> p c t", c=8))
                    else:
                        inst = g.dma_start(out=routed_dram[:, t0:t0 + CH].rearrange("(c p) t -> p c t", p=128),
                                           in_=wk["mix"][:, :].rearrange("p (c t) -> p c t", c=8),
                                           accum_op=OP.add)
                    mark(ctx, inst, "gd")
                ctx.g(gacc)
            ctx.barrier()
        if dbg_stage == "moer":
            def gdr(g):
                mark(ctx, g.dma_start(out=out_dram, in_=routed_dram), "gd")
            ctx.g(gdr)
            ctx.barrier()
            return
        for tb in range(8):
            t0 = tb * CH
            gw_consumers(te=True, v=True, s=True)
            _sdn = ctx.n["sd"] * 16
            def gsd2(g, _sdn=_sdn):
                if _sdn:
                    g.wait_ge(sems["sd"], _sdn)
            ctx.g(gsd2)
            gload(ctx, wk["normed"][:, :].rearrange("p (c t) -> p c t", c=8), stream_dram[:, t0:t0 + CH].rearrange("(c p) t -> p c t", p=128))
            for half in range(2):
                vprev = ctx.n["v"]
                def gacl(g, vprev=vprev):
                    if vprev:
                        g.wait_ge(sems["v"], vprev)
                ctx.g(gacl)
                gload(ctx, accf[:, :].rearrange("p (c t) -> p c t", c=4),
                      routed_dram[half * 512:half * 512 + 512, t0:t0 + CH].rearrange(
                          "(c p) t -> p c t", p=128))
                gw2 = ctx.n["gd"] * 16
                for j in range(4):
                    oc = half * 4 + j
                    sprev = ctx.n["s"]
                    vprev2 = ctx.n["v"]
                    def tbr(te, oc=oc, tb=tb, sprev=sprev, vprev2=vprev2):
                        if sprev:
                            te.wait_ge(sems["s"], sprev)
                        if vprev2:
                            te.wait_ge(sems["v"], vprev2)
                        mark(ctx, te.matmul(ps_x[:, :], vecs["br"][:, oc * 128:(oc + 1) * 128],
                                            gfm[:, tb * CH:(tb + 1) * CH], start=True, stop=True), "te")
                    ctx.te(tbr)
                    tw = ctx.n["te"]
                    def vas(v, oc=oc, j=j, tw=tw, gw2=gw2):
                        v.wait_ge(sems["te"], tw)
                        v.wait_ge(sems["gd"], gw2)
                        v.tensor_tensor(wk["sp1"][:, 0:CH], accf[:, j * CH:(j + 1) * CH], ps_x[:, :], OP.add)
                        v.tensor_tensor(wk["sp1"][:, 0:CH], wk["sp1"][:, 0:CH],
                                        wk["normed"][:, oc * CH:(oc + 1) * CH], OP.add)
                        mark(ctx, v.scalar_tensor_tensor(wk["mix"][:, oc * CH:(oc + 1) * CH],
                                                         wk["sp1"][:, 0:CH], bsum2[:, oc:oc + 1],
                                                         wk["normed"][:, oc * CH:(oc + 1) * CH],
                                                         OP.add, OP.add), "v")
                    ctx.v(vas)
            mix_tiles = [wk["mix"][:, oc * CH:(oc + 1) * CH] for oc in range(8)]
            emit_norm_stats(ctx, lnp, mix_tiles, CH, ps_mm[0], ps_mm[1], C, rms=False, vwait=ctx.n["v"])
            swn = ctx.n["s"]
            for oc in range(8):
                def vap(v, oc=oc, swn=swn):
                    v.wait_ge(sems["s"], swn)
                    v.tensor_tensor(wk["sp1"][:, 0:CH], wk["mix"][:, oc * CH:(oc + 1) * CH],
                                    lnp.mu_b[:, 0:CH], OP.subtract)
                    mark(ctx, v.tensor_tensor(wk["srcx"][:, oc * CH:(oc + 1) * CH],
                                              wk["sp1"][:, 0:CH], lnp.istd_b[:, 0:CH], OP.mult), "v")
                ctx.v(vap)
            vw = ctx.n["v"]
            def syu(sy, vw=vw):
                sy.wait_ge(sems["v"], vw)
            ctx.sy(syu)
            sload(ctx, stream_dram[:, t0:t0 + CH].rearrange("(c p) t -> p c t", p=128), wk["srcx"][:, :].rearrange("p (c t) -> p c t", c=8))
        ctx.barrier()

    ktm_dram = ctx.dram_tmp("ktm_dram", [TL, D])

    def emit_l1a():
        qs_sb = ctx.sbuf("qs_sb", [128, 8 * CH], BF16)
        kexp_sb = ctx.sbuf("kexp_sb", [128, 8 * CH], BF16)
        rkn_sb = ctx.sbuf("rkn_sb", [128, 8 * CH], BF16)
        Ssb = ctx.sbuf("Ssb", [128, 8 * 64], F32)
        Ssb_bf = ctx.sbuf("Ssb_bf", [128, 8 * 64], BF16)
        ksum_sb = ctx.sbuf("ksum_sb", [128, 10], F32)
        kcar = ctx.sbuf("kcar", [128, 9], F32)
        ktmp = ctx.sbuf("ktmp", [128, 64], F32)
        zt = ctx.sbuf("zt", [128, CH], BF16)
        rden_sb = ctx.sbuf("rden_sb", [16, CH], BF16)
        vtm_sb = ctx.sbuf("vtm_sb", [128, 1024], BF16)
        ktm_sb = ctx.sbuf("ktm_sb", [128, 1024], BF16)
        msb = ctx.sbuf("msb", [128, 2 * 128], BF16)
        def vz(v):
            v.memset(zt[:, :], 0.0)
            v.memset(ksum_sb[:, :], 0.0)
            mark(ctx, v.memset(Ssb[:, :], 0.0), "v")
        ctx.v(vz)
        for kc in range(8):
            gload(ctx, wbig[:, kc * 2048:(kc + 1) * 2048],
                  din["qkvT"][kc * 128:(kc + 1) * 128, 0:2048])
        gw = ctx.n["gd"] * 16
        ctx.barrier()
        for tb in range(8):
            t0 = tb * CH
            gw_consumers(te=True, v=True)
            gload(ctx, wk["normed"][:, :].rearrange("p (c t) -> p c t", c=8), stream_dram[:, t0:t0 + CH].rearrange("(c p) t -> p c t", p=128))
            gwu = ctx.n["gd"] * 16
            for oc in range(16):
                sprev = ctx.n["s"]
                def tqk(te, oc=oc, gw=gw, gwu=gwu, sprev=sprev):
                    te.wait_ge(sems["gd"], max(gw, gwu))
                    if sprev:
                        te.wait_ge(sems["s"], sprev)
                    for kc in range(8):
                        inst = te.matmul(ps_mm[oc % 2][:, :],
                                         wbig[:, kc * 2048 + oc * 128:kc * 2048 + (oc + 1) * 128],
                                         wk["normed"][:, kc * CH:(kc + 1) * CH],
                                         start=(kc == 0), stop=(kc == 7))
                    mark(ctx, inst, "te")
                ctx.te(tqk)
                tw = ctx.n["te"]
                def sqk(s, oc=oc, tw=tw):
                    s.wait_ge(sems["te"], tw)
                    mark(ctx, s.activation(wk["sp1"][:, (oc % 2) * CH:(oc % 2 + 1) * CH],
                                           ps_mm[oc % 2][:, :], AF.Exp, scale=0.125), "s")
                ctx.s(sqk)
                sw = ctx.n["s"]
                sdp = ctx.n["sd"] * 16
                if oc < 8:
                    def syq(sy, sw=sw):
                        sy.wait_ge(sems["s"], sw)
                    ctx.sy(syq)
                    sload(ctx, p_dram[oc * 128:(oc + 1) * 128, t0:t0 + CH],
                          wk["sp1"][:, (oc % 2) * CH:(oc % 2 + 1) * CH])
                else:
                    ko = oc - 8
                    def vke(v, ko=ko, oc=oc, tb=tb, sw=sw, sdp=sdp):
                        v.wait_ge(sems["s"], sw)
                        if sdp:
                            v.wait_ge(sems["sd"], sdp)
                        v.tensor_scalar(kexp_sb[:, ko * CH:(ko + 1) * CH],
                                        wk["sp1"][:, (oc % 2) * CH:(oc % 2 + 1) * CH],
                                        C["misc"][:, 4:5], None, OP.add)
                        mark(ctx, v.reduce_sum(ktmp[:, tb * 8 + ko:tb * 8 + ko + 1],
                                               kexp_sb[:, ko * CH:(ko + 1) * CH], axis=AX.X), "v")
                    ctx.v(vke)
                    vw = ctx.n["v"]
                    def syk(sy, vw=vw):
                        sy.wait_ge(sems["v"], vw)
                    ctx.sy(syk)
                    sload(ctx, kexp_dram[ko * 128:(ko + 1) * 128, t0:t0 + CH],
                          kexp_sb[:, ko * CH:(ko + 1) * CH])
        ctx.barrier()
        def vks(v):
            for ko in range(8):
                inst = v.reduce_sum(ksum_sb[:, ko:ko + 1], ktmp[:, ko::8], axis=AX.X)
            mark(ctx, inst, "v")
        ctx.v(vks)
        ctx.barrier()
        for kc in range(8):
            gload(ctx, wbig[:, kc * 1024:(kc + 1) * 1024],
                  din["qkvT"][kc * 128:(kc + 1) * 128, 2048:3072])
        gw = ctx.n["gd"] * 16
        ctx.barrier()
        nst_tot = [0]
        for tb in range(8):
            t0 = tb * CH
            gw_consumers(te=True, v=True, s=True)
            gload(ctx, wk["normed"][:, :].rearrange("p (c t) -> p c t", c=8), stream_dram[:, t0:t0 + CH].rearrange("(c p) t -> p c t", p=128))
            gload(ctx, kexp_sb[:, :].rearrange("p (c t) -> p c t", c=8), kexp_dram[:, t0:t0 + CH].rearrange("(c p) t -> p c t", p=128))
            gwu = ctx.n["gd"] * 16
            for oc in range(8):
                sprev = ctx.n["s"]
                def tv(te, oc=oc, gw=gw, gwu=gwu, sprev=sprev):
                    te.wait_ge(sems["gd"], max(gw, gwu))
                    if sprev:
                        te.wait_ge(sems["s"], sprev)
                    for kc in range(8):
                        inst = te.matmul(ps_mm[oc % 2][:, :],
                                         wbig[:, kc * 1024 + oc * 128:kc * 1024 + (oc + 1) * 128],
                                         wk["normed"][:, kc * CH:(kc + 1) * CH],
                                         start=(kc == 0), stop=(kc == 7))
                    mark(ctx, inst, "te")
                ctx.te(tv)
                tw = ctx.n["te"]
                def sv(s, oc=oc, tw=tw):
                    s.wait_ge(sems["te"], tw)
                    mark(ctx, s.copy(wk["mix"][:, oc * CH:(oc + 1) * CH], ps_mm[oc % 2][:, :]), "s")
                ctx.s(sv)
            swv = ctx.n["s"]
            for tsc in range(4):
                sdp = ctx.n["sd"] * 16
                for fc in range(8):
                    for which, srct, dstt in [(0, wk["mix"], vtm_sb), (1, kexp_sb, ktm_sb)]:
                        sprev = ctx.n["s"]
                        def ttr(te, tsc=tsc, fc=fc, srct=srct, swv=swv, sprev=sprev, sdp=sdp):
                            te.wait_ge(sems["s"], max(swv, sprev))
                            if sdp:
                                te.wait_ge(sems["sd"], sdp)
                            mark(ctx, te.transpose(ps_tp[:, :],
                                                   srct[:, fc * CH + tsc * 128:fc * CH + (tsc + 1) * 128],
                                                   C["ident_bf"][:, :]), "te")
                        ctx.te(ttr)
                        tw = ctx.n["te"]
                        sdpt = ctx.n["sd"] * 16
                        def str_(s, fc=fc, dstt=dstt, tw=tw, sdpt=sdpt):
                            s.wait_ge(sems["te"], tw)
                            if sdpt:
                                s.wait_ge(sems["sd"], sdpt)
                            mark(ctx, s.copy(dstt[:, fc * 128:(fc + 1) * 128], ps_tp[:, :]), "s")
                        ctx.s(str_)
                swt = ctx.n["s"]
                step = tb * 4 + tsc
                def tst(te, swt=swt, step=step):
                    te.wait_ge(sems["s"], swt)
                    for h in range(16):
                        j, hp = h % 2, h // 2
                        inst = te.matmul(ps_bl[0][j * 64:(j + 1) * 64, hp * 64:(hp + 1) * 64],
                                         ktm_sb[:, h * 64:(h + 1) * 64],
                                         vtm_sb[:, h * 64:(h + 1) * 64],
                                         start=(step == 0), stop=(step == 31))
                    mark(ctx, inst, "te")
                ctx.te(tst)
                twd = ctx.n["te"]
                def syt(sy, twd=twd):
                    sy.wait_ge(sems["te"], twd)
                ctx.sy(syt)
                sload(ctx, vtm_dram[t0 + tsc * 128:t0 + (tsc + 1) * 128, :], vtm_sb[:, :])
                sload(ctx, ktm_dram[t0 + tsc * 128:t0 + (tsc + 1) * 128, :], ktm_sb[:, :])
        ctx.barrier()
        def sst(s):
            mark(ctx, s.copy(Ssb[:, :], ps_bl[0][:, :]), "s")
        ctx.s(sst)
        ctx.barrier()
        sload(ctx, sag_in[:, 0:512], Ssb[:, :])
        sload(ctx, sag_in[:, 512:520], ksum_sb[:, 0:8])
        ctx.barrier()
        collective(ctx, "AllGather", OP.bypass, PAIRS, sag_in, sag_out)
        ctx.barrier()
        sini = ctx.sbuf("sini", [128, 520], F32)
        knf = ctx.sbuf("knf", [128, CH], F32)
        gload(ctx, sini[:, :], sag_out[0:128, :])
        for kc in range(8):
            gload(ctx, wbig[:, kc * 1024:(kc + 1) * 1024], din["woutT"][kc * 128:(kc + 1) * 128, :])
        ctx.barrier()
        def vsi(v):
            v.tensor_scalar(Ssb[:, :], sini[:, 0:512], C["flag"][:, 0:1], None, OP.mult)
            mark(ctx, v.tensor_copy(Ssb_bf[:, :], Ssb[:, :]), "v")
        ctx.v(vsi)
        _vw = ctx.n["v"]
        def ssi(s, _vw=_vw):
            s.wait_ge(sems["v"], _vw)
            mark(ctx, s.activation(kcar[:, 0:8], sini[:, 512:520], AF.Identity,
                                   scale=C["flag"][:, 0:1]), "s")
        ctx.s(ssi)
        ctx.barrier()
        for tb in range(8):
            t0 = tb * CH
            gw_consumers(te=True, v=True, s=True)
            gload(ctx, wk["normed"][:, :].rearrange("p (c t) -> p c t", c=8), p_dram[:, t0:t0 + CH].rearrange("(c p) t -> p c t", p=128))
            gload(ctx, kexp_sb[:, :].rearrange("p (c t) -> p c t", c=8), kexp_dram[:, t0:t0 + CH].rearrange("(c p) t -> p c t", p=128))
            gwp = ctx.n["gd"] * 16
            for oc in range(8):
                vprev = ctx.n["v"]
                sprev = ctx.n["s"]
                def vkn(v, oc=oc, gwp=gwp, sprev=sprev):
                    v.wait_ge(sems["gd"], gwp)
                    if sprev:
                        v.wait_ge(sems["s"], sprev)
                    mark(ctx, v.tensor_tensor_scan(knf[:, :], kexp_sb[:, oc * CH:(oc + 1) * CH],
                                         zt[:, :], kcar[:, oc:oc + 1], OP.add, OP.add), "v")
                ctx.v(vkn)
                vw = ctx.n["v"]
                def skn(s, oc=oc, vw=vw):
                    s.wait_ge(sems["v"], vw)
                    mark(ctx, s.copy(kcar[:, oc:oc + 1], knf[:, CH - 1:CH]), "s")
                ctx.s(skn)
                def vkr(v, oc=oc):
                    with ctx.nc.allow_low_precision(reason="1/knorm to bf16 is ample here"):
                        mark(ctx, v.reciprocal(rkn_sb[:, oc * CH:(oc + 1) * CH], knf[:, :]), "v")
                ctx.v(vkr)
            swd = ctx.n["s"]
            def tdn(te, gwp=gwp, swd=swd):
                te.wait_ge(sems["gd"], gwp)
                te.wait_ge(sems["s"], swd)
                for hp in range(8):
                    inst = te.matmul(ps_x[0:16, :], C["selH"][:, hp * 16:(hp + 1) * 16],
                                     wk["normed"][:, hp * CH:(hp + 1) * CH],
                                     start=(hp == 0), stop=(hp == 7))
                mark(ctx, inst, "te")
            ctx.te(tdn)
            tw = ctx.n["te"]
            def sdn(v, tw=tw):
                v.wait_ge(sems["te"], tw)
                with ctx.nc.allow_low_precision(reason="1/denom to bf16 is ample here"):
                    mark(ctx, v.reciprocal(rden_sb[:, :], ps_x[0:16, :]), "v")
            ctx.v(sdn)
            swd2 = ctx.n["s"]
            for oc in range(8):
                vprev = ctx.n["v"]
                def trb(te, oc=oc, swd2=swd2, vprev=vprev):
                    te.wait_ge(sems["s"], swd2)
                    if vprev:
                        te.wait_ge(sems["v"], vprev)
                    mark(ctx, te.matmul(ps_x[:, :], C["selHB"][:, oc * 128:(oc + 1) * 128],
                                        rden_sb[:, :], start=True, stop=True), "te")
                ctx.te(trb)
                tw = ctx.n["te"]
                def srb(s, tw=tw):
                    s.wait_ge(sems["te"], tw)
                    mark(ctx, s.copy(wk["sp1"][:, 0:CH], ps_x[:, :]), "s")
                ctx.s(srb)
                sw = ctx.n["s"]
                def vqs(v, oc=oc, sw=sw):
                    v.wait_ge(sems["s"], sw)
                    v.tensor_tensor(qs_sb[:, oc * CH:(oc + 1) * CH],
                                    wk["normed"][:, oc * CH:(oc + 1) * CH],
                                    rkn_sb[:, oc * CH:(oc + 1) * CH], OP.mult)
                    mark(ctx, v.tensor_tensor(qs_sb[:, oc * CH:(oc + 1) * CH],
                                              qs_sb[:, oc * CH:(oc + 1) * CH],
                                              wk["sp1"][:, 0:CH], OP.mult), "v")
                ctx.v(vqs)
            for tsc in range(4):
                tprev = ctx.n["te"]
                def syc(sy, tprev=tprev):
                    if tprev:
                        sy.wait_ge(sems["te"], tprev)
                ctx.sy(syc)
                sload(ctx, vtm_sb[:, :], vtm_dram[t0 + tsc * 128:t0 + (tsc + 1) * 128, :])
                sload(ctx, ktm_sb[:, :], ktm_dram[t0 + tsc * 128:t0 + (tsc + 1) * 128, :])
                sdw = ctx.n["sd"] * 16
                vw_all = ctx.n["v"]
                for hp in range(8):
                    sprev = ctx.n["s"]
                    vprev = ctx.n["v"]
                    def tsco(te, hp=hp, tsc=tsc, sdw=sdw, vw_all=vw_all, sprev=sprev, vprev=vprev):
                        te.wait_ge(sems["sd"], sdw)
                        te.wait_ge(sems["v"], max(vw_all, vprev))
                        if sprev:
                            te.wait_ge(sems["s"], sprev)
                        for j in range(2):
                            mark(ctx, te.matmul(ps_mm[j][:, 0:128],
                                                kexp_sb[j * 64:(j + 1) * 64,
                                                        hp * CH + tsc * 128:hp * CH + (tsc + 1) * 128],
                                                qs_sb[j * 64:(j + 1) * 64,
                                                      hp * CH + tsc * 128:hp * CH + (tsc + 1) * 128],
                                                start=True, stop=True), "te")
                    ctx.te(tsco)
                    tw = ctx.n["te"]
                    def vmask(v, tw=tw):
                        v.wait_ge(sems["te"], tw)
                        v.tensor_tensor(msb[:, 0:128], ps_mm[0][:, 0:128], C["cmask"][:, :], OP.mult)
                        mark(ctx, v.tensor_tensor(msb[:, 128:256], ps_mm[1][:, 0:128],
                                                  C["cmask"][:, :], OP.mult), "v")
                    ctx.v(vmask)
                    vwm = ctx.n["v"]
                    sprev3 = ctx.n["s"]
                    def tyy(te, hp=hp, tsc=tsc, vwm=vwm, sprev3=sprev3):
                        te.wait_ge(sems["v"], vwm)
                        if sprev3 > 1:
                            te.wait_ge(sems["s"], sprev3 - 1)
                        for j in range(2):
                            h = 2 * hp + j
                            te.matmul(ps_y[hp % 2][j * 64:(j + 1) * 64, 0:128],
                                      vtm_sb[:, h * 64:(h + 1) * 64], msb[:, j * 128:(j + 1) * 128],
                                      start=True, stop=False)
                            te.matmul(ps_y[hp % 2][j * 64:(j + 1) * 64, 0:128],
                                      Ssb_bf[j * 64:(j + 1) * 64, hp * 64:(hp + 1) * 64],
                                      qs_sb[j * 64:(j + 1) * 64,
                                            hp * CH + tsc * 128:hp * CH + (tsc + 1) * 128],
                                      start=False, stop=True)
                            inst = te.matmul(ps_bl[hp % 2][j * 64:(j + 1) * 64, 0:64],
                                             ktm_sb[:, h * 64:(h + 1) * 64],
                                             vtm_sb[:, h * 64:(h + 1) * 64],
                                             start=True, stop=True)
                        mark(ctx, inst, "te")
                    ctx.te(tyy)
                    tw2 = ctx.n["te"]
                    def sev(s, hp=hp, tsc=tsc, tw2=tw2):
                        s.wait_ge(sems["te"], tw2)
                        mark(ctx, s.copy(wk["mix"][:, hp * CH + tsc * 128:hp * CH + (tsc + 1) * 128],
                                         ps_y[hp % 2][:, 0:128]), "s")
                    ctx.s(sev)
                    sw2 = ctx.n["s"]
                    def vsu(v, hp=hp, sw2=sw2, tw2=tw2):
                        v.wait_ge(sems["te"], tw2)
                        for j in range(2):
                            v.tensor_tensor(Ssb[j * 64:(j + 1) * 64, hp * 64:(hp + 1) * 64],
                                            Ssb[j * 64:(j + 1) * 64, hp * 64:(hp + 1) * 64],
                                            ps_bl[hp % 2][j * 64:(j + 1) * 64, 0:64], OP.add)
                        mark(ctx, v.tensor_copy(Ssb_bf[:, hp * 64:(hp + 1) * 64],
                                                Ssb[:, hp * 64:(hp + 1) * 64]), "v")
                    ctx.v(vsu)
            if dbg_stage == "l1q":
                _v = ctx.n["v"]
                def syq2(sy, _v=_v):
                    sy.wait_ge(sems["v"], _v)
                ctx.sy(syq2)
                sload(ctx, v_dram[0:D, t0:t0 + CH].rearrange("(c p) t -> p c t", p=128),
                      qs_sb[:, :].rearrange("p (c t) -> p c t", c=8))
            if dbg_stage == "l1y":
                _s = ctx.n["s"]
                def syy2(sy, _s=_s):
                    sy.wait_ge(sems["s"], _s)
                ctx.sy(syy2)
                sload(ctx, v_dram[0:D, t0:t0 + CH].rearrange("(c p) t -> p c t", p=128),
                      wk["mix"][:, :].rearrange("p (c t) -> p c t", c=8))
            gw_consumers(te=True, v=True)
            gload(ctx, wk["normed"][:, :].rearrange("p (c t) -> p c t", c=8), stream_dram[:, t0:t0 + CH].rearrange("(c p) t -> p c t", p=128))
            gws = ctx.n["gd"] * 16
            vwc = ctx.n["v"]
            swc = ctx.n["s"]
            for oc in range(8):
                sprev = ctx.n["s"]
                def two_(te, oc=oc, swc=swc, sprev=sprev):
                    te.wait_ge(sems["s"], max(swc, sprev))
                    for hp in range(8):
                        inst = te.matmul(ps_mm[oc % 2][:, :],
                                         wbig[:, hp * 1024 + oc * 128:hp * 1024 + (oc + 1) * 128],
                                         wk["mix"][:, hp * CH:(hp + 1) * CH],
                                         start=(hp == 0), stop=(hp == 7))
                    mark(ctx, inst, "te")
                ctx.te(two_)
                tw = ctx.n["te"]
                def swo(s, oc=oc, tw=tw):
                    s.wait_ge(sems["te"], tw)
                    mark(ctx, s.copy(wk["srcx"][:, oc * CH:(oc + 1) * CH], ps_mm[oc % 2][:, :]), "s")
                ctx.s(swo)
            sww = ctx.n["s"]
            def vres(v, sww=sww, gws=gws):
                v.wait_ge(sems["s"], sww)
                v.wait_ge(sems["gd"], gws)
                for oc in range(8):
                    inst = v.tensor_tensor(wk["mix"][:, oc * CH:(oc + 1) * CH],
                                           wk["srcx"][:, oc * CH:(oc + 1) * CH],
                                           wk["normed"][:, oc * CH:(oc + 1) * CH], OP.add)
                mark(ctx, inst, "v")
            ctx.v(vres)
            mix_tiles = [wk["mix"][:, oc * CH:(oc + 1) * CH] for oc in range(8)]
            emit_norm_stats(ctx, lnp, mix_tiles, CH, ps_mm[0], ps_mm[1], C, rms=False, vwait=ctx.n["v"])
            swn = ctx.n["s"]
            for oc in range(8):
                def vap(v, oc=oc, swn=swn):
                    v.wait_ge(sems["s"], swn)
                    v.tensor_tensor(wk["sp1"][:, 0:CH], wk["mix"][:, oc * CH:(oc + 1) * CH],
                                    lnp.mu_b[:, 0:CH], OP.subtract)
                    mark(ctx, v.tensor_tensor(wk["srcx"][:, oc * CH:(oc + 1) * CH],
                                              wk["sp1"][:, 0:CH], lnp.istd_b[:, 0:CH], OP.mult), "v")
                ctx.v(vap)
            vw = ctx.n["v"]
            def syu(sy, vw=vw):
                sy.wait_ge(sems["v"], vw)
            ctx.sy(syu)
            sload(ctx, stream_dram[:, t0:t0 + CH].rearrange("(c p) t -> p c t", p=128), wk["srcx"][:, :].rearrange("p (c t) -> p c t", c=8))
            ctx.barrier()

    def emit_l1f():
        fq32 = ctx.sbuf("fq32", [128, 1024], F32)
        vq1 = ctx.sbuf("vq1", [128, 1024], F32)
        vq2 = ctx.sbuf("vq2", [128, 1024], F32)
        qcols = ctx.sbuf("qcols", [128, 24], F32)
        fout = ctx.sbuf("fout", [128, 2 * CH], F32)
        xq_sb = ctx.sbuf("xq_sb", [128, 8 * CH], BF16)
        knf2 = ctx.sbuf("knf2", [128, CH], F32)
        misc = C["misc"]
        def vz0(v):
            mark(ctx, v.memset(qcols[:, :], 0.0), "v")
        ctx.v(vz0)
        ctx.barrier()
        for kc in range(8):
            sload(ctx, fq32[:, :], din["fqT"][kc * 128:(kc + 1) * 128, :])
            sdw = ctx.n["sd"] * 16
            def vwa(v, kc=kc, sdw=sdw):
                v.wait_ge(sems["sd"], sdw)
                v.reduce_max(qcols[:, 0:1], fq32[:, :], axis=AX.X, apply_absolute_value=True)
                if kc == 0:
                    inst = v.tensor_copy(qcols[:, 1:2], qcols[:, 0:1])
                else:
                    inst = v.tensor_tensor(qcols[:, 1:2], qcols[:, 1:2], qcols[:, 0:1], OP.max)
                mark(ctx, inst, "v")
            ctx.v(vwa)
            ctx.barrier()
        colmax_bcast(qcols[:, 1:2], qcols[:, 2:3])
        ctx.barrier()
        def vsw(v):
            mark(ctx, v.tensor_scalar(qcols[:, 16:17], qcols[:, 2:3], misc[:, 7:8], None, OP.max), "v")
        ctx.v(vsw)
        _vw = ctx.n["v"]
        def ssw(s, _vw=_vw):
            s.wait_ge(sems["v"], _vw)
            mark(ctx, s.copy(qcols[:, 18:19], qcols[:, 16:17]), "s")
        ctx.s(ssw)
        _sw = ctx.n["s"]
        def vsw2(v, _sw=_sw):
            v.wait_ge(sems["s"], _sw)
            mark(ctx, v.reciprocal(qcols[:, 20:21], qcols[:, 18:19]), "v")
        ctx.v(vsw2)
        _vw = ctx.n["v"]
        def ssw2(s, _vw=_vw):
            s.wait_ge(sems["v"], _vw)
            mark(ctx, s.activation(qcols[:, 3:4], qcols[:, 20:21], AF.Identity,
                                   scale=misc[:, 6:7]), "s")
        ctx.s(ssw2)
        ctx.barrier()
        for kc in range(8):
            sload(ctx, fq32[:, :], din["fqT"][kc * 128:(kc + 1) * 128, :])
            sdw = ctx.n["sd"] * 16
            def vq(v, kc=kc, sdw=sdw):
                v.wait_ge(sems["sd"], sdw)
                v.tensor_scalar(vq1[:, :], fq32[:, :], qcols[:, 3:4], None, OP.mult)
                v.tensor_scalar(vq2[:, :], vq1[:, :], misc[:, 5:6], None, OP.mult)
                v.tensor_tensor(vq1[:, :], vq2[:, :], vq1[:, :], OP.subtract)
                mark(ctx, v.tensor_tensor(wbig[:, kc * 1024:(kc + 1) * 1024], vq2[:, :], vq1[:, :],
                                          OP.subtract), "v")
            ctx.v(vq)
            ctx.barrier()
        for tb in range(8):
            t0 = tb * CH
            gw_consumers(v=True)
            gload(ctx, wk["normed"][:, :].rearrange("p (c t) -> p c t", c=8), stream_dram[:, t0:t0 + CH].rearrange("(c p) t -> p c t", p=128))
            gw = ctx.n["gd"] * 16
            def vxa(v, tb=tb, gw=gw):
                v.wait_ge(sems["gd"], gw)
                for oc in range(8):
                    v.reduce_max(qcols[:, 4:5], wk["normed"][:, oc * CH:(oc + 1) * CH],
                                 axis=AX.X, apply_absolute_value=True)
                    if tb == 0 and oc == 0:
                        inst = v.tensor_copy(qcols[:, 5:6], qcols[:, 4:5])
                    else:
                        inst = v.tensor_tensor(qcols[:, 5:6], qcols[:, 5:6], qcols[:, 4:5], OP.max)
                mark(ctx, inst, "v")
            ctx.v(vxa)
            ctx.barrier()
        colmax_bcast(qcols[:, 5:6], qcols[:, 6:7])
        ctx.barrier()
        sload(ctx, amax_in, qcols[0:1, 6:14])
        ctx.barrier()
        collective(ctx, "AllReduce", OP.max, ALL8, amax_in, amax_out)
        ctx.barrier()
        gload(ctx, qcols[0:1, 7:8], amax_out[0:1, 0:1])
        ctx.barrier()
        def tbc2(te):
            mark(ctx, te.matmul(ps_x[:, 0:1], C["ones_f32"][:, :], qcols[0:1, 7:8],
                                start=True, stop=True), "te")
        ctx.te(tbc2)
        tw_b = ctx.n["te"]
        def vbc2(v, tw_b=tw_b):
            v.wait_ge(sems["te"], tw_b)
            mark(ctx, v.tensor_copy(qcols[:, 8:9], ps_x[:, 0:1]), "v")
        ctx.v(vbc2)
        ctx.barrier()
        def vsx(v):
            mark(ctx, v.tensor_scalar(qcols[:, 17:18], qcols[:, 8:9], misc[:, 7:8], None, OP.max), "v")
        ctx.v(vsx)
        _vw = ctx.n["v"]
        def ssx(s, _vw=_vw):
            s.wait_ge(sems["v"], _vw)
            mark(ctx, s.copy(qcols[:, 19:20], qcols[:, 17:18]), "s")
        ctx.s(ssx)
        _sw = ctx.n["s"]
        def vsx2(v, _sw=_sw):
            v.wait_ge(sems["s"], _sw)
            v.reciprocal(qcols[:, 21:22], qcols[:, 19:20])
            mark(ctx, v.scalar_tensor_tensor(qcols[:, 22:23], qcols[:, 18:19], misc[:, 8:9],
                                             qcols[:, 19:20], OP.mult, OP.mult), "v")
        ctx.v(vsx2)
        _vw = ctx.n["v"]
        def ssx2(s, _vw=_vw):
            s.wait_ge(sems["v"], _vw)
            s.activation(qcols[:, 9:10], qcols[:, 21:22], AF.Identity, scale=misc[:, 6:7])
            mark(ctx, s.copy(qcols[:, 10:11], qcols[:, 22:23]), "s")
        ctx.s(ssx2)
        ctx.barrier()
        if dbg_stage == "l1fs":
            _v = ctx.n["v"]
            _s = ctx.n["s"]
            def syd(sy, _v=_v, _s=_s):
                sy.wait_ge(sems["v"], _v)
                sy.wait_ge(sems["s"], _s)
            ctx.sy(syd)
            sload(ctx, out_dram[0:128, 0:16], qcols[:, :])
            ctx.barrier()
            return
        for tb in range(8):
            t0 = tb * CH
            gw_consumers(te=True, v=True, s=True)
            gload(ctx, wk["normed"][:, :].rearrange("p (c t) -> p c t", c=8), stream_dram[:, t0:t0 + CH].rearrange("(c p) t -> p c t", p=128))
            gw = ctx.n["gd"] * 16
            for oc in range(8):
                def vqx(v, oc=oc, gw=gw):
                    v.wait_ge(sems["gd"], gw)
                    v.tensor_scalar(vq1[:, 0:CH], wk["normed"][:, oc * CH:(oc + 1) * CH],
                                    qcols[:, 9:10], None, OP.mult)
                    v.tensor_scalar(vq2[:, 0:CH], vq1[:, 0:CH], misc[:, 5:6], None, OP.mult)
                    v.tensor_tensor(knf2[:, :], vq2[:, 0:CH], vq1[:, 0:CH], OP.subtract)
                    mark(ctx, v.tensor_tensor(xq_sb[:, oc * CH:(oc + 1) * CH], vq2[:, 0:CH],
                                              knf2[:, :], OP.subtract), "v")
                ctx.v(vqx)
            vwq = ctx.n["v"]
            for oc in range(8):
                vprev = ctx.n["v"]
                def tfq(te, oc=oc, vwq=vwq, vprev=vprev):
                    te.wait_ge(sems["v"], max(vwq, vprev))
                    for kc in range(8):
                        inst = te.matmul(ps_mm[oc % 2][:, :],
                                         wbig[:, kc * 1024 + oc * 128:kc * 1024 + (oc + 1) * 128],
                                         xq_sb[:, kc * CH:(kc + 1) * CH],
                                         start=(kc == 0), stop=(kc == 7))
                    mark(ctx, inst, "te")
                ctx.te(tfq)
                tw = ctx.n["te"]
                def vev(v, oc=oc, tw=tw):
                    v.wait_ge(sems["te"], tw)
                    v.tensor_scalar(wk["sp1"][:, 0:CH], ps_mm[oc % 2][:, :],
                                    qcols[:, 10:11], vecs["fqb"][:, oc:oc + 1], OP.mult, OP.add)
                    mark(ctx, v.tensor_tensor(wk["mix"][:, oc * CH:(oc + 1) * CH],
                                              wk["sp1"][:, 0:CH],
                                              wk["normed"][:, oc * CH:(oc + 1) * CH], OP.add), "v")
                ctx.v(vev)
            mix_tiles = [wk["mix"][:, oc * CH:(oc + 1) * CH] for oc in range(8)]
            emit_norm_stats(ctx, lnp, mix_tiles, CH, ps_mm[0], ps_mm[1], C, rms=False, vwait=ctx.n["v"])
            swn = ctx.n["s"]
            for oc in range(8):
                sdp = ctx.n["sd"] * 16
                def vapf(v, oc=oc, swn=swn, sdp=sdp):
                    v.wait_ge(sems["s"], swn)
                    if oc >= 2:
                        v.wait_ge(sems["sd"], sdp)
                    v.tensor_tensor(wk["sp1"][:, 0:CH], wk["mix"][:, oc * CH:(oc + 1) * CH],
                                    lnp.mu_b[:, 0:CH], OP.subtract)
                    mark(ctx, v.tensor_tensor(fout[:, (oc % 2) * CH:(oc % 2 + 1) * CH],
                                              wk["sp1"][:, 0:CH], lnp.istd_b[:, 0:CH], OP.mult), "v")
                ctx.v(vapf)
                vwf = ctx.n["v"]
                def syo(sy, vwf=vwf):
                    sy.wait_ge(sems["v"], vwf)
                ctx.sy(syo)
                sload(ctx, out_dram[oc * 128:(oc + 1) * 128, t0:t0 + CH],
                      fout[:, (oc % 2) * CH:(oc % 2 + 1) * CH])
            ctx.barrier()

    def dump_stream(src_dram_ap):
        ctx.barrier()
        def gdump(g):
            mark(ctx, g.dma_start(out=out_dram, in_=src_dram_ap), "gd")
        ctx.g(gdump)
        ctx.barrier()

    if dbg_stage == "l0a":
        dump_stream(res_dram)
    else:
        ctx.barrier()
        emit_ffn()
        if dbg_stage == "ffv":
            dump_stream(v_dram[0:D, :])
        elif dbg_stage == "ff2":
            dump_stream(stream_dram)
        elif dbg_stage == "ffn":
            dump_stream(stream_dram)
        else:
            ctx.push_scope()
            emit_moe()
            ctx.pop_scope()
            if dbg_stage in ("moe", "moeg", "moer"):
                if dbg_stage == "moe":
                    dump_stream(stream_dram)
            else:
                ctx.push_scope()
                emit_l1a()
                ctx.pop_scope()
                if dbg_stage in ("l1a", "l1q", "l1y"):
                    dump_stream(stream_dram if dbg_stage == "l1a" else v_dram[0:D, :])
                elif dbg_stage in (None, "l1fs"):
                    ctx.push_scope()
                    emit_l1f()
                    ctx.pop_scope()
    ctx.barrier()
    return ctx



def kernel(**inputs):
    shards = stage_inputs(inputs)
    ctx = build_program(dbg_stage=DBG_STAGE[0])
    ctx.stack.close()
    res = run_bass_kernel_spmd(ctx.nc, shards, core_ids=list(range(8)))
    out = np.zeros((B, T, D), np.float32)
    for c in range(8):
        b, half = c // 2, c % 2
        out[b, half * TL:(half + 1) * TL, :] = res.results[c]["out"].T
    return out
